# version 05
# baseline (speedup 1.0000x reference)
import contextlib
import numpy as np
import concourse.bass as bass
import concourse.mybir as mybir
from concourse.bass_utils import run_bass_kernel_spmd
from concourse.bass_isa import ReduceOp

F32 = mybir.dt.float32
BF16 = mybir.dt.bfloat16
AF = mybir.ActivationFunctionType
ALU = mybir.AluOpType
AX = mybir.AxisListType

NCORES = 8
B = 4
TG = 8192
TL = TG // NCORES
D = 1024
NCH = D // 128
NS = 16
H = 16
DK = 64
MAX_REL = 128

SKIP_CC = False
ENGS = ("sync", "scalar", "vector", "tensor", "gpsimd")


class Prog:

    def __init__(self, sem_names):
        self.counts = {s: 0 for s in sem_names}
        self.owner = {}
        self.ops = {e: [] for e in ENGS}
        self.cursor = {e: 0 for e in ENGS}
        self.last_wait = {e: {} for e in ENGS}

    def snap(self):
        return dict(self.counts)

    def op(self, eng, fn, waits=(), inc=None, self_wait=False):
        w = [(s, v) for (s, v) in waits if v > 0]
        self.ops[eng].append((w, inc, fn, self_wait))
        if inc is not None:
            s, k = inc
            if s in self.owner:
                assert self.owner[s] == eng, (s, eng, self.owner[s])
            self.owner[s] = eng
            self.counts[s] += k

    def dep(self, *sems):
        return [(s, self.counts[s]) for s in sems]

    def replay(self, engine_obj, eng_name, handles):
        last = self.last_wait[eng_name]
        ops = self.ops[eng_name]
        start = self.cursor[eng_name]
        self.cursor[eng_name] = len(ops)
        for waits, inc, fn, self_wait in ops[start:]:
            for s, v in waits:
                if self.owner.get(s) == eng_name and not self_wait:
                    continue
                if last.get(s, 0) >= v:
                    continue
                engine_obj.wait_ge(handles[s], v)
                last[s] = v
            instr = fn(engine_obj)
            if inc is not None:
                s, k = inc
                assert instr is not None
                instr.then_inc(handles[s], k)


def _replay_all(prog, block, handles):
    @block.sync
    def _(e):
        prog.replay(e, "sync", handles)

    @block.scalar
    def _(e):
        prog.replay(e, "scalar", handles)

    @block.vector
    def _(e):
        prog.replay(e, "vector", handles)

    @block.tensor
    def _(e):
        prog.replay(e, "tensor", handles)

    @block.gpsimd
    def _(e):
        prog.replay(e, "gpsimd", handles)


SEMS = ("ld", "pe", "act", "dve", "gpc", "st")


def _mk_sems(nc, st_):
    return {s: st_.enter_context(nc.semaphore(s)) for s in SEMS}



def emit_l1(nc, p, sem, io):
    u_fm = io["u_fm"]; wdt_t = io["wdt_t"]; wb_t = io["wb_t"]
    wc_t = io["wc_t"]; bdt_t = io["bdt_t"]
    st_xln = io["st_xln"]; st_lns = io["st_lns"]; st_bc = io["st_bc"]

    C1 = 512
    ntile = B * (TL // C1)

    with contextlib.ExitStack() as stk:

        def sb(name, shape, dt=F32):
            return stk.enter_context(nc.sbuf_tensor("L1" + name, shape, dt))

        wdt_f = sb("wdt_f", [128, NCH * D])
        wdt_b = sb("wdt_b", [128, NCH * D], BF16)
        wbc_f = sb("wbc_f", [128, 2 * NCH * NS])
        wbc_b = sb("wbc_b", [128, 2 * NCH * NS], BF16)
        nb = sb("nb", [128, NCH])
        onesP = sb("onesP", [128, 1])
        ones1 = sb("ones1", [1, 128])
        ubuf = sb("ubuf", [128, NCH * C1])
        usq = sb("usq", [128, NCH * C1])
        xln = sb("xln", [128, NCH * C1])
        xln_b = sb("xln_b", [128, NCH * C1], BF16)
        lnS = sb("lnS", [128, NCH * C1])
        stile = sb("stile", [128, C1])
        m_row = sb("m_row", [1, C1])
        msq_row = sb("msq_row", [1, C1])
        var_row = sb("var_row", [1, C1])
        rstd_row = sb("rstd_row", [1, C1])
        mr_row = sb("mr_row", [1, C1])
        brow = sb("brow", [NS, C1])
        crow = sb("crow", [NS, C1])
        cst = sb("cst", [128, 2])

        ps_sx = stk.enter_context(nc.psum_tensor("L1ps_sx", [1, C1], F32))
        ps_sx2 = stk.enter_context(nc.psum_tensor("L1ps_sx2", [1, C1], F32))
        ps_bc1 = stk.enter_context(nc.psum_tensor("L1ps_bc1", [128, C1], F32))
        ps_bc2 = stk.enter_context(nc.psum_tensor("L1ps_bc2", [128, C1], F32))
        ps_dt = stk.enter_context(nc.psum_tensor("L1ps_dt", [128, C1], F32))
        ps_B = stk.enter_context(nc.psum_tensor("L1ps_B", [NS, C1], F32))
        ps_C = stk.enter_context(nc.psum_tensor("L1ps_C", [NS, C1], F32))

        for kc in range(NCH):
            p.op("sync", (lambda kc=kc: (lambda e: e.dma_start(
                out=wdt_f[:, kc * D:(kc + 1) * D],
                in_=wdt_t[kc * 128:(kc + 1) * 128, :])))(),
                inc=("ld", 16))
            p.op("sync", (lambda kc=kc: (lambda e: e.dma_start(
                out=wbc_f[:, kc * NS:(kc + 1) * NS],
                in_=wb_t[kc * 128:(kc + 1) * 128, :])))(),
                inc=("ld", 16))
            p.op("sync", (lambda kc=kc: (lambda e: e.dma_start(
                out=wbc_f[:, (NCH + kc) * NS:(NCH + kc + 1) * NS],
                in_=wc_t[kc * 128:(kc + 1) * 128, :])))(),
                inc=("ld", 16))
        p.op("sync", lambda e: e.dma_start(out=nb[:], in_=bdt_t), inc=("ld", 16))
        p.op("scalar", lambda e: e.copy(out=wdt_b[:], in_=wdt_f[:]),
             waits=p.dep("ld"), inc=("act", 1))
        p.op("scalar", lambda e: e.copy(out=wbc_b[:], in_=wbc_f[:]),
             waits=p.dep("ld"), inc=("act", 1))
        p.op("vector", lambda e: e.tensor_scalar_mul(out=nb[:], in0=nb[:], scalar1=-1.0),
             waits=p.dep("ld"), inc=("dve", 1))
        p.op("vector", lambda e: e.memset(onesP[:], 1.0), inc=("dve", 1))
        p.op("vector", lambda e: e.memset(ones1[:], 1.0), inc=("dve", 1))
        p.op("vector", lambda e: e.memset(cst[:, 0:1], 1e-5), inc=("dve", 1))
        p.op("vector", lambda e: e.memset(cst[:, 0:1], 1e-5), inc=("dve", 1))
        p.op("vector", lambda e: e.memset(cst[:, 1:2], 1e-38), inc=("dve", 1))

        for it in range(ntile):
            b, hf = divmod(it, TL // C1)
            cs = hf * C1

            start = p.snap()
            first = {e: True for e in ENGS}

            def op(eng, fn, waits=(), inc=None, self_wait=False):
                w = list(waits)
                if first[eng]:
                    w += [(s, v) for s, v in start.items()]
                    first[eng] = False
                p.op(eng, fn, w, inc, self_wait=self_wait)

            for c in range(NCH):
                op("sync", (lambda b=b, c=c, cs=cs: (lambda e: e.dma_start(
                    out=ubuf[:, c * C1:(c + 1) * C1],
                    in_=u_fm[b, c * 128:(c + 1) * 128, cs:cs + C1])))(),
                    inc=("ld", 16))
            for c in range(NCH):
                op("scalar", (lambda c=c: (lambda e: e.activation(
                    out=usq[:, c * C1:(c + 1) * C1],
                    in_=ubuf[:, c * C1:(c + 1) * C1], func=AF.Square)))(),
                    waits=p.dep("ld"), inc=("act", 1))
            for c in range(NCH):
                op("tensor", (lambda c=c: (lambda e: e.matmul(
                    ps_sx[:, :], onesP[:], ubuf[:, c * C1:(c + 1) * C1],
                    start=(c == 0), stop=(c == NCH - 1))))(),
                    waits=p.dep("ld"),
                    inc=("pe", 1) if c == NCH - 1 else None)
            for c in range(NCH):
                op("tensor", (lambda c=c: (lambda e: e.matmul(
                    ps_sx2[:, :], onesP[:], usq[:, c * C1:(c + 1) * C1],
                    start=(c == 0), stop=(c == NCH - 1))))(),
                    waits=p.dep("act"),
                    inc=("pe", 1) if c == NCH - 1 else None)
            op("scalar", lambda e: e.activation(out=m_row[:], in_=ps_sx[:],
                                                func=AF.Copy, scale=1.0 / D),
               waits=p.dep("pe"), inc=("act", 1))
            op("scalar", lambda e: e.activation(out=msq_row[:], in_=ps_sx[:],
                                                func=AF.Square, scale=1.0 / D),
               waits=p.dep("pe"), inc=("act", 1))
            op("vector", lambda e: e.scalar_tensor_tensor(
                out=var_row[:], in0=ps_sx2[:], scalar=1.0 / D, in1=msq_row[:],
                op0=ALU.mult, op1=ALU.subtract),
               waits=p.dep("pe", "act"), inc=("dve", 1))
            op("scalar", lambda e: e.activation(out=rstd_row[:], in_=var_row[:],
                                                func=AF.Sqrt, bias=cst[0:1, 0:1]),
               waits=p.dep("dve"), inc=("act", 1))
            op("vector", lambda e: e.reciprocal(out=rstd_row[:], in_=rstd_row[:]),
               waits=p.dep("act"), inc=("dve", 1))
            op("vector", lambda e: e.tensor_mul(out=mr_row[:], in0=m_row[:], in1=rstd_row[:]),
               waits=p.dep("act", "dve"), inc=("dve", 1), self_wait=True)
            op("tensor", lambda e: e.matmul(ps_bc1[:, :], ones1[:], rstd_row[:],
                                            start=True, stop=True),
               waits=p.dep("act", "dve"), inc=("pe", 1))
            op("tensor", lambda e: e.matmul(ps_bc2[:, :], ones1[:], mr_row[:],
                                            start=True, stop=True),
               waits=p.dep("dve"), inc=("pe", 1))
            for c in range(NCH):
                op("vector", (lambda c=c: (lambda e: e.tensor_mul(
                    out=xln[:, c * C1:(c + 1) * C1],
                    in0=ubuf[:, c * C1:(c + 1) * C1], in1=ps_bc1[:, :])))(),
                    waits=p.dep("pe"), inc=("dve", 1))
                op("vector", (lambda c=c: (lambda e: e.tensor_sub(
                    out=xln[:, c * C1:(c + 1) * C1],
                    in0=xln[:, c * C1:(c + 1) * C1], in1=ps_bc2[:, :])))(),
                    inc=("dve", 1))
                op("scalar", (lambda c=c: (lambda e: e.copy(
                    out=xln_b[:, c * C1:(c + 1) * C1],
                    in_=xln[:, c * C1:(c + 1) * C1])))(),
                    waits=p.dep("dve"), inc=("act", 1))
                op("gpsimd", (lambda b=b, c=c, cs=cs: (lambda e: e.dma_start(
                    out=st_xln[c, b, :, cs:cs + C1],
                    in_=xln[:, c * C1:(c + 1) * C1])))(),
                    waits=p.dep("dve"), inc=("st", 16))
            for ec in range(NCH):
                for kc in range(NCH):
                    op("tensor", (lambda ec=ec, kc=kc: (lambda e: e.matmul(
                        ps_dt[:, :],
                        wdt_b[:, kc * D + ec * 128: kc * D + (ec + 1) * 128],
                        xln_b[:, kc * C1:(kc + 1) * C1],
                        start=(kc == 0), stop=(kc == NCH - 1))))(),
                        waits=p.dep("act"),
                        inc=("pe", 1) if kc == NCH - 1 else None)
                op("scalar", (lambda ec=ec: (lambda e: e.activation(
                    out=stile[:], in_=ps_dt[:], func=AF.Sigmoid,
                    bias=nb[:, ec:ec + 1], scale=-1.0)))(),
                    waits=p.dep("pe"), inc=("act", 1))
                op("scalar", (lambda ec=ec: (lambda e: e.activation(
                    out=lnS[:, ec * C1:(ec + 1) * C1], in_=stile[:],
                    func=AF.Ln, bias=cst[:, 1:2])))(),
                    inc=("act", 1))
                op("gpsimd", (lambda b=b, ec=ec, cs=cs: (lambda e: e.dma_start(
                    out=st_lns[ec, b, :, cs:cs + C1],
                    in_=lnS[:, ec * C1:(ec + 1) * C1])))(),
                    waits=p.dep("act"), inc=("st", 16))
            for kc in range(NCH):
                op("tensor", (lambda kc=kc: (lambda e: e.matmul(
                    ps_B[:, :], wbc_b[:, kc * NS:(kc + 1) * NS],
                    xln_b[:, kc * C1:(kc + 1) * C1],
                    start=(kc == 0), stop=(kc == NCH - 1))))(),
                    waits=p.dep("act"),
                    inc=("pe", 1) if kc == NCH - 1 else None)
            for kc in range(NCH):
                op("tensor", (lambda kc=kc: (lambda e: e.matmul(
                    ps_C[:, :], wbc_b[:, (NCH + kc) * NS:(NCH + kc + 1) * NS],
                    xln_b[:, kc * C1:(kc + 1) * C1],
                    start=(kc == 0), stop=(kc == NCH - 1))))(),
                    inc=("pe", 1) if kc == NCH - 1 else None)
            op("scalar", lambda e: e.copy(out=brow[:], in_=ps_B[:]),
               waits=p.dep("pe"), inc=("act", 1))
            op("scalar", lambda e: e.copy(out=crow[:], in_=ps_C[:]),
               inc=("act", 1))
            op("gpsimd", (lambda b=b, cs=cs: (lambda e: e.dma_start(
                out=st_bc[b, 0:NS, cs:cs + C1], in_=brow[:])))(),
                waits=p.dep("act"), inc=("st", 16))
            op("gpsimd", (lambda b=b, cs=cs: (lambda e: e.dma_start(
                out=st_bc[b, NS:2 * NS, cs:cs + C1], in_=crow[:])))(),
                inc=("st", 16))

        if SKIP_CC:
            p.op("gpsimd", lambda e: e.sem_inc(sem["gpc"], 3), waits=p.dep("st"))
            p.counts["gpc"] += 3
            with nc.Block() as block:
                _replay_all(p, block, sem)
            return
        p.op("gpsimd", lambda e: e.collective_compute(
            "AllGather", ALU.bypass, replica_groups=[list(range(NCORES))],
            ins=[io["st_xln"].opt()], outs=[io["st_xln_G"].opt()]),
            waits=p.dep("st"), inc=("gpc", 1))
        p.op("gpsimd", lambda e: e.collective_compute(
            "AllGather", ALU.bypass, replica_groups=[list(range(NCORES))],
            ins=[io["st_lns"].opt()], outs=[io["st_lns_G"].opt()]),
            waits=p.dep("gpc"), inc=("gpc", 1))
        p.op("gpsimd", lambda e: e.collective_compute(
            "AllGather", ALU.bypass, replica_groups=[list(range(NCORES))],
            ins=[io["st_bc"].opt()], outs=[io["st_bc_g"].opt()]),
            waits=p.dep("gpc"), inc=("gpc", 1))

        with nc.Block() as block:
            _replay_all(p, block, sem)



def emit_l2(nc, p, sem, io):
    logA_d = io["logA_d"]; Dp_d = io["Dp_d"]
    id128 = io["id128"]; rowsel = io["rowsel"]
    st_xln_g = io["st_xln_loc"]; st_lns_g = io["st_lns_loc"]; st_bc_g = io["st_bc_g"]
    st_y = io["st_y"]

    TC = 1024
    with contextlib.ExitStack() as stk:

        def sb(name, shape, dt=F32):
            return stk.enter_context(nc.sbuf_tensor("L2" + name, shape, dt))

        Ub = sb("Ub", [128, TG])
        Lb = sb("Lb", [128, TG])
        dut = sb("dut", [128, TC])
        Bm_sb = sb("Bm_sb", [NS, TG])
        Cm_sb = sb("Cm_sb", [NS, TG])
        logA_sb = sb("logA_sb", [128, NS])
        scaleP = sb("scaleP", [128, NS])
        Dp_sb = sb("Dp_sb", [128, 1])
        id_sb = sb("id_sb", [128, 128])
        rs_sb = sb("rs_sb", [NS, NS * 128])
        ones1 = sb("ones1", [1, 128])
        da = sb("da", [128, TC])
        dbu = sb("dbu", [128, TC])
        xbuf = sb("xbuf", [128, TC])
        zbuf = sb("zbuf", [128, TC])
        state = sb("state", [128, B * NS])
        yout = sb("yout", [128, TC])

        ps_Bbc = stk.enter_context(nc.psum_tensor("L2ps_Bbc", [128, TC], F32))
        ps_Cbc = stk.enter_context(nc.psum_tensor("L2ps_Cbc", [128, TC], F32))
        ps_y = stk.enter_context(nc.psum_tensor("L2ps_y", [128, TC], F32))

        sect = p.snap()
        p.op("sync", lambda e: e.dma_start(out=logA_sb[:], in_=logA_d),
             waits=[(s, v) for s, v in sect.items()], inc=("ld", 16))
        p.op("sync", lambda e: e.dma_start(out=Dp_sb[:], in_=Dp_d), inc=("ld", 16))
        p.op("sync", lambda e: e.dma_start(out=id_sb[:], in_=id128), inc=("ld", 16))
        p.op("sync", lambda e: e.dma_start(out=rs_sb[:], in_=rowsel), inc=("ld", 16))
        p.op("scalar", lambda e: e.activation(out=scaleP[:], in_=logA_sb[:], func=AF.Exp),
             waits=p.dep("ld"), inc=("act", 1))
        p.op("vector", lambda e: e.memset(ones1[:], 1.0), inc=("dve", 1))

        for b in range(B):
            start = p.snap()
            first = {e: True for e in ENGS}

            def op(eng, fn, waits=(), inc=None, self_wait=False):
                w = list(waits)
                if first[eng]:
                    w += [(s, v) for s, v in start.items()]
                    first[eng] = False
                p.op(eng, fn, w, inc, self_wait=self_wait)

            for sc_ in range(NCORES):
                op("sync", (lambda b=b, sc_=sc_: (lambda e: e.dma_start(
                    out=Ub[:, sc_ * TL:(sc_ + 1) * TL],
                    in_=st_xln_g[sc_, b])))(), inc=("ld", 16))
                op("sync", (lambda b=b, sc_=sc_: (lambda e: e.dma_start(
                    out=Lb[:, sc_ * TL:(sc_ + 1) * TL],
                    in_=st_lns_g[sc_, b])))(), inc=("ld", 16))
                op("sync", (lambda b=b, sc_=sc_: (lambda e: e.dma_start(
                    out=Bm_sb[:, sc_ * TL:(sc_ + 1) * TL],
                    in_=st_bc_g[sc_, b, 0:NS])))(), inc=("ld", 16))
                op("sync", (lambda b=b, sc_=sc_: (lambda e: e.dma_start(
                    out=Cm_sb[:, sc_ * TL:(sc_ + 1) * TL],
                    in_=st_bc_g[sc_, b, NS:2 * NS])))(), inc=("ld", 16))
            for ti in range(TG // TC):
                cs = ti * TC
                op("vector", (lambda cs=cs: (lambda e: e.scalar_tensor_tensor(
                    out=dut[:], in0=Lb[:, cs:cs + TC], scalar=-1.0,
                    in1=Ub[:, cs:cs + TC], op0=ALU.mult, op1=ALU.mult)))(),
                    waits=p.dep("ld", "pe"), inc=("dve", 1))
                for n in range(NS):
                    sidx = b * NS + n
                    for h2 in range(2):
                        op("tensor", (lambda n=n, cs=cs, h2=h2: (lambda e: e.matmul(
                            ps_Bbc[:, h2 * 512:(h2 + 1) * 512],
                            rs_sb[:, n * 128:(n + 1) * 128],
                            Bm_sb[0:NS, cs + h2 * 512: cs + (h2 + 1) * 512],
                            start=True, stop=True)))(),
                            waits=p.dep("dve"),
                            inc=("pe", 1) if h2 == 1 else None)
                        op("tensor", (lambda n=n, cs=cs, h2=h2: (lambda e: e.matmul(
                            ps_Cbc[:, h2 * 512:(h2 + 1) * 512],
                            rs_sb[:, n * 128:(n + 1) * 128],
                            Cm_sb[0:NS, cs + h2 * 512: cs + (h2 + 1) * 512],
                            start=True, stop=True)))(),
                            inc=("pe", 1) if h2 == 1 else None)
                    op("scalar", (lambda n=n, cs=cs: (lambda e: e.activation(
                        out=da[:], in_=Lb[:, cs:cs + TC], func=AF.Exp,
                        scale=scaleP[:, n:n + 1])))(),
                        waits=p.dep("dve"), inc=("act", 1))
                    op("vector", (lambda: (lambda e: e.tensor_mul(
                        out=dbu[:], in0=dut[:], in1=ps_Bbc[:, :])))(),
                        waits=p.dep("pe", "act"), inc=("dve", 1))
                    if ti == 0:
                        op("vector", (lambda: (lambda e: e.tensor_tensor_scan(
                            out=xbuf[:], data0=da[:], data1=dbu[:], initial=0.0,
                            op0=ALU.mult, op1=ALU.add)))(),
                            inc=("dve", 1))
                    else:
                        op("vector", (lambda sidx=sidx: (lambda e: e.tensor_tensor_scan(
                            out=xbuf[:], data0=da[:], data1=dbu[:],
                            initial=state[:, sidx:sidx + 1],
                            op0=ALU.mult, op1=ALU.add)))(),
                            waits=p.dep("act"), inc=("dve", 1))
                    op("scalar", (lambda sidx=sidx: (lambda e: e.copy(
                        out=state[:, sidx:sidx + 1], in_=xbuf[:, TC - 1:TC])))(),
                        waits=p.dep("dve"), inc=("act", 1))
                    op("vector", (lambda: (lambda e: e.tensor_mul(
                        out=zbuf[:], in0=xbuf[:], in1=ps_Cbc[:, :])))(),
                        waits=p.dep("pe"), inc=("dve", 1))
                    for h2 in range(2):
                        op("tensor", (lambda n=n, h2=h2: (lambda e: e.matmul(
                            ps_y[:, h2 * 512:(h2 + 1) * 512], id_sb[:],
                            zbuf[:, h2 * 512:(h2 + 1) * 512],
                            start=(n == 0), stop=(n == NS - 1))))(),
                            waits=p.dep("dve"),
                            inc=("pe", 1) if h2 == 1 else None)
                op("vector", (lambda cs=cs: (lambda e: e.scalar_tensor_tensor(
                    out=yout[:], in0=Ub[:, cs:cs + TC], scalar=Dp_sb[:],
                    in1=ps_y[:, :], op0=ALU.mult, op1=ALU.add)))(),
                    waits=p.dep("pe"), inc=("dve", 1))
                op("gpsimd", (lambda b=b, ti=ti: (lambda e: e.dma_start(
                    out=st_y[ti, b], in_=yout[:])))(),
                    waits=p.dep("dve"), inc=("st", 16))

        p.op("gpsimd", lambda e: e.collective_compute(
            "AllGather", ALU.bypass, replica_groups=[list(range(NCORES))],
            ins=[io["st_y"].opt()], outs=[io["st_y_G"].opt()]),
            waits=p.dep("st"), inc=("gpc", 1))
        with nc.Block() as block:
            _replay_all(p, block, sem)



def emit_l3(nc, p, sem, io):
    u3 = io["u3"]; wq_t = io["wq_t"]; wk_t = io["wk_t"]; wv_t = io["wv_t"]
    wo_t = io["wo_t"]; bo_t = io["bo_t"]; krl = io["krl"]; krr = io["krr"]
    bias_ri = io["bias_ri"]; sel_c = io["sel_c"]; rk_in = io["rk_in"]
    exp_in = io["exp_in"]; rv_in = io["rv_in"]
    st_y_g = io["st_y_loc"]; out_h = io["out_h"]

    CT = 512
    NT = TL * B // CT

    with contextlib.ExitStack() as stk:

        def sb(name, shape, dt=F32):
            return stk.enter_context(nc.sbuf_tensor("L3" + name, shape, dt))

        w_bf = {}
        for nm in ("wq", "wk", "wv", "wo"):
            w_bf[nm] = sb(nm + "_bf", [128, NCH * D], BF16)
        wrt_bf = sb("wrt_bf", [128, 4 * 512], BF16)
        wit_bf = sb("wit_bf", [128, 4 * 512], BF16)
        witn_bf = sb("witn_bf", [128, 4 * 512], BF16)
        bo_sb = sb("bo_sb", [128, NCH])
        bri_sb = sb("bri_sb", [128, NCH])
        sel_sb = sb("sel_sb", [128, NCH * H])
        rk_sb = sb("rk_sb", [128, B * B * NCH * H])
        exp_sb = sb("exp_sb", [H, NCH * 128])
        rv_sb = sb("rv_sb", [128, H])
        onesP = sb("onesP", [128, 1])
        ones1 = sb("ones1", [1, 128])
        cst = sb("cst", [128, 2])
        zrow = sb("zrow", [1, CT])

        xA = sb("xA", [128, NCH * CT])
        xB = sb("xB", [128, NCH * CT])
        ybuf4 = sb("ybuf4", [128, CT])
        xln_b = sb("xln_b", [128, NCH * CT], BF16)
        tmp = sb("tmp", [128, CT])
        qtile = sb("qtile", [128, CT])
        ktile = sb("ktile", [128, CT])
        vf = sb("vf", [128, NCH * CT])
        usq = vf
        zbuf = sb("zbuf", [128, CT])
        scbuf = sb("scbuf", [H, B * CT])
        rrow = sb("rrow", [1, CT])
        vrv = sb("vrv", [128, CT])
        vz = sb("vz", [128, CT])
        aout_b = sb("aout_b", [128, NCH * CT], BF16)
        m_row = sb("m_row", [1, CT])
        msq_row = sb("msq_row", [1, CT])
        var_row = sb("var_row", [1, CT])
        rstd_row = sb("rstd_row", [1, CT])
        mr_row = sb("mr_row", [1, CT])

        ps_bc1 = stk.enter_context(nc.psum_tensor("L3ps_bc1", [128, CT], F32))
        ps_bc2 = stk.enter_context(nc.psum_tensor("L3ps_bc2", [128, CT], F32))
        ps_mm = stk.enter_context(nc.psum_tensor("L3ps_mm", [128, CT], F32))
        ps_sc_all = [stk.enter_context(nc.psum_tensor(f"ps_sc{s}", [128, CT], F32))
                     for s in range(4)]
        ps_sx = ps_sc_all[0]
        ps_sx2 = ps_sc_all[1]

        sect = p.snap()
        sect_first = [True]

        def ld(dst, src):
            w = []
            if sect_first[0]:
                w = [(s, v) for s, v in sect.items()]
                sect_first[0] = False
            p.op("sync", (lambda dst=dst, src=src: (lambda e: e.dma_start(
                out=dst, in_=src)))(), waits=w, inc=("ld", 16))

        ld(bo_sb[:], bo_t)
        ld(bri_sb[:], bias_ri)
        ld(sel_sb[:], sel_c)
        ld(rk_sb[:], rk_in)
        ld(exp_sb[:], exp_in)
        ld(rv_sb[:], rv_in)
        p.op("vector", lambda e: e.memset(onesP[:], 1.0), inc=("dve", 1))
        p.op("vector", lambda e: e.memset(ones1[:], 1.0), inc=("dve", 1))
        p.op("vector", lambda e: e.memset(cst[:, 0:1], 1e-5), inc=("dve", 1))
        p.op("vector", lambda e: e.memset(zrow[:], 0.0), inc=("dve", 1))

        for ri in range(2):
            dstf = wrt_bf if ri == 0 else wit_bf
            for c in range(4):
                for pc in range(4):
                    p.op("sync", (lambda ri=ri, c=c, pc=pc: (lambda e: e.dma_start(
                        out=xA[:, 0:512],
                        in_=krr[ri, c, pc * 128:(pc + 1) * 128, :])))(),
                        waits=p.dep("pe"), inc=("ld", 16))
                    p.op("sync", (lambda ri=ri, c=c, pc=pc: (lambda e: e.dma_start(
                        out=xA[:, 512:1024],
                        in_=krl[ri, c, pc * 128:(pc + 1) * 128, :])))(),
                        inc=("ld", 16))
                    for qc in range(4):
                        p.op("tensor", (lambda qc=qc, c=c, pc=pc: (lambda e: e.matmul(
                            ps_sc_all[qc][:, :],
                            xA[:, 512 + qc * 128: 512 + (qc + 1) * 128],
                            xA[:, 0:512],
                            start=(c == 0 and pc == 0),
                            stop=(c == 3 and pc == 3))))(),
                            waits=p.dep("ld", "act"),
                            inc=("pe", 1) if qc == 3 else None)
            for qc in range(4):
                p.op("scalar", (lambda qc=qc, dstf=dstf: (lambda e: e.copy(
                    out=dstf[:, qc * 512:(qc + 1) * 512],
                    in_=ps_sc_all[qc][:, :])))(),
                    waits=p.dep("pe"), inc=("act", 1))
                if ri == 1:
                    p.op("scalar", (lambda qc=qc: (lambda e: e.activation(
                        out=witn_bf[:, qc * 512:(qc + 1) * 512],
                        in_=ps_sc_all[qc][:, :], func=AF.Copy, scale=-1.0)))(),
                        inc=("act", 1))
        for nm, srcw in (("wq", wq_t), ("wk", wk_t), ("wv", wv_t), ("wo", wo_t)):
            for kc in range(NCH):
                p.op("sync", (lambda kc=kc, srcw=srcw: (lambda e: e.dma_start(
                    out=xA[:, (kc % 4) * 1024:(kc % 4 + 1) * 1024],
                    in_=srcw[kc * 128:(kc + 1) * 128, :])))(),
                    waits=p.dep("act"), inc=("ld", 16))
                p.op("scalar", (lambda nm=nm, kc=kc: (lambda e: e.copy(
                    out=w_bf[nm][:, kc * D:(kc + 1) * D],
                    in_=xA[:, (kc % 4) * 1024:(kc % 4 + 1) * 1024])))(),
                    waits=p.dep("ld"), inc=("act", 1))

        def ln_block(op, xsrc, dep_sem):
            for c in range(NCH):
                op("scalar", (lambda c=c: (lambda e: e.activation(
                    out=usq[:, c * CT:(c + 1) * CT],
                    in_=xsrc[:, c * CT:(c + 1) * CT], func=AF.Square)))(),
                    waits=p.dep(dep_sem), inc=("act", 1))
            for c in range(NCH):
                op("tensor", (lambda c=c: (lambda e: e.matmul(
                    ps_sx[0:1, :], onesP[:], xsrc[:, c * CT:(c + 1) * CT],
                    start=(c == 0), stop=(c == NCH - 1))))(),
                    waits=p.dep(dep_sem),
                    inc=("pe", 1) if c == NCH - 1 else None)
            for c in range(NCH):
                op("tensor", (lambda c=c: (lambda e: e.matmul(
                    ps_sx2[0:1, :], onesP[:], usq[:, c * CT:(c + 1) * CT],
                    start=(c == 0), stop=(c == NCH - 1))))(),
                    waits=p.dep("act"),
                    inc=("pe", 1) if c == NCH - 1 else None)
            op("scalar", lambda e: e.activation(out=m_row[:], in_=ps_sx[0:1, :],
                                                func=AF.Copy, scale=1.0 / D),
               waits=p.dep("pe"), inc=("act", 1))
            op("scalar", lambda e: e.activation(out=msq_row[:], in_=ps_sx[0:1, :],
                                                func=AF.Square, scale=1.0 / D),
               waits=p.dep("pe"), inc=("act", 1))
            op("vector", lambda e: e.scalar_tensor_tensor(
                out=var_row[:], in0=ps_sx2[0:1, :], scalar=1.0 / D, in1=msq_row[:],
                op0=ALU.mult, op1=ALU.subtract),
               waits=p.dep("pe", "act"), inc=("dve", 1))
            op("scalar", lambda e: e.activation(out=rstd_row[:], in_=var_row[:],
                                                func=AF.Sqrt, bias=cst[0:1, 0:1]),
               waits=p.dep("dve"), inc=("act", 1))
            op("vector", lambda e: e.reciprocal(out=rstd_row[:], in_=rstd_row[:]),
               waits=p.dep("act"), inc=("dve", 1))
            op("vector", lambda e: e.tensor_mul(out=mr_row[:], in0=m_row[:],
                                                in1=rstd_row[:]),
               waits=p.dep("act", "dve"), inc=("dve", 1), self_wait=True)
            op("tensor", lambda e: e.matmul(ps_bc1[:, :], ones1[:], rstd_row[:],
                                            start=True, stop=True),
               waits=p.dep("act", "dve"), inc=("pe", 1))
            op("tensor", lambda e: e.matmul(ps_bc2[:, :], ones1[:], mr_row[:],
                                            start=True, stop=True),
               waits=p.dep("dve"), inc=("pe", 1))
            for c in range(NCH):
                op("vector", (lambda c=c: (lambda e: e.tensor_mul(
                    out=tmp[:], in0=xsrc[:, c * CT:(c + 1) * CT], in1=ps_bc1[:, :])))(),
                    waits=p.dep("pe"), inc=("dve", 1))
                op("vector", (lambda c=c: (lambda e: e.tensor_tensor(
                    out=xln_b[:, c * CT:(c + 1) * CT], in0=tmp[:], in1=ps_bc2[:, :],
                    op=ALU.subtract)))(),
                    inc=("dve", 1))

        def ffn(op, xres, xdst):
            for oc in range(NCH):
                half = oc // 4
                o128 = (oc % 4) * 128
                wa, wb2 = (wrt_bf, witn_bf) if half == 0 else (wit_bf, wrt_bf)
                for qc in range(4):
                    op("tensor", (lambda wa=wa, qc=qc, o128=o128: (lambda e: e.matmul(
                        ps_mm[:, :], wa[:, qc * 512 + o128: qc * 512 + o128 + 128],
                        xln_b[:, qc * CT:(qc + 1) * CT],
                        start=(qc == 0), stop=False)))(),
                        waits=p.dep("dve", "act") if qc == 0 else ())
                for qc in range(4):
                    op("tensor", (lambda wb2=wb2, qc=qc, o128=o128: (lambda e: e.matmul(
                        ps_mm[:, :], wb2[:, qc * 512 + o128: qc * 512 + o128 + 128],
                        xln_b[:, (4 + qc) * CT:(4 + qc + 1) * CT],
                        start=False, stop=(qc == 3))))(),
                        inc=("pe", 1) if qc == 3 else None)
                op("vector", (lambda oc=oc: (lambda e: e.scalar_tensor_tensor(
                    out=xdst[:, oc * CT:(oc + 1) * CT], in0=ps_mm[:, :],
                    scalar=bri_sb[:, oc:oc + 1],
                    in1=xres[:, oc * CT:(oc + 1) * CT],
                    op0=ALU.add, op1=ALU.add)))(),
                    waits=p.dep("pe"), inc=("dve", 1))

        def proj_chunk(op, wname, dst, oc, rhs_bf, evac_eng="scalar"):
            for kc in range(NCH):
                op("tensor", (lambda wname=wname, oc=oc, kc=kc: (lambda e: e.matmul(
                    ps_mm[:, :],
                    w_bf[wname][:, kc * D + oc * 128: kc * D + (oc + 1) * 128],
                    rhs_bf[:, kc * CT:(kc + 1) * CT],
                    start=(kc == 0), stop=(kc == NCH - 1))))(),
                    waits=p.dep("dve", "act") if kc == 0 else (),
                    inc=("pe", 1) if kc == NCH - 1 else None)
            op(evac_eng, (lambda dst=dst: (lambda e: e.copy(
                out=dst, in_=ps_mm[:, :])))(),
                waits=p.dep("pe"), inc=("act", 1))

        for it in range(NT):
            cs = it * CT
            start = p.snap()
            first = {e: True for e in ENGS}

            def op(eng, fn, waits=(), inc=None, self_wait=False):
                w = list(waits)
                if first[eng]:
                    w += [(s, v) for s, v in start.items()]
                    first[eng] = False
                p.op(eng, fn, w, inc, self_wait=self_wait)

            u3v = u3.rearrange("d t b -> d (t b)")
            outv = out_h.rearrange("d t b -> d (t b)")

            tau0 = it * 128
            for c in range(NCH):
                op("sync", (lambda c=c, cs=cs: (lambda e: e.dma_start(
                    out=xA[:, c * CT:(c + 1) * CT],
                    in_=u3v[c * 128:(c + 1) * 128, cs:cs + CT])))(),
                    waits=p.dep("dve"), inc=("ld", 16))
                for bb in range(B):
                    op("sync", (lambda c=c, bb=bb, tau0=tau0: (lambda e: e.dma_start(
                        out=ybuf4[:, bb * 128:(bb + 1) * 128],
                        in_=st_y_g[c, bb, :, tau0:tau0 + 128])))(),
                        waits=p.dep("dve") if bb == 0 else (), inc=("ld", 16))
                op("vector", (lambda c=c: (lambda e: e.tensor_add(
                    out=xA[:, c * CT:(c + 1) * CT].rearrange(
                        "p (t b) -> p t b", b=B),
                    in0=xA[:, c * CT:(c + 1) * CT].rearrange(
                        "p (t b) -> p t b", b=B),
                    in1=ybuf4[:].rearrange("p (b t) -> p t b", b=B))))(),
                    waits=p.dep("ld"), inc=("dve", 1))

            ln_block(op, xA, "dve")
            ffn(op, xA, xB)

            ln_block(op, xB, "dve")

            for oc in range(NCH):
                proj_chunk(op, "wv", vf[:, oc * CT:(oc + 1) * CT], oc, xln_b)

            for c in range(NCH):
                proj_chunk(op, "wq", qtile[:, :], c, xln_b)
                proj_chunk(op, "wk", ktile[:, :], c, xln_b)
                for s in range(B):
                    qs = qtile[:].rearrange("p (t b) -> p t b", b=B)[:, :, s:s + 1]
                    op("vector", (lambda qs=qs: (lambda e: e.tensor_tensor(
                        out=zbuf[:].rearrange("p (t b) -> p t b", b=B),
                        in0=qs.broadcast_to([128, CT // B, B]),
                        in1=ktile[:].rearrange("p (t b) -> p t b", b=B),
                        op=ALU.mult)))(),
                        waits=p.dep("act", "pe"), inc=("dve", 1))
                    if c == 0:
                        op("tensor", (lambda s=s: (lambda e: e.matmul(
                            ps_sc_all[s][:, :], zrow[:, 0:128], zrow[:],
                            start=True, stop=False,
                            skip_group_check=True)))(),
                            waits=p.dep("dve"), inc=("pe", 1))
                    op("tensor", (lambda s=s, c=c: (lambda e: e.matmul(
                        ps_sc_all[s][0:16, :], sel_sb[:, c * H:(c + 1) * H],
                        zbuf[:],
                        start=False, stop=False,
                        skip_group_check=True)))(),
                        waits=p.dep("dve"), inc=("pe", 1))
                    for t in range(B):
                        rk_tile = rk_sb[:, ((s * B + t) * NCH + c) * H:
                                        ((s * B + t) * NCH + c + 1) * H]
                        qsv = qtile[:].rearrange("p (t b) -> p b t", b=B)[:, s, :]
                        op("tensor", (lambda t=t, s=s, c=c, rk_tile=rk_tile, qsv=qsv:
                            (lambda e: e.matmul(
                                ps_sc_all[s][32:48, t * 128:(t + 1) * 128],
                                rk_tile, qsv,
                                start=False,
                                stop=(c == NCH - 1 and t == B - 1),
                                skip_group_check=True)))(),
                            inc=("pe", 1) if c == NCH - 1 else None)
            for s in range(B):
                op("scalar", (lambda s=s: (lambda e: e.copy(
                    out=scbuf[:, s * CT:(s + 1) * CT],
                    in_=ps_sc_all[s][0:16, :])))(),
                    waits=p.dep("pe"), inc=("act", 1))
                op("vector", (lambda s=s: (lambda e: e.tensor_add(
                    out=scbuf[:, s * CT:(s + 1) * CT].rearrange(
                        "h (t b) -> h t b", b=B),
                    in0=scbuf[:, s * CT:(s + 1) * CT].rearrange(
                        "h (t b) -> h t b", b=B),
                    in1=ps_sc_all[s][32:48, :].rearrange(
                        "h (t tau) -> h tau t", tau=128))))(),
                    waits=p.dep("pe", "act"), inc=("dve", 1))
                op("scalar", (lambda s=s: (lambda e: e.activation(
                    out=scbuf[:, s * CT:(s + 1) * CT],
                    in_=scbuf[:, s * CT:(s + 1) * CT],
                    func=AF.Exp, scale=0.125)))(),
                    waits=p.dep("dve"), inc=("act", 1))
                op("tensor", (lambda s=s: (lambda e: e.matmul(
                    ps_bc2[0:1, :], onesP[0:16, :],
                    scbuf[:, s * CT:(s + 1) * CT],
                    start=True, stop=True)))(),
                    waits=p.dep("act"), inc=("pe", 1))
                op("vector", lambda e: e.reciprocal(out=rrow[:], in_=ps_bc2[0:1, :]),
                   waits=p.dep("pe"), inc=("dve", 1))
                op("tensor", lambda e: e.matmul(
                    ps_bc2[32:48, :], ones1[:, 0:16], rrow[:],
                    start=True, stop=True),
                   waits=p.dep("dve"), inc=("pe", 1))
                op("vector", (lambda s=s: (lambda e: e.tensor_mul(
                    out=scbuf[:, s * CT:(s + 1) * CT],
                    in0=scbuf[:, s * CT:(s + 1) * CT],
                    in1=ps_bc2[32:48, :])))(),
                    waits=p.dep("pe"), inc=("dve", 1))
            for s in range(B):
                for c in range(NCH):
                    op("tensor", (lambda s=s, c=c: (lambda e: e.matmul(
                        ps_bc1[:, :], exp_sb[:, c * 128:(c + 1) * 128],
                        scbuf[:, s * CT:(s + 1) * CT],
                        start=True, stop=True)))(),
                        waits=p.dep("dve"), inc=("pe", 1))
                    op("vector", (lambda s=s, c=c: (lambda e: e.tensor_tensor(
                        out=vrv[:].rearrange("p (t b) -> p t b", b=B),
                        in0=vf[:, c * CT:(c + 1) * CT].rearrange(
                            "p (t b) -> p t b", b=B),
                        in1=rv_sb[:, None, s * B:(s + 1) * B].broadcast_to(
                            [128, CT // B, B]),
                        op=ALU.add)))(),
                        inc=("dve", 1))
                    op("vector", (lambda: (lambda e: e.tensor_mul(
                        out=vz[:], in0=ps_bc1[:, :], in1=vrv[:])))(),
                        waits=p.dep("pe"), inc=("dve", 1))
                    op("vector", (lambda s=s, c=c: (lambda e: e.tensor_reduce(
                        out=aout_b[:, c * CT:(c + 1) * CT].rearrange(
                            "p (t b) -> p b t", b=B)[:, s, :],
                        in_=vz[:].rearrange("p (t b) -> p t b", b=B),
                        axis=AX.X, op=ALU.add)))(),
                        inc=("dve", 1))
            for oc in range(NCH):
                for kc in range(NCH):
                    op("tensor", (lambda oc=oc, kc=kc: (lambda e: e.matmul(
                        ps_mm[:, :],
                        w_bf["wo"][:, kc * D + oc * 128: kc * D + (oc + 1) * 128],
                        aout_b[:, kc * CT:(kc + 1) * CT],
                        start=(kc == 0), stop=(kc == NCH - 1))))(),
                        waits=p.dep("dve", "act") if kc == 0 else (),
                        inc=("pe", 1) if kc == NCH - 1 else None)
                op("vector", (lambda oc=oc: (lambda e: e.scalar_tensor_tensor(
                    out=xA[:, oc * CT:(oc + 1) * CT], in0=ps_mm[:, :],
                    scalar=bo_sb[:, oc:oc + 1],
                    in1=xB[:, oc * CT:(oc + 1) * CT],
                    op0=ALU.add, op1=ALU.add)))(),
                    waits=p.dep("pe"), inc=("dve", 1))
            ln_block(op, xA, "dve")
            ffn(op, xA, xB)
            for c in range(NCH):
                op("gpsimd", (lambda c=c, cs=cs: (lambda e: e.dma_start(
                    out=outv[c * 128:(c + 1) * 128, cs:cs + CT],
                    in_=xB[:, c * CT:(c + 1) * CT])))(),
                    waits=p.dep("dve"), inc=("st", 16))

        p.op("sync", lambda e: e.nop(), waits=p.dep("st"))
        with nc.allow_low_precision("DVE reduces in fp32 internally; bf16 is the matmul operand dtype"):
            with nc.Block() as block:
                _replay_all(p, block, sem)




def build_full(upto=3):
    nc = bass.Bass("TRN2", target_bir_lowering=False,
                   detect_race_conditions=False)
    io = {}
    ext_in = [
        ("u_fm", [B, D, TL]), ("wdt_t", [D, D]), ("wb_t", [D, NS]),
        ("wc_t", [D, NS]), ("bdt_t", [128, NCH]),
        ("logA_d", [128, NS]), ("Dp_d", [128, 1]), ("id128", [128, 128]),
        ("rowsel", [NS, NS * 128]),
        ("u3", [D, TL, B]), ("wq_t", [D, D]), ("wk_t", [D, D]),
        ("wv_t", [D, D]), ("wo_t", [D, D]), ("bo_t", [128, NCH]),
        ("krl", [2, 4, 512, 512]), ("krr", [2, 4, 512, 512]),
        ("bias_ri", [128, NCH]), ("sel_c", [128, NCH * H]),
        ("rk_in", [128, B * B * NCH * H]), ("exp_in", [H, NCH * 128]),
        ("rv_in", [128, H]),
    ]
    for nm, shp in ext_in:
        io[nm] = nc.dram_tensor(nm, shp, F32, kind="ExternalInput").ap()
    io["out_h"] = nc.dram_tensor("out_h", [D, TL, B], F32,
                                 kind="ExternalOutput").ap()
    for nm in ("st_xln", "st_lns", "st_y",
               "st_xln_loc", "st_lns_loc", "st_y_loc"):
        io[nm] = nc.dram_tensor(nm, [NCH, B, 128, TL], F32).ap()
    for nm in ("st_xln_G", "st_lns_G", "st_y_G"):
        io[nm] = nc.dram_tensor(nm, [NCORES, NCH, B, 128, TL], F32).ap()
    io["st_bc"] = nc.dram_tensor("st_bc", [B, 2 * NS, TL], F32).ap()
    io["st_bc_g"] = nc.dram_tensor("st_bc_g", [NCORES, B, 2 * NS, TL], F32).ap()

    with contextlib.ExitStack() as stk:
        sem = _mk_sems(nc, stk)
        p = Prog(SEMS)
        emit_l1(nc, p, sem, io)
        gpc_t = p.counts["gpc"]
        ld_t = p.counts["ld"]
        for arm in nc.switch_core_id(engines=[nc.sync], n=128):
            me = arm.logical % NCORES
            nc.sync.wait_ge(sem["gpc"], gpc_t)
            nc.sync.dma_start(out=io["st_xln_loc"],
                              in_=io["st_xln_G"][:, me]).then_inc(sem["ld"], 16)
            nc.sync.dma_start(out=io["st_lns_loc"],
                              in_=io["st_lns_G"][:, me]).then_inc(sem["ld"], 16)
        p.counts["ld"] = ld_t + 32
        _section_barrier(p)
        if upto >= 2:
            emit_l2(nc, p, sem, io)
            gpc_t = p.counts["gpc"]
            ld_t = p.counts["ld"]
            for arm in nc.switch_core_id(engines=[nc.sync], n=128):
                me = arm.logical % NCORES
                nc.sync.wait_ge(sem["gpc"], gpc_t)
                nc.sync.dma_start(out=io["st_y_loc"],
                                  in_=io["st_y_G"][:, me]).then_inc(sem["ld"], 16)
            p.counts["ld"] = ld_t + 16
            _section_barrier(p)
        if upto >= 3:
            emit_l3(nc, p, sem, io)
        if upto < 3:
            with nc.Block() as block:
                _replay_all(p, block, sem)
    return nc


def _section_barrier(p):
    snap = p.snap()
    for eng in ENGS:
        p.op(eng, lambda e: e.nop(), waits=[(s, v) for s, v in snap.items()])


def _host_consts():
    sel = np.zeros((NCH, 128, H), np.float32)
    for c in range(NCH):
        for j in range(2):
            sel[c, j * 64:(j + 1) * 64, 2 * c + j] = 1.0
    expand = np.zeros((NCH, H, 128), np.float32)
    for c in range(NCH):
        for j in range(2):
            expand[c, 2 * c + j, j * 64:(j + 1) * 64] = 1.0
    id128 = np.eye(128, dtype=np.float32)
    return sel, expand, id128


def kernel(**inp):
    u = np.asarray(inp["L0_attn_a_u"], np.float32)
    W_B = np.asarray(inp["L0_attn_a_W_B"], np.float32)
    W_C = np.asarray(inp["L0_attn_a_W_C"], np.float32)
    W_dt = np.asarray(inp["L0_attn_a_W_dt"], np.float32)
    b_dt = np.asarray(inp["L0_attn_a_b_dt"], np.float32)
    log_A = np.asarray(inp["L0_attn_a_log_A"], np.float32)
    D_param = np.asarray(inp["L0_attn_a_D_param"], np.float32)
    w_q = np.asarray(inp["L0_attn_b_w_q"], np.float32)
    w_k = np.asarray(inp["L0_attn_b_w_k"], np.float32)
    w_v = np.asarray(inp["L0_attn_b_w_v"], np.float32)
    w_o = np.asarray(inp["L0_attn_b_w_o"], np.float32)
    b_o = np.asarray(inp["L0_attn_b_b_o"], np.float32)
    rel_k = np.asarray(inp["L0_attn_b_rel_k"], np.float32)
    rel_v = np.asarray(inp["L0_attn_b_rel_v"], np.float32)
    G1_R = np.asarray(inp["L0_ffn_G1_R"], np.float32)
    G2_R = np.asarray(inp["L0_ffn_G2_R"], np.float32)
    G1_I = np.asarray(inp["L0_ffn_G1_I"], np.float32)
    G2_I = np.asarray(inp["L0_ffn_G2_I"], np.float32)
    bias_real = np.asarray(inp["L0_ffn_bias_real"], np.float32)
    bias_imag = np.asarray(inp["L0_ffn_bias_imag"], np.float32)

    sel, expand, id128 = _host_consts()
    sel_in = np.ascontiguousarray(sel.transpose(1, 0, 2).reshape(128, NCH * H))
    exp_in = np.ascontiguousarray(expand.transpose(1, 0, 2).reshape(H, NCH * 128))
    rowsel = np.zeros((NS, NS, 128), np.float32)
    for n in range(NS):
        rowsel[n, n, :] = 1.0
    rowsel = np.ascontiguousarray(rowsel.transpose(1, 0, 2).reshape(NS, NS * 128))

    rel_idx = np.clip(np.arange(B)[:, None] - np.arange(B)[None, :],
                      -MAX_REL, MAX_REL) + MAX_REL
    rk_sel = rel_k[rel_idx]
    rv_sel = rel_v[rel_idx]
    rk_lhsT = np.zeros((B, B, NCH, 128, H), np.float32)
    for s in range(B):
        for t in range(B):
            for c in range(NCH):
                for j in range(2):
                    rk_lhsT[s, t, c, j * 64:(j + 1) * 64, 2 * c + j] = rk_sel[s, t]
    rk_in = np.ascontiguousarray(
        rk_lhsT.transpose(3, 0, 1, 2, 4).reshape(128, B * B * NCH * H))
    rv_in = np.zeros((128, H), np.float32)
    for s in range(B):
        for t in range(B):
            rv_in[:, s * B + t] = np.tile(rv_sel[s, t], 2)
    krl = np.zeros((2, 4, 512, 512), np.float32)
    krr = np.zeros((2, 4, 512, 512), np.float32)
    for ri, (G1x, G2x) in enumerate(((G1_R, G2_R), (G1_I, G2_I))):
        for c in range(4):
            for g in range(16):
                krl[ri, c, g::16, g::16] = G1x[0][:, c, :]
            for b2 in range(32):
                krr[ri, c, b2 * 16:(b2 + 1) * 16, b2 * 16:(b2 + 1) * 16] = \
                    G2x[c, :, 0, :].T
    bias_ri = np.ascontiguousarray(
        np.concatenate([bias_real, bias_imag]).reshape(NCH, 128).T)
    bo_t = np.ascontiguousarray(b_o.reshape(NCH, 128).T)
    bdt_t = np.ascontiguousarray(b_dt.reshape(NCH, 128).T)

    shared = {
        "wdt_t": np.ascontiguousarray(W_dt.T),
        "wb_t": np.ascontiguousarray(W_B.T),
        "wc_t": np.ascontiguousarray(W_C.T),
        "bdt_t": bdt_t, "id128": id128, "rowsel": rowsel,
        "wq_t": np.ascontiguousarray(w_q.T),
        "wk_t": np.ascontiguousarray(w_k.T),
        "wv_t": np.ascontiguousarray(w_v.T),
        "wo_t": np.ascontiguousarray(w_o.T),
        "bo_t": bo_t, "krl": krl, "krr": krr, "bias_ri": bias_ri,
        "sel_c": sel_in, "rk_in": rk_in, "exp_in": exp_in, "rv_in": rv_in,
    }

    nc = build_full()
    in_maps = []
    for k in range(NCORES):
        sl = slice(k * TL, (k + 1) * TL)
        m = dict(shared)
        m["u_fm"] = np.ascontiguousarray(u[:, sl, :].transpose(0, 2, 1))
        m["u3"] = np.ascontiguousarray(u[:, sl, :].transpose(2, 1, 0))
        m["logA_d"] = np.ascontiguousarray(log_A[k * 128:(k + 1) * 128])
        m["Dp_d"] = np.ascontiguousarray(D_param[k * 128:(k + 1) * 128, None])
        in_maps.append(m)
    res = run_bass_kernel_spmd(nc, in_maps, core_ids=list(range(NCORES)))

    out = np.empty((B, TG, D), np.float32)
    for k in range(NCORES):
        oh = res.results[k]["out_h"]
        out[:, k * TL:(k + 1) * TL, :] = oh.transpose(2, 1, 0)
    return out


# version 06
# speedup vs baseline: 1.0482x; 1.0482x over previous
import contextlib
import numpy as np
import concourse.bass as bass
import concourse.mybir as mybir
from concourse.bass_utils import run_bass_kernel_spmd
from concourse.bass_isa import ReduceOp

F32 = mybir.dt.float32
BF16 = mybir.dt.bfloat16
AF = mybir.ActivationFunctionType
ALU = mybir.AluOpType
AX = mybir.AxisListType

NCORES = 8
B = 4
TG = 8192
TL = TG // NCORES
D = 1024
NCH = D // 128
NS = 16
H = 16
DK = 64
MAX_REL = 128

SKIP_CC = False
ENGS = ("sync", "scalar", "vector", "tensor", "gpsimd")


class Prog:

    def __init__(self, sem_names):
        self.counts = {s: 0 for s in sem_names}
        self.owner = {}
        self.ops = {e: [] for e in ENGS}
        self.cursor = {e: 0 for e in ENGS}
        self.last_wait = {e: {} for e in ENGS}

    def snap(self):
        return dict(self.counts)

    def op(self, eng, fn, waits=(), inc=None, self_wait=False):
        w = [(s, v) for (s, v) in waits if v > 0]
        self.ops[eng].append((w, inc, fn, self_wait))
        if inc is not None:
            s, k = inc
            if s in self.owner:
                assert self.owner[s] == eng, (s, eng, self.owner[s])
            self.owner[s] = eng
            self.counts[s] += k

    def dep(self, *sems):
        return [(s, self.counts[s]) for s in sems]

    def replay(self, engine_obj, eng_name, handles):
        last = self.last_wait[eng_name]
        ops = self.ops[eng_name]
        start = self.cursor[eng_name]
        self.cursor[eng_name] = len(ops)
        for waits, inc, fn, self_wait in ops[start:]:
            for s, v in waits:
                if self.owner.get(s) == eng_name and not self_wait:
                    continue
                if last.get(s, 0) >= v:
                    continue
                engine_obj.wait_ge(handles[s], v)
                last[s] = v
            instr = fn(engine_obj)
            if inc is not None:
                s, k = inc
                assert instr is not None
                instr.then_inc(handles[s], k)


def _replay_all(prog, block, handles):
    @block.sync
    def _(e):
        prog.replay(e, "sync", handles)

    @block.scalar
    def _(e):
        prog.replay(e, "scalar", handles)

    @block.vector
    def _(e):
        prog.replay(e, "vector", handles)

    @block.tensor
    def _(e):
        prog.replay(e, "tensor", handles)

    @block.gpsimd
    def _(e):
        prog.replay(e, "gpsimd", handles)


SEMS = ("ld", "pe", "act", "dve", "gpc", "st")


def _mk_sems(nc, st_):
    return {s: st_.enter_context(nc.semaphore(s)) for s in SEMS}



def emit_l1(nc, p, sem, io):
    u_fm = io["u_fm"]; wdt_t = io["wdt_t"]; wb_t = io["wb_t"]
    wc_t = io["wc_t"]; bdt_t = io["bdt_t"]
    st_xln = io["st_xln"]; st_lns = io["st_lns"]; st_bc = io["st_bc"]

    C1 = 512
    ntile = B * (TL // C1)

    with contextlib.ExitStack() as stk:

        def sb(name, shape, dt=F32):
            return stk.enter_context(nc.sbuf_tensor("L1" + name, shape, dt))

        wdt_f = sb("wdt_f", [128, NCH * D])
        wdt_b = sb("wdt_b", [128, NCH * D], BF16)
        wbc_f = sb("wbc_f", [128, 2 * NCH * NS])
        wbc_b = sb("wbc_b", [128, 2 * NCH * NS], BF16)
        nb = sb("nb", [128, NCH])
        onesP = sb("onesP", [128, 1])
        ones1 = sb("ones1", [1, 128])
        ubuf = sb("ubuf", [128, NCH * C1])
        usq = sb("usq", [128, NCH * C1])
        xln = sb("xln", [128, NCH * C1])
        xln_b = sb("xln_b", [128, NCH * C1], BF16)
        lnS = sb("lnS", [128, NCH * C1])
        stile = sb("stile", [128, C1])
        m_row = sb("m_row", [1, C1])
        msq_row = sb("msq_row", [1, C1])
        var_row = sb("var_row", [1, C1])
        rstd_row = sb("rstd_row", [1, C1])
        mr_row = sb("mr_row", [1, C1])
        brow = sb("brow", [NS, C1])
        crow = sb("crow", [NS, C1])
        cst = sb("cst", [128, 2])

        ps_sx = stk.enter_context(nc.psum_tensor("L1ps_sx", [1, C1], F32))
        ps_sx2 = stk.enter_context(nc.psum_tensor("L1ps_sx2", [1, C1], F32))
        ps_bc1 = stk.enter_context(nc.psum_tensor("L1ps_bc1", [128, C1], F32))
        ps_bc2 = stk.enter_context(nc.psum_tensor("L1ps_bc2", [128, C1], F32))
        ps_dt = stk.enter_context(nc.psum_tensor("L1ps_dt", [128, C1], F32))
        ps_B = stk.enter_context(nc.psum_tensor("L1ps_B", [NS, C1], F32))
        ps_C = stk.enter_context(nc.psum_tensor("L1ps_C", [NS, C1], F32))

        for kc in range(NCH):
            p.op("sync", (lambda kc=kc: (lambda e: e.dma_start(
                out=wdt_f[:, kc * D:(kc + 1) * D],
                in_=wdt_t[kc * 128:(kc + 1) * 128, :])))(),
                inc=("ld", 16))
            p.op("sync", (lambda kc=kc: (lambda e: e.dma_start(
                out=wbc_f[:, kc * NS:(kc + 1) * NS],
                in_=wb_t[kc * 128:(kc + 1) * 128, :])))(),
                inc=("ld", 16))
            p.op("sync", (lambda kc=kc: (lambda e: e.dma_start(
                out=wbc_f[:, (NCH + kc) * NS:(NCH + kc + 1) * NS],
                in_=wc_t[kc * 128:(kc + 1) * 128, :])))(),
                inc=("ld", 16))
        p.op("sync", lambda e: e.dma_start(out=nb[:], in_=bdt_t), inc=("ld", 16))
        p.op("scalar", lambda e: e.copy(out=wdt_b[:], in_=wdt_f[:]),
             waits=p.dep("ld"), inc=("act", 1))
        p.op("scalar", lambda e: e.copy(out=wbc_b[:], in_=wbc_f[:]),
             waits=p.dep("ld"), inc=("act", 1))
        p.op("vector", lambda e: e.tensor_scalar_mul(out=nb[:], in0=nb[:], scalar1=-1.0),
             waits=p.dep("ld"), inc=("dve", 1))
        p.op("vector", lambda e: e.memset(onesP[:], 1.0), inc=("dve", 1))
        p.op("vector", lambda e: e.memset(ones1[:], 1.0), inc=("dve", 1))
        p.op("vector", lambda e: e.memset(cst[:, 0:1], 1e-5), inc=("dve", 1))
        p.op("vector", lambda e: e.memset(cst[:, 0:1], 1e-5), inc=("dve", 1))
        p.op("vector", lambda e: e.memset(cst[:, 1:2], 1e-38), inc=("dve", 1))

        for it in range(ntile):
            b, hf = divmod(it, TL // C1)
            cs = hf * C1

            start = p.snap()
            first = {e: True for e in ENGS}

            def op(eng, fn, waits=(), inc=None, self_wait=False):
                w = list(waits)
                if first[eng]:
                    w += [(s, v) for s, v in start.items()]
                    first[eng] = False
                p.op(eng, fn, w, inc, self_wait=self_wait)

            for c in range(NCH):
                op("sync", (lambda b=b, c=c, cs=cs: (lambda e: e.dma_start(
                    out=ubuf[:, c * C1:(c + 1) * C1],
                    in_=u_fm[b, c * 128:(c + 1) * 128, cs:cs + C1])))(),
                    inc=("ld", 16))
            for c in range(NCH):
                op("scalar", (lambda c=c: (lambda e: e.activation(
                    out=usq[:, c * C1:(c + 1) * C1],
                    in_=ubuf[:, c * C1:(c + 1) * C1], func=AF.Square)))(),
                    waits=p.dep("ld"), inc=("act", 1))
            for c in range(NCH):
                op("tensor", (lambda c=c: (lambda e: e.matmul(
                    ps_sx[:, :], onesP[:], ubuf[:, c * C1:(c + 1) * C1],
                    start=(c == 0), stop=(c == NCH - 1))))(),
                    waits=p.dep("ld"),
                    inc=("pe", 1) if c == NCH - 1 else None)
            for c in range(NCH):
                op("tensor", (lambda c=c: (lambda e: e.matmul(
                    ps_sx2[:, :], onesP[:], usq[:, c * C1:(c + 1) * C1],
                    start=(c == 0), stop=(c == NCH - 1))))(),
                    waits=p.dep("act"),
                    inc=("pe", 1) if c == NCH - 1 else None)
            op("scalar", lambda e: e.activation(out=m_row[:], in_=ps_sx[:],
                                                func=AF.Copy, scale=1.0 / D),
               waits=p.dep("pe"), inc=("act", 1))
            op("scalar", lambda e: e.activation(out=msq_row[:], in_=ps_sx[:],
                                                func=AF.Square, scale=1.0 / D),
               waits=p.dep("pe"), inc=("act", 1))
            op("vector", lambda e: e.scalar_tensor_tensor(
                out=var_row[:], in0=ps_sx2[:], scalar=1.0 / D, in1=msq_row[:],
                op0=ALU.mult, op1=ALU.subtract),
               waits=p.dep("pe", "act"), inc=("dve", 1))
            op("scalar", lambda e: e.activation(out=rstd_row[:], in_=var_row[:],
                                                func=AF.Sqrt, bias=cst[0:1, 0:1]),
               waits=p.dep("dve"), inc=("act", 1))
            op("vector", lambda e: e.reciprocal(out=rstd_row[:], in_=rstd_row[:]),
               waits=p.dep("act"), inc=("dve", 1))
            op("vector", lambda e: e.tensor_mul(out=mr_row[:], in0=m_row[:], in1=rstd_row[:]),
               waits=p.dep("act", "dve"), inc=("dve", 1), self_wait=True)
            op("tensor", lambda e: e.matmul(ps_bc1[:, :], ones1[:], rstd_row[:],
                                            start=True, stop=True),
               waits=p.dep("act", "dve"), inc=("pe", 1))
            op("tensor", lambda e: e.matmul(ps_bc2[:, :], ones1[:], mr_row[:],
                                            start=True, stop=True),
               waits=p.dep("dve"), inc=("pe", 1))
            for c in range(NCH):
                op("vector", (lambda c=c: (lambda e: e.tensor_mul(
                    out=xln[:, c * C1:(c + 1) * C1],
                    in0=ubuf[:, c * C1:(c + 1) * C1], in1=ps_bc1[:, :])))(),
                    waits=p.dep("pe"), inc=("dve", 1))
                op("vector", (lambda c=c: (lambda e: e.tensor_sub(
                    out=xln[:, c * C1:(c + 1) * C1],
                    in0=xln[:, c * C1:(c + 1) * C1], in1=ps_bc2[:, :])))(),
                    inc=("dve", 1))
                op("scalar", (lambda c=c: (lambda e: e.copy(
                    out=xln_b[:, c * C1:(c + 1) * C1],
                    in_=xln[:, c * C1:(c + 1) * C1])))(),
                    waits=p.dep("dve"), inc=("act", 1))
                op("gpsimd", (lambda b=b, c=c, cs=cs: (lambda e: e.dma_start(
                    out=st_xln[c, b, :, cs:cs + C1],
                    in_=xln[:, c * C1:(c + 1) * C1])))(),
                    waits=p.dep("dve"), inc=("st", 16))
            for ec in range(NCH):
                for kc in range(NCH):
                    op("tensor", (lambda ec=ec, kc=kc: (lambda e: e.matmul(
                        ps_dt[:, :],
                        wdt_b[:, kc * D + ec * 128: kc * D + (ec + 1) * 128],
                        xln_b[:, kc * C1:(kc + 1) * C1],
                        start=(kc == 0), stop=(kc == NCH - 1))))(),
                        waits=p.dep("act"),
                        inc=("pe", 1) if kc == NCH - 1 else None)
                op("scalar", (lambda ec=ec: (lambda e: e.activation(
                    out=stile[:], in_=ps_dt[:], func=AF.Sigmoid,
                    bias=nb[:, ec:ec + 1], scale=-1.0)))(),
                    waits=p.dep("pe"), inc=("act", 1))
                op("scalar", (lambda ec=ec: (lambda e: e.activation(
                    out=lnS[:, ec * C1:(ec + 1) * C1], in_=stile[:],
                    func=AF.Ln, bias=cst[:, 1:2])))(),
                    inc=("act", 1))
                op("gpsimd", (lambda b=b, ec=ec, cs=cs: (lambda e: e.dma_start(
                    out=st_lns[ec, b, :, cs:cs + C1],
                    in_=lnS[:, ec * C1:(ec + 1) * C1])))(),
                    waits=p.dep("act"), inc=("st", 16))
            for kc in range(NCH):
                op("tensor", (lambda kc=kc: (lambda e: e.matmul(
                    ps_B[:, :], wbc_b[:, kc * NS:(kc + 1) * NS],
                    xln_b[:, kc * C1:(kc + 1) * C1],
                    start=(kc == 0), stop=(kc == NCH - 1))))(),
                    waits=p.dep("act"),
                    inc=("pe", 1) if kc == NCH - 1 else None)
            for kc in range(NCH):
                op("tensor", (lambda kc=kc: (lambda e: e.matmul(
                    ps_C[:, :], wbc_b[:, (NCH + kc) * NS:(NCH + kc + 1) * NS],
                    xln_b[:, kc * C1:(kc + 1) * C1],
                    start=(kc == 0), stop=(kc == NCH - 1))))(),
                    inc=("pe", 1) if kc == NCH - 1 else None)
            op("scalar", lambda e: e.copy(out=brow[:], in_=ps_B[:]),
               waits=p.dep("pe"), inc=("act", 1))
            op("scalar", lambda e: e.copy(out=crow[:], in_=ps_C[:]),
               inc=("act", 1))
            op("gpsimd", (lambda b=b, cs=cs: (lambda e: e.dma_start(
                out=st_bc[b, 0:NS, cs:cs + C1], in_=brow[:])))(),
                waits=p.dep("act"), inc=("st", 16))
            op("gpsimd", (lambda b=b, cs=cs: (lambda e: e.dma_start(
                out=st_bc[b, NS:2 * NS, cs:cs + C1], in_=crow[:])))(),
                inc=("st", 16))

        if SKIP_CC:
            p.op("gpsimd", lambda e: e.sem_inc(sem["gpc"], 3), waits=p.dep("st"))
            p.counts["gpc"] += 3
            with nc.Block() as block:
                _replay_all(p, block, sem)
            return
        p.op("gpsimd", lambda e: e.collective_compute(
            "AllGather", ALU.bypass, replica_groups=[list(range(NCORES))],
            ins=[io["st_xln"].opt()], outs=[io["st_xln_G"].opt()]),
            waits=p.dep("st"), inc=("gpc", 1))
        p.op("gpsimd", lambda e: e.collective_compute(
            "AllGather", ALU.bypass, replica_groups=[list(range(NCORES))],
            ins=[io["st_lns"].opt()], outs=[io["st_lns_G"].opt()]),
            waits=p.dep("gpc"), inc=("gpc", 1))
        p.op("gpsimd", lambda e: e.collective_compute(
            "AllGather", ALU.bypass, replica_groups=[list(range(NCORES))],
            ins=[io["st_bc"].opt()], outs=[io["st_bc_g"].opt()]),
            waits=p.dep("gpc"), inc=("gpc", 1))

        with nc.Block() as block:
            _replay_all(p, block, sem)



def emit_l2(nc, p, sem, io):
    logA_d = io["logA_d"]; Dp_d = io["Dp_d"]
    rowsel = io["rowsel"]
    st_xln_g = io["st_xln_loc"]; st_lns_g = io["st_lns_loc"]; st_bc_g = io["st_bc_g"]
    st_y = io["st_y"]

    TC = 1024
    with contextlib.ExitStack() as stk:

        def sb(name, shape, dt=F32):
            return stk.enter_context(nc.sbuf_tensor("L2" + name, shape, dt))

        Ut = sb("Ut", [128, TC])
        Lt = sb("Lt", [128, TC])
        dut = sb("dut", [128, TC])
        bc_scr = sb("bc_scr", [NS, TG])
        Bhi = sb("Bhi", [NS, TG], BF16)
        Blo = sb("Blo", [NS, TG], BF16)
        Chi = sb("Chi", [NS, TG], BF16)
        Clo = sb("Clo", [NS, TG], BF16)
        logA_sb = sb("logA_sb", [128, NS])
        scaleP = sb("scaleP", [128, NS])
        Dp_sb = sb("Dp_sb", [128, 1])
        rs_sb = sb("rs_sb", [NS, NS * 128])
        rs_bf = sb("rs_bf", [NS, NS * 128], BF16)
        da = sb("da", [128, TC])
        dbu = sb("dbu", [128, TC])
        xbuf = sb("xbuf", [128, TC])
        zsheet = sb("zsheet", [128, TC * NS], BF16)
        state = sb("state", [128, B * NS])
        ysum = sb("ysum", [128, TC])
        yout = sb("yout", [128, TC])

        ps_Bbc = stk.enter_context(nc.psum_tensor("L2ps_Bbc", [128, TC], F32))
        ps_Cbc = stk.enter_context(nc.psum_tensor("L2ps_Cbc", [128, TC], F32))

        sect = p.snap()
        p.op("sync", lambda e: e.dma_start(out=logA_sb[:], in_=logA_d),
             waits=[(s, v) for s, v in sect.items()], inc=("ld", 16))
        p.op("sync", lambda e: e.dma_start(out=Dp_sb[:], in_=Dp_d), inc=("ld", 16))
        p.op("sync", lambda e: e.dma_start(out=rs_sb[:], in_=rowsel), inc=("ld", 16))
        p.op("scalar", lambda e: e.activation(out=scaleP[:], in_=logA_sb[:], func=AF.Exp),
             waits=p.dep("ld"), inc=("act", 1))
        p.op("scalar", lambda e: e.copy(out=rs_bf[:], in_=rs_sb[:]),
             waits=p.dep("ld"), inc=("act", 1))

        for b in range(B):
            start = p.snap()
            first = {e: True for e in ENGS}

            def op(eng, fn, waits=(), inc=None, self_wait=False):
                w = list(waits)
                if first[eng]:
                    w += [(s, v) for s, v in start.items()]
                    first[eng] = False
                p.op(eng, fn, w, inc, self_wait=self_wait)

            for sc_ in range(NCORES):
                op("sync", (lambda b=b, sc_=sc_: (lambda e: e.dma_start(
                    out=bc_scr[:, sc_ * TL:(sc_ + 1) * TL],
                    in_=st_bc_g[sc_, b, 0:NS])))(), inc=("ld", 16))
            op("scalar", lambda e: e.copy(out=Bhi[:], in_=bc_scr[:]),
               waits=p.dep("ld"), inc=("act", 1))
            op("vector", lambda e: e.tensor_sub(out=Blo[:], in0=bc_scr[:], in1=Bhi[:]),
               waits=p.dep("act"), inc=("dve", 1))
            for sc_ in range(NCORES):
                op("sync", (lambda b=b, sc_=sc_: (lambda e: e.dma_start(
                    out=bc_scr[:, sc_ * TL:(sc_ + 1) * TL],
                    in_=st_bc_g[sc_, b, NS:2 * NS])))(),
                    waits=p.dep("dve"), inc=("ld", 16))
            op("scalar", lambda e: e.copy(out=Chi[:], in_=bc_scr[:]),
               waits=p.dep("ld"), inc=("act", 1))
            op("vector", lambda e: e.tensor_sub(out=Clo[:], in0=bc_scr[:], in1=Chi[:]),
               waits=p.dep("act"), inc=("dve", 1))

            for ti in range(TG // TC):
                cs = ti * TC
                op("sync", (lambda b=b, ti=ti: (lambda e: e.dma_start(
                    out=Ut[:], in_=st_xln_g[ti, b])))(),
                    waits=p.dep("dve"), inc=("ld", 16))
                op("sync", (lambda b=b, ti=ti: (lambda e: e.dma_start(
                    out=Lt[:], in_=st_lns_g[ti, b])))(), inc=("ld", 16))
                op("vector", (lambda: (lambda e: e.scalar_tensor_tensor(
                    out=dut[:], in0=Lt[:], scalar=-1.0,
                    in1=Ut[:], op0=ALU.mult, op1=ALU.mult)))(),
                    waits=p.dep("ld"), inc=("dve", 1))
                for n in range(NS):
                    sidx = b * NS + n
                    for h2 in range(2):
                        for hl, srcm in ((0, Bhi), (1, Blo)):
                            op("tensor", (lambda n=n, cs=cs, h2=h2, hl=hl, srcm=srcm:
                                (lambda e: e.matmul(
                                    ps_Bbc[:, h2 * 512:(h2 + 1) * 512],
                                    rs_bf[:, n * 128:(n + 1) * 128],
                                    srcm[0:NS, cs + h2 * 512: cs + (h2 + 1) * 512],
                                    start=(hl == 0), stop=(hl == 1))))(),
                                waits=p.dep("dve") if (h2 == 0 and hl == 0) else (),
                                inc=("pe", 1) if (h2 == 1 and hl == 1) else None)
                        for hl, srcm in ((0, Chi), (1, Clo)):
                            op("tensor", (lambda n=n, cs=cs, h2=h2, hl=hl, srcm=srcm:
                                (lambda e: e.matmul(
                                    ps_Cbc[:, h2 * 512:(h2 + 1) * 512],
                                    rs_bf[:, n * 128:(n + 1) * 128],
                                    srcm[0:NS, cs + h2 * 512: cs + (h2 + 1) * 512],
                                    start=(hl == 0), stop=(hl == 1))))(),
                                inc=("pe", 1) if (h2 == 1 and hl == 1) else None)
                    op("scalar", (lambda n=n: (lambda e: e.activation(
                        out=da[:], in_=Lt[:], func=AF.Exp,
                        scale=scaleP[:, n:n + 1])))(),
                        waits=p.dep("dve"), inc=("act", 1))
                    op("vector", (lambda: (lambda e: e.tensor_mul(
                        out=dbu[:], in0=dut[:], in1=ps_Bbc[:, :])))(),
                        waits=p.dep("pe", "act"), inc=("dve", 1))
                    if ti == 0:
                        op("vector", (lambda: (lambda e: e.tensor_tensor_scan(
                            out=xbuf[:], data0=da[:], data1=dbu[:], initial=0.0,
                            op0=ALU.mult, op1=ALU.add)))(),
                            waits=p.dep("act"), inc=("dve", 1))
                    else:
                        op("vector", (lambda sidx=sidx: (lambda e: e.tensor_tensor_scan(
                            out=xbuf[:], data0=da[:], data1=dbu[:],
                            initial=state[:, sidx:sidx + 1],
                            op0=ALU.mult, op1=ALU.add)))(),
                            waits=p.dep("act"), inc=("dve", 1))
                    op("scalar", (lambda sidx=sidx: (lambda e: e.copy(
                        out=state[:, sidx:sidx + 1], in_=xbuf[:, TC - 1:TC])))(),
                        waits=p.dep("dve"), inc=("act", 1))
                    op("vector", (lambda n=n: (lambda e: e.tensor_tensor(
                        out=zsheet[:].rearrange("p (t n) -> p t n", n=NS)[:, :, n],
                        in0=xbuf[:], in1=ps_Cbc[:, :], op=ALU.mult)))(),
                        inc=("dve", 1))
                op("vector", (lambda: (lambda e: e.tensor_reduce(
                    out=ysum[:], in_=zsheet[:].rearrange("p (t n) -> p t n", n=NS),
                    axis=AX.X, op=ALU.add)))(),
                    inc=("dve", 1))
                op("vector", (lambda: (lambda e: e.scalar_tensor_tensor(
                    out=yout[:], in0=Ut[:], scalar=Dp_sb[:],
                    in1=ysum[:], op0=ALU.mult, op1=ALU.add)))(),
                    inc=("dve", 1))
                op("gpsimd", (lambda b=b, ti=ti: (lambda e: e.dma_start(
                    out=st_y[ti, b], in_=yout[:])))(),
                    waits=p.dep("dve"), inc=("st", 16))

        p.op("gpsimd", lambda e: e.collective_compute(
            "AllGather", ALU.bypass, replica_groups=[list(range(NCORES))],
            ins=[io["st_y"].opt()], outs=[io["st_y_G"].opt()]),
            waits=p.dep("st"), inc=("gpc", 1))
        with nc.allow_low_precision("z-sheet / y-contraction are bf16 matmul-class operands; reduce accumulates fp32 internally"):
            with nc.Block() as block:
                _replay_all(p, block, sem)


def emit_l3(nc, p, sem, io):
    u3 = io["u3"]; wq_t = io["wq_t"]; wk_t = io["wk_t"]; wv_t = io["wv_t"]
    wo_t = io["wo_t"]; bo_t = io["bo_t"]; krl = io["krl"]; krr = io["krr"]
    bias_ri = io["bias_ri"]; sel_c = io["sel_c"]; rk_in = io["rk_in"]
    exp_in = io["exp_in"]; rv_in = io["rv_in"]
    st_y_g = io["st_y_loc"]; out_h = io["out_h"]

    CT = 512
    NT = TL * B // CT

    with contextlib.ExitStack() as stk:

        def sb(name, shape, dt=F32):
            return stk.enter_context(nc.sbuf_tensor("L3" + name, shape, dt))

        w_bf = {}
        for nm in ("wq", "wk", "wv", "wo"):
            w_bf[nm] = sb(nm + "_bf", [128, NCH * D], BF16)
        wrt_bf = sb("wrt_bf", [128, 4 * 512], BF16)
        wit_bf = sb("wit_bf", [128, 4 * 512], BF16)
        witn_bf = sb("witn_bf", [128, 4 * 512], BF16)
        bo_sb = sb("bo_sb", [128, NCH])
        bri_sb = sb("bri_sb", [128, NCH])
        sel_bf = sb("sel_bf", [128, NCH * H], BF16)
        rk_bf = sb("rk_bf", [128, B * B * NCH * H], BF16)
        exp_bf = sb("exp_bf", [H, NCH * 128], BF16)
        attn_bf = sb("attn_bf", [H, B * CT], BF16)
        rv_sb = sb("rv_sb", [128, H])
        onesP = sb("onesP", [128, 1])
        ones1 = sb("ones1", [1, 128])
        cst = sb("cst", [128, 2])
        zrow = sb("zrow", [1, CT])

        xA = sb("xA", [128, NCH * CT])
        xB = sb("xB", [128, NCH * CT])
        ybuf4 = sb("ybuf4", [128, CT])
        xln_b = sb("xln_b", [128, NCH * CT], BF16)
        tmp = sb("tmp", [128, CT])
        qtile = sb("qtile", [128, CT], BF16)
        ktile = sb("ktile", [128, CT], BF16)
        vf = sb("vf", [128, NCH * CT])
        usq = vf
        zbuf = sb("zbuf", [128, CT], BF16)
        scbuf = sb("scbuf", [H, B * CT])
        rrow = sb("rrow", [1, CT])
        vrv = sb("vrv", [128, CT])
        vz = sb("vz", [128, CT])
        aout_b = sb("aout_b", [128, NCH * CT], BF16)
        m_row = sb("m_row", [1, CT])
        msq_row = sb("msq_row", [1, CT])
        var_row = sb("var_row", [1, CT])
        rstd_row = sb("rstd_row", [1, CT])
        mr_row = sb("mr_row", [1, CT])

        ps_bc1 = stk.enter_context(nc.psum_tensor("L3ps_bc1", [128, CT], F32))
        ps_bc2 = stk.enter_context(nc.psum_tensor("L3ps_bc2", [128, CT], F32))
        ps_mm = stk.enter_context(nc.psum_tensor("L3ps_mm", [128, CT], F32))
        ps_sc_all = [stk.enter_context(nc.psum_tensor(f"ps_sc{s}", [128, CT], F32))
                     for s in range(4)]
        ps_sx = ps_sc_all[0]
        ps_sx2 = ps_sc_all[1]

        sect = p.snap()
        sect_first = [True]

        def ld(dst, src):
            w = []
            if sect_first[0]:
                w = [(s, v) for s, v in sect.items()]
                sect_first[0] = False
            p.op("sync", (lambda dst=dst, src=src: (lambda e: e.dma_start(
                out=dst, in_=src)))(), waits=w, inc=("ld", 16))

        ld(bo_sb[:], bo_t)
        ld(bri_sb[:], bias_ri)
        ld(xA[:, 0:NCH * H], sel_c)
        ld(xA[:, 128:128 + B * B * NCH * H], rk_in)
        ld(xA[0:H, 2400:2400 + NCH * 128], exp_in)
        ld(rv_sb[:], rv_in)
        p.op("scalar", lambda e: e.copy(out=sel_bf[:], in_=xA[:, 0:NCH * H]),
             waits=p.dep("ld"), inc=("act", 1))
        p.op("scalar", lambda e: e.copy(out=rk_bf[:], in_=xA[:, 128:128 + B * B * NCH * H]),
             waits=p.dep("ld"), inc=("act", 1))
        p.op("scalar", lambda e: e.copy(out=exp_bf[:], in_=xA[0:H, 2400:2400 + NCH * 128]),
             waits=p.dep("ld"), inc=("act", 1))
        p.op("vector", lambda e: e.memset(onesP[:], 1.0), inc=("dve", 1))
        p.op("vector", lambda e: e.memset(ones1[:], 1.0), inc=("dve", 1))
        p.op("vector", lambda e: e.memset(cst[:, 0:1], 1e-5), inc=("dve", 1))
        p.op("vector", lambda e: e.memset(zrow[:], 0.0), inc=("dve", 1))

        for ri in range(2):
            dstf = wrt_bf if ri == 0 else wit_bf
            for c in range(4):
                for pc in range(4):
                    p.op("sync", (lambda ri=ri, c=c, pc=pc: (lambda e: e.dma_start(
                        out=xA[:, 0:512],
                        in_=krr[ri, c, pc * 128:(pc + 1) * 128, :])))(),
                        waits=p.dep("pe", "act"), inc=("ld", 16))
                    p.op("sync", (lambda ri=ri, c=c, pc=pc: (lambda e: e.dma_start(
                        out=xA[:, 512:1024],
                        in_=krl[ri, c, pc * 128:(pc + 1) * 128, :])))(),
                        inc=("ld", 16))
                    for qc in range(4):
                        p.op("tensor", (lambda qc=qc, c=c, pc=pc: (lambda e: e.matmul(
                            ps_sc_all[qc][:, :],
                            xA[:, 512 + qc * 128: 512 + (qc + 1) * 128],
                            xA[:, 0:512],
                            start=(c == 0 and pc == 0),
                            stop=(c == 3 and pc == 3))))(),
                            waits=p.dep("ld", "act"),
                            inc=("pe", 1) if qc == 3 else None)
            for qc in range(4):
                p.op("scalar", (lambda qc=qc, dstf=dstf: (lambda e: e.copy(
                    out=dstf[:, qc * 512:(qc + 1) * 512],
                    in_=ps_sc_all[qc][:, :])))(),
                    waits=p.dep("pe"), inc=("act", 1))
                if ri == 1:
                    p.op("scalar", (lambda qc=qc: (lambda e: e.activation(
                        out=witn_bf[:, qc * 512:(qc + 1) * 512],
                        in_=ps_sc_all[qc][:, :], func=AF.Copy, scale=-1.0)))(),
                        inc=("act", 1))
        for nm, srcw in (("wq", wq_t), ("wk", wk_t), ("wv", wv_t), ("wo", wo_t)):
            for kc in range(NCH):
                p.op("sync", (lambda kc=kc, srcw=srcw: (lambda e: e.dma_start(
                    out=xA[:, (kc % 4) * 1024:(kc % 4 + 1) * 1024],
                    in_=srcw[kc * 128:(kc + 1) * 128, :])))(),
                    waits=p.dep("act"), inc=("ld", 16))
                p.op("scalar", (lambda nm=nm, kc=kc: (lambda e: e.copy(
                    out=w_bf[nm][:, kc * D:(kc + 1) * D],
                    in_=xA[:, (kc % 4) * 1024:(kc % 4 + 1) * 1024])))(),
                    waits=p.dep("ld"), inc=("act", 1))

        def ln_block(op, xsrc, dep_sem):
            for c in range(NCH):
                op("scalar", (lambda c=c: (lambda e: e.activation(
                    out=usq[:, c * CT:(c + 1) * CT],
                    in_=xsrc[:, c * CT:(c + 1) * CT], func=AF.Square)))(),
                    waits=p.dep(dep_sem), inc=("act", 1))
            for c in range(NCH):
                op("tensor", (lambda c=c: (lambda e: e.matmul(
                    ps_sx[0:1, :], onesP[:], xsrc[:, c * CT:(c + 1) * CT],
                    start=(c == 0), stop=(c == NCH - 1))))(),
                    waits=p.dep(dep_sem),
                    inc=("pe", 1) if c == NCH - 1 else None)
            for c in range(NCH):
                op("tensor", (lambda c=c: (lambda e: e.matmul(
                    ps_sx2[0:1, :], onesP[:], usq[:, c * CT:(c + 1) * CT],
                    start=(c == 0), stop=(c == NCH - 1))))(),
                    waits=p.dep("act"),
                    inc=("pe", 1) if c == NCH - 1 else None)
            op("scalar", lambda e: e.activation(out=m_row[:], in_=ps_sx[0:1, :],
                                                func=AF.Copy, scale=1.0 / D),
               waits=p.dep("pe"), inc=("act", 1))
            op("scalar", lambda e: e.activation(out=msq_row[:], in_=ps_sx[0:1, :],
                                                func=AF.Square, scale=1.0 / D),
               waits=p.dep("pe"), inc=("act", 1))
            op("vector", lambda e: e.scalar_tensor_tensor(
                out=var_row[:], in0=ps_sx2[0:1, :], scalar=1.0 / D, in1=msq_row[:],
                op0=ALU.mult, op1=ALU.subtract),
               waits=p.dep("pe", "act"), inc=("dve", 1))
            op("scalar", lambda e: e.activation(out=rstd_row[:], in_=var_row[:],
                                                func=AF.Sqrt, bias=cst[0:1, 0:1]),
               waits=p.dep("dve"), inc=("act", 1))
            op("vector", lambda e: e.reciprocal(out=rstd_row[:], in_=rstd_row[:]),
               waits=p.dep("act"), inc=("dve", 1))
            op("vector", lambda e: e.tensor_mul(out=mr_row[:], in0=m_row[:],
                                                in1=rstd_row[:]),
               waits=p.dep("act", "dve"), inc=("dve", 1), self_wait=True)
            op("tensor", lambda e: e.matmul(ps_bc1[:, :], ones1[:], rstd_row[:],
                                            start=True, stop=True),
               waits=p.dep("act", "dve"), inc=("pe", 1))
            op("tensor", lambda e: e.matmul(ps_bc2[:, :], ones1[:], mr_row[:],
                                            start=True, stop=True),
               waits=p.dep("dve"), inc=("pe", 1))
            for c in range(NCH):
                op("vector", (lambda c=c: (lambda e: e.tensor_mul(
                    out=tmp[:], in0=xsrc[:, c * CT:(c + 1) * CT], in1=ps_bc1[:, :])))(),
                    waits=p.dep("pe"), inc=("dve", 1))
                op("vector", (lambda c=c: (lambda e: e.tensor_tensor(
                    out=xln_b[:, c * CT:(c + 1) * CT], in0=tmp[:], in1=ps_bc2[:, :],
                    op=ALU.subtract)))(),
                    inc=("dve", 1))

        def ffn(op, xres, xdst):
            for oc in range(NCH):
                half = oc // 4
                o128 = (oc % 4) * 128
                wa, wb2 = (wrt_bf, witn_bf) if half == 0 else (wit_bf, wrt_bf)
                for qc in range(4):
                    op("tensor", (lambda wa=wa, qc=qc, o128=o128: (lambda e: e.matmul(
                        ps_mm[:, :], wa[:, qc * 512 + o128: qc * 512 + o128 + 128],
                        xln_b[:, qc * CT:(qc + 1) * CT],
                        start=(qc == 0), stop=False)))(),
                        waits=p.dep("dve", "act") if qc == 0 else ())
                for qc in range(4):
                    op("tensor", (lambda wb2=wb2, qc=qc, o128=o128: (lambda e: e.matmul(
                        ps_mm[:, :], wb2[:, qc * 512 + o128: qc * 512 + o128 + 128],
                        xln_b[:, (4 + qc) * CT:(4 + qc + 1) * CT],
                        start=False, stop=(qc == 3))))(),
                        inc=("pe", 1) if qc == 3 else None)
                op("vector", (lambda oc=oc: (lambda e: e.scalar_tensor_tensor(
                    out=xdst[:, oc * CT:(oc + 1) * CT], in0=ps_mm[:, :],
                    scalar=bri_sb[:, oc:oc + 1],
                    in1=xres[:, oc * CT:(oc + 1) * CT],
                    op0=ALU.add, op1=ALU.add)))(),
                    waits=p.dep("pe"), inc=("dve", 1))

        def proj_chunk(op, wname, dst, oc, rhs_bf, evac_eng="scalar"):
            for kc in range(NCH):
                op("tensor", (lambda wname=wname, oc=oc, kc=kc: (lambda e: e.matmul(
                    ps_mm[:, :],
                    w_bf[wname][:, kc * D + oc * 128: kc * D + (oc + 1) * 128],
                    rhs_bf[:, kc * CT:(kc + 1) * CT],
                    start=(kc == 0), stop=(kc == NCH - 1))))(),
                    waits=p.dep("dve", "act") if kc == 0 else (),
                    inc=("pe", 1) if kc == NCH - 1 else None)
            op(evac_eng, (lambda dst=dst: (lambda e: e.copy(
                out=dst, in_=ps_mm[:, :])))(),
                waits=p.dep("pe"), inc=("act", 1))

        for it in range(NT):
            cs = it * CT
            start = p.snap()
            first = {e: True for e in ENGS}

            def op(eng, fn, waits=(), inc=None, self_wait=False):
                w = list(waits)
                if first[eng]:
                    w += [(s, v) for s, v in start.items()]
                    first[eng] = False
                p.op(eng, fn, w, inc, self_wait=self_wait)

            u3v = u3.rearrange("d t b -> d (t b)")
            outv = out_h.rearrange("d t b -> d (t b)")

            tau0 = it * 128
            for c in range(NCH):
                op("sync", (lambda c=c, cs=cs: (lambda e: e.dma_start(
                    out=xA[:, c * CT:(c + 1) * CT],
                    in_=u3v[c * 128:(c + 1) * 128, cs:cs + CT])))(),
                    waits=p.dep("dve"), inc=("ld", 16))
                for bb in range(B):
                    op("sync", (lambda c=c, bb=bb, tau0=tau0: (lambda e: e.dma_start(
                        out=ybuf4[:, bb * 128:(bb + 1) * 128],
                        in_=st_y_g[c, bb, :, tau0:tau0 + 128])))(),
                        waits=p.dep("dve") if bb == 0 else (), inc=("ld", 16))
                op("vector", (lambda c=c: (lambda e: e.tensor_add(
                    out=xA[:, c * CT:(c + 1) * CT].rearrange(
                        "p (t b) -> p t b", b=B),
                    in0=xA[:, c * CT:(c + 1) * CT].rearrange(
                        "p (t b) -> p t b", b=B),
                    in1=ybuf4[:].rearrange("p (b t) -> p t b", b=B))))(),
                    waits=p.dep("ld"), inc=("dve", 1))

            ln_block(op, xA, "dve")
            ffn(op, xA, xB)

            ln_block(op, xB, "dve")

            for oc in range(NCH):
                proj_chunk(op, "wv", vf[:, oc * CT:(oc + 1) * CT], oc, xln_b)

            for c in range(NCH):
                proj_chunk(op, "wq", qtile[:, :], c, xln_b)
                proj_chunk(op, "wk", ktile[:, :], c, xln_b)
                for s in range(B):
                    qs = qtile[:].rearrange("p (t b) -> p t b", b=B)[:, :, s:s + 1]
                    op("vector", (lambda qs=qs: (lambda e: e.tensor_tensor(
                        out=zbuf[:].rearrange("p (t b) -> p t b", b=B),
                        in0=qs.broadcast_to([128, CT // B, B]),
                        in1=ktile[:].rearrange("p (t b) -> p t b", b=B),
                        op=ALU.mult)))(),
                        waits=p.dep("act", "pe"), inc=("dve", 1))
                    if c == 0:
                        op("tensor", (lambda s=s: (lambda e: e.matmul(
                            ps_sc_all[s][:, :], zrow[:, 0:128], zrow[:],
                            start=True, stop=False,
                            skip_group_check=True)))(),
                            waits=p.dep("dve"), inc=("pe", 1))
                    op("tensor", (lambda s=s, c=c: (lambda e: e.matmul(
                        ps_sc_all[s][0:16, :], sel_bf[:, c * H:(c + 1) * H],
                        zbuf[:],
                        start=False, stop=False,
                        skip_group_check=True)))(),
                        waits=p.dep("dve"), inc=("pe", 1))
                    for t in range(B):
                        rk_tile = rk_bf[:, ((s * B + t) * NCH + c) * H:
                                        ((s * B + t) * NCH + c + 1) * H]
                        qsv = qtile[:].rearrange("p (t b) -> p b t", b=B)[:, s, :]
                        op("tensor", (lambda t=t, s=s, c=c, rk_tile=rk_tile, qsv=qsv:
                            (lambda e: e.matmul(
                                ps_sc_all[s][32:48, t * 128:(t + 1) * 128],
                                rk_tile, qsv,
                                start=False,
                                stop=(c == NCH - 1 and t == B - 1),
                                skip_group_check=True)))(),
                            inc=("pe", 1) if c == NCH - 1 else None)
            for s in range(B):
                op("scalar", (lambda s=s: (lambda e: e.copy(
                    out=scbuf[:, s * CT:(s + 1) * CT],
                    in_=ps_sc_all[s][0:16, :])))(),
                    waits=p.dep("pe"), inc=("act", 1))
                op("vector", (lambda s=s: (lambda e: e.tensor_add(
                    out=scbuf[:, s * CT:(s + 1) * CT].rearrange(
                        "h (t b) -> h t b", b=B),
                    in0=scbuf[:, s * CT:(s + 1) * CT].rearrange(
                        "h (t b) -> h t b", b=B),
                    in1=ps_sc_all[s][32:48, :].rearrange(
                        "h (t tau) -> h tau t", tau=128))))(),
                    waits=p.dep("pe", "act"), inc=("dve", 1))
                op("scalar", (lambda s=s: (lambda e: e.activation(
                    out=scbuf[:, s * CT:(s + 1) * CT],
                    in_=scbuf[:, s * CT:(s + 1) * CT],
                    func=AF.Exp, scale=0.125)))(),
                    waits=p.dep("dve"), inc=("act", 1))
                op("tensor", (lambda s=s: (lambda e: e.matmul(
                    ps_bc2[0:1, :], onesP[0:16, :],
                    scbuf[:, s * CT:(s + 1) * CT],
                    start=True, stop=True)))(),
                    waits=p.dep("act"), inc=("pe", 1))
                op("vector", lambda e: e.reciprocal(out=rrow[:], in_=ps_bc2[0:1, :]),
                   waits=p.dep("pe"), inc=("dve", 1))
                op("tensor", lambda e: e.matmul(
                    ps_bc2[32:48, :], ones1[:, 0:16], rrow[:],
                    start=True, stop=True),
                   waits=p.dep("dve"), inc=("pe", 1))
                op("vector", (lambda s=s: (lambda e: e.tensor_mul(
                    out=scbuf[:, s * CT:(s + 1) * CT],
                    in0=scbuf[:, s * CT:(s + 1) * CT],
                    in1=ps_bc2[32:48, :])))(),
                    waits=p.dep("pe"), inc=("dve", 1))
                op("scalar", (lambda s=s: (lambda e: e.copy(
                    out=attn_bf[:, s * CT:(s + 1) * CT],
                    in_=scbuf[:, s * CT:(s + 1) * CT])))(),
                    waits=p.dep("dve"), inc=("act", 1))
            for s in range(B):
                for c in range(NCH):
                    op("tensor", (lambda s=s, c=c: (lambda e: e.matmul(
                        ps_bc1[:, :], exp_bf[:, c * 128:(c + 1) * 128],
                        attn_bf[:, s * CT:(s + 1) * CT],
                        start=True, stop=True)))(),
                        waits=p.dep("dve", "act"), inc=("pe", 1))
                    op("vector", (lambda s=s, c=c: (lambda e: e.tensor_tensor(
                        out=vrv[:].rearrange("p (t b) -> p t b", b=B),
                        in0=vf[:, c * CT:(c + 1) * CT].rearrange(
                            "p (t b) -> p t b", b=B),
                        in1=rv_sb[:, None, s * B:(s + 1) * B].broadcast_to(
                            [128, CT // B, B]),
                        op=ALU.add)))(),
                        inc=("dve", 1))
                    op("vector", (lambda: (lambda e: e.tensor_mul(
                        out=vz[:], in0=ps_bc1[:, :], in1=vrv[:])))(),
                        waits=p.dep("pe"), inc=("dve", 1))
                    op("vector", (lambda s=s, c=c: (lambda e: e.tensor_reduce(
                        out=aout_b[:, c * CT:(c + 1) * CT].rearrange(
                            "p (t b) -> p b t", b=B)[:, s, :],
                        in_=vz[:].rearrange("p (t b) -> p t b", b=B),
                        axis=AX.X, op=ALU.add)))(),
                        inc=("dve", 1))
            for oc in range(NCH):
                for kc in range(NCH):
                    op("tensor", (lambda oc=oc, kc=kc: (lambda e: e.matmul(
                        ps_mm[:, :],
                        w_bf["wo"][:, kc * D + oc * 128: kc * D + (oc + 1) * 128],
                        aout_b[:, kc * CT:(kc + 1) * CT],
                        start=(kc == 0), stop=(kc == NCH - 1))))(),
                        waits=p.dep("dve", "act") if kc == 0 else (),
                        inc=("pe", 1) if kc == NCH - 1 else None)
                op("vector", (lambda oc=oc: (lambda e: e.scalar_tensor_tensor(
                    out=xA[:, oc * CT:(oc + 1) * CT], in0=ps_mm[:, :],
                    scalar=bo_sb[:, oc:oc + 1],
                    in1=xB[:, oc * CT:(oc + 1) * CT],
                    op0=ALU.add, op1=ALU.add)))(),
                    waits=p.dep("pe"), inc=("dve", 1))
            ln_block(op, xA, "dve")
            ffn(op, xA, xB)
            for c in range(NCH):
                op("gpsimd", (lambda c=c, cs=cs: (lambda e: e.dma_start(
                    out=outv[c * 128:(c + 1) * 128, cs:cs + CT],
                    in_=xB[:, c * CT:(c + 1) * CT])))(),
                    waits=p.dep("dve"), inc=("st", 16))

        p.op("sync", lambda e: e.nop(), waits=p.dep("st"))
        with nc.allow_low_precision("DVE reduces in fp32 internally; bf16 is the matmul operand dtype"):
            with nc.Block() as block:
                _replay_all(p, block, sem)




def build_full(upto=3):
    nc = bass.Bass("TRN2", target_bir_lowering=False,
                   detect_race_conditions=False)
    io = {}
    ext_in = [
        ("u_fm", [B, D, TL]), ("wdt_t", [D, D]), ("wb_t", [D, NS]),
        ("wc_t", [D, NS]), ("bdt_t", [128, NCH]),
        ("logA_d", [128, NS]), ("Dp_d", [128, 1]), ("id128", [128, 128]),
        ("rowsel", [NS, NS * 128]),
        ("u3", [D, TL, B]), ("wq_t", [D, D]), ("wk_t", [D, D]),
        ("wv_t", [D, D]), ("wo_t", [D, D]), ("bo_t", [128, NCH]),
        ("krl", [2, 4, 512, 512]), ("krr", [2, 4, 512, 512]),
        ("bias_ri", [128, NCH]), ("sel_c", [128, NCH * H]),
        ("rk_in", [128, B * B * NCH * H]), ("exp_in", [H, NCH * 128]),
        ("rv_in", [128, H]),
    ]
    for nm, shp in ext_in:
        io[nm] = nc.dram_tensor(nm, shp, F32, kind="ExternalInput").ap()
    io["out_h"] = nc.dram_tensor("out_h", [D, TL, B], F32,
                                 kind="ExternalOutput").ap()
    for nm in ("st_xln", "st_lns", "st_y",
               "st_xln_loc", "st_lns_loc", "st_y_loc"):
        io[nm] = nc.dram_tensor(nm, [NCH, B, 128, TL], F32).ap()
    for nm in ("st_xln_G", "st_lns_G", "st_y_G"):
        io[nm] = nc.dram_tensor(nm, [NCORES, NCH, B, 128, TL], F32).ap()
    io["st_bc"] = nc.dram_tensor("st_bc", [B, 2 * NS, TL], F32).ap()
    io["st_bc_g"] = nc.dram_tensor("st_bc_g", [NCORES, B, 2 * NS, TL], F32).ap()

    with contextlib.ExitStack() as stk:
        sem = _mk_sems(nc, stk)
        p = Prog(SEMS)
        emit_l1(nc, p, sem, io)
        gpc_t = p.counts["gpc"]
        ld_t = p.counts["ld"]
        for arm in nc.switch_core_id(engines=[nc.sync], n=128):
            me = arm.logical % NCORES
            nc.sync.wait_ge(sem["gpc"], gpc_t)
            nc.sync.dma_start(out=io["st_xln_loc"],
                              in_=io["st_xln_G"][:, me]).then_inc(sem["ld"], 16)
            nc.sync.dma_start(out=io["st_lns_loc"],
                              in_=io["st_lns_G"][:, me]).then_inc(sem["ld"], 16)
        p.counts["ld"] = ld_t + 32
        _section_barrier(p)
        if upto >= 2:
            emit_l2(nc, p, sem, io)
            gpc_t = p.counts["gpc"]
            ld_t = p.counts["ld"]
            for arm in nc.switch_core_id(engines=[nc.sync], n=128):
                me = arm.logical % NCORES
                nc.sync.wait_ge(sem["gpc"], gpc_t)
                nc.sync.dma_start(out=io["st_y_loc"],
                                  in_=io["st_y_G"][:, me]).then_inc(sem["ld"], 16)
            p.counts["ld"] = ld_t + 16
            _section_barrier(p)
        if upto >= 3:
            emit_l3(nc, p, sem, io)
        if upto < 3:
            with nc.Block() as block:
                _replay_all(p, block, sem)
    return nc


def _section_barrier(p):
    snap = p.snap()
    for eng in ENGS:
        p.op(eng, lambda e: e.nop(), waits=[(s, v) for s, v in snap.items()])


def _host_consts():
    sel = np.zeros((NCH, 128, H), np.float32)
    for c in range(NCH):
        for j in range(2):
            sel[c, j * 64:(j + 1) * 64, 2 * c + j] = 1.0
    expand = np.zeros((NCH, H, 128), np.float32)
    for c in range(NCH):
        for j in range(2):
            expand[c, 2 * c + j, j * 64:(j + 1) * 64] = 1.0
    id128 = np.eye(128, dtype=np.float32)
    return sel, expand, id128


def kernel(**inp):
    u = np.asarray(inp["L0_attn_a_u"], np.float32)
    W_B = np.asarray(inp["L0_attn_a_W_B"], np.float32)
    W_C = np.asarray(inp["L0_attn_a_W_C"], np.float32)
    W_dt = np.asarray(inp["L0_attn_a_W_dt"], np.float32)
    b_dt = np.asarray(inp["L0_attn_a_b_dt"], np.float32)
    log_A = np.asarray(inp["L0_attn_a_log_A"], np.float32)
    D_param = np.asarray(inp["L0_attn_a_D_param"], np.float32)
    w_q = np.asarray(inp["L0_attn_b_w_q"], np.float32)
    w_k = np.asarray(inp["L0_attn_b_w_k"], np.float32)
    w_v = np.asarray(inp["L0_attn_b_w_v"], np.float32)
    w_o = np.asarray(inp["L0_attn_b_w_o"], np.float32)
    b_o = np.asarray(inp["L0_attn_b_b_o"], np.float32)
    rel_k = np.asarray(inp["L0_attn_b_rel_k"], np.float32)
    rel_v = np.asarray(inp["L0_attn_b_rel_v"], np.float32)
    G1_R = np.asarray(inp["L0_ffn_G1_R"], np.float32)
    G2_R = np.asarray(inp["L0_ffn_G2_R"], np.float32)
    G1_I = np.asarray(inp["L0_ffn_G1_I"], np.float32)
    G2_I = np.asarray(inp["L0_ffn_G2_I"], np.float32)
    bias_real = np.asarray(inp["L0_ffn_bias_real"], np.float32)
    bias_imag = np.asarray(inp["L0_ffn_bias_imag"], np.float32)

    sel, expand, id128 = _host_consts()
    sel_in = np.ascontiguousarray(sel.transpose(1, 0, 2).reshape(128, NCH * H))
    exp_in = np.ascontiguousarray(expand.transpose(1, 0, 2).reshape(H, NCH * 128))
    rowsel = np.zeros((NS, NS, 128), np.float32)
    for n in range(NS):
        rowsel[n, n, :] = 1.0
    rowsel = np.ascontiguousarray(rowsel.transpose(1, 0, 2).reshape(NS, NS * 128))

    rel_idx = np.clip(np.arange(B)[:, None] - np.arange(B)[None, :],
                      -MAX_REL, MAX_REL) + MAX_REL
    rk_sel = rel_k[rel_idx]
    rv_sel = rel_v[rel_idx]
    rk_lhsT = np.zeros((B, B, NCH, 128, H), np.float32)
    for s in range(B):
        for t in range(B):
            for c in range(NCH):
                for j in range(2):
                    rk_lhsT[s, t, c, j * 64:(j + 1) * 64, 2 * c + j] = rk_sel[s, t]
    rk_in = np.ascontiguousarray(
        rk_lhsT.transpose(3, 0, 1, 2, 4).reshape(128, B * B * NCH * H))
    rv_in = np.zeros((128, H), np.float32)
    for s in range(B):
        for t in range(B):
            rv_in[:, s * B + t] = np.tile(rv_sel[s, t], 2)
    krl = np.zeros((2, 4, 512, 512), np.float32)
    krr = np.zeros((2, 4, 512, 512), np.float32)
    for ri, (G1x, G2x) in enumerate(((G1_R, G2_R), (G1_I, G2_I))):
        for c in range(4):
            for g in range(16):
                krl[ri, c, g::16, g::16] = G1x[0][:, c, :]
            for b2 in range(32):
                krr[ri, c, b2 * 16:(b2 + 1) * 16, b2 * 16:(b2 + 1) * 16] = \
                    G2x[c, :, 0, :].T
    bias_ri = np.ascontiguousarray(
        np.concatenate([bias_real, bias_imag]).reshape(NCH, 128).T)
    bo_t = np.ascontiguousarray(b_o.reshape(NCH, 128).T)
    bdt_t = np.ascontiguousarray(b_dt.reshape(NCH, 128).T)

    shared = {
        "wdt_t": np.ascontiguousarray(W_dt.T),
        "wb_t": np.ascontiguousarray(W_B.T),
        "wc_t": np.ascontiguousarray(W_C.T),
        "bdt_t": bdt_t, "id128": id128, "rowsel": rowsel,
        "wq_t": np.ascontiguousarray(w_q.T),
        "wk_t": np.ascontiguousarray(w_k.T),
        "wv_t": np.ascontiguousarray(w_v.T),
        "wo_t": np.ascontiguousarray(w_o.T),
        "bo_t": bo_t, "krl": krl, "krr": krr, "bias_ri": bias_ri,
        "sel_c": sel_in, "rk_in": rk_in, "exp_in": exp_in, "rv_in": rv_in,
    }

    nc = build_full()
    in_maps = []
    for k in range(NCORES):
        sl = slice(k * TL, (k + 1) * TL)
        m = dict(shared)
        m["u_fm"] = np.ascontiguousarray(u[:, sl, :].transpose(0, 2, 1))
        m["u3"] = np.ascontiguousarray(u[:, sl, :].transpose(2, 1, 0))
        m["logA_d"] = np.ascontiguousarray(log_A[k * 128:(k + 1) * 128])
        m["Dp_d"] = np.ascontiguousarray(D_param[k * 128:(k + 1) * 128, None])
        in_maps.append(m)
    res = run_bass_kernel_spmd(nc, in_maps, core_ids=list(range(NCORES)))

    out = np.empty((B, TG, D), np.float32)
    for k in range(NCORES):
        oh = res.results[k]["out_h"]
        out[:, k * TL:(k + 1) * TL, :] = oh.transpose(2, 1, 0)
    return out


# version 13
# speedup vs baseline: 1.4566x; 1.3896x over previous
import contextlib
import numpy as np
import concourse.bass as bass
import concourse.mybir as mybir
from concourse.bass_utils import run_bass_kernel_spmd
from concourse.bass_isa import ReduceOp

F32 = mybir.dt.float32
BF16 = mybir.dt.bfloat16
AF = mybir.ActivationFunctionType
ALU = mybir.AluOpType
AX = mybir.AxisListType

NCORES = 8
B = 4
TG = 8192
TL = TG // NCORES
D = 1024
NCH = D // 128
NS = 16
H = 16
DK = 64
MAX_REL = 128

SKIP_CC = False
ENGS = ("sync", "scalar", "vector", "tensor", "gpsimd")


class Prog:

    def __init__(self, sem_names):
        self.counts = {s: 0 for s in sem_names}
        self.owner = {}
        self.ops = {e: [] for e in ENGS}
        self.cursor = {e: 0 for e in ENGS}
        self.last_wait = {e: {} for e in ENGS}

    def snap(self):
        return dict(self.counts)

    def op(self, eng, fn, waits=(), inc=None, self_wait=False):
        w = [(s, v) for (s, v) in waits if v > 0]
        self.ops[eng].append((w, inc, fn, self_wait))
        if inc is not None:
            s, k = inc
            if s in self.owner:
                assert self.owner[s] == eng, (s, eng, self.owner[s])
            self.owner[s] = eng
            self.counts[s] += k

    def dep(self, *sems):
        return [(s, self.counts[s]) for s in sems]

    def replay(self, engine_obj, eng_name, handles):
        last = self.last_wait[eng_name]
        ops = self.ops[eng_name]
        start = self.cursor[eng_name]
        self.cursor[eng_name] = len(ops)
        for waits, inc, fn, self_wait in ops[start:]:
            for s, v in waits:
                if self.owner.get(s) == eng_name and not self_wait:
                    continue
                if last.get(s, 0) >= v:
                    continue
                engine_obj.wait_ge(handles[s], v)
                last[s] = v
            instr = fn(engine_obj)
            if inc is not None:
                s, k = inc
                assert instr is not None
                instr.then_inc(handles[s], k)


def _replay_all(prog, block, handles):
    @block.sync
    def _(e):
        prog.replay(e, "sync", handles)

    @block.scalar
    def _(e):
        prog.replay(e, "scalar", handles)

    @block.vector
    def _(e):
        prog.replay(e, "vector", handles)

    @block.tensor
    def _(e):
        prog.replay(e, "tensor", handles)

    @block.gpsimd
    def _(e):
        prog.replay(e, "gpsimd", handles)


SEMS = ("ld", "pe", "act", "dve", "gpc", "st", "swx",
        "ccx0", "ccx1", "ccx2", "ccx3", "ccy0", "ccy1", "ccy2", "ccy3")


def _mk_sems(nc, st_):
    return {s: st_.enter_context(nc.semaphore(s)) for s in SEMS}



def emit_l1(nc, p, sem, io):
    u_fm = io["u_fm"]; wdt_t = io["wdt_t"]; wb_t = io["wb_t"]
    wc_t = io["wc_t"]; bdt_t = io["bdt_t"]
    st_xln = io["st_xln"]; st_lns = io["st_lns"]; st_bc = io["st_bc"]

    C1 = 512
    ntile = B * (TL // C1)

    with contextlib.ExitStack() as stk:

        def sb(name, shape, dt=F32):
            return stk.enter_context(nc.sbuf_tensor("L1" + name, shape, dt))

        wdt_b = sb("wdt_b", [128, NCH * D], BF16)
        wbc_f = sb("wbc_f", [128, 2 * NCH * NS])
        wbc_b = sb("wbc_b", [128, 2 * NCH * NS], BF16)
        nb = sb("nb", [128, NCH])
        onesP = sb("onesP", [128, 1])
        ones1 = sb("ones1", [1, 128])
        ubuf_d = [sb(f"ubuf{i}", [128, NCH * C1]) for i in range(2)]
        usq_d = [sb(f"usq{i}", [128, NCH * C1]) for i in range(2)]
        xln_d = [sb(f"xln{i}", [128, NCH * C1]) for i in range(2)]
        xln_b_d = [sb(f"xln_b{i}", [128, NCH * C1], BF16) for i in range(2)]
        lnS_d = [sb(f"lnS{i}", [128, NCH * C1]) for i in range(2)]
        stile = sb("stile", [128, C1])
        m_row = sb("m_row", [1, C1])
        msq_row = sb("msq_row", [1, C1])
        var_row = sb("var_row", [1, C1])
        rstd_row = sb("rstd_row", [1, C1])
        mr_row = sb("mr_row", [1, C1])
        brow_d = [sb(f"brow{i}", [NS, C1]) for i in range(2)]
        crow_d = [sb(f"crow{i}", [NS, C1]) for i in range(2)]
        cst = sb("cst", [128, 2])

        ps_sx = stk.enter_context(nc.psum_tensor("L1ps_sx", [1, C1], F32))
        ps_sx2 = stk.enter_context(nc.psum_tensor("L1ps_sx2", [1, C1], F32))
        ps_bc1 = stk.enter_context(nc.psum_tensor("L1ps_bc1", [128, C1], F32))
        ps_bc2 = stk.enter_context(nc.psum_tensor("L1ps_bc2", [128, C1], F32))
        ps_dt = stk.enter_context(nc.psum_tensor("L1ps_dt", [128, C1], F32))
        ps_B = stk.enter_context(nc.psum_tensor("L1ps_B", [NS, C1], F32))
        ps_C = stk.enter_context(nc.psum_tensor("L1ps_C", [NS, C1], F32))

        for h in range(2):
            for kc in range(4):
                p.op("sync", (lambda h=h, kc=kc: (lambda e: e.dma_start(
                    out=ubuf_d[0][:, kc * D:(kc + 1) * D],
                    in_=wdt_t[(4 * h + kc) * 128:(4 * h + kc + 1) * 128, :])))(),
                    waits=(p.dep("act") if (h == 1 and kc == 0) else ()),
                    inc=("ld", 16))
            p.op("scalar", (lambda h=h: (lambda e: e.copy(
                out=wdt_b[:, h * 4 * D:(h + 1) * 4 * D],
                in_=ubuf_d[0][:, 0:4 * D])))(),
                waits=p.dep("ld"), inc=("act", 1))
        for kc in range(NCH):
            p.op("sync", (lambda kc=kc: (lambda e: e.dma_start(
                out=wbc_f[:, kc * NS:(kc + 1) * NS],
                in_=wb_t[kc * 128:(kc + 1) * 128, :])))(),
                inc=("ld", 16))
            p.op("sync", (lambda kc=kc: (lambda e: e.dma_start(
                out=wbc_f[:, (NCH + kc) * NS:(NCH + kc + 1) * NS],
                in_=wc_t[kc * 128:(kc + 1) * 128, :])))(),
                inc=("ld", 16))
        p.op("sync", lambda e: e.dma_start(out=nb[:], in_=bdt_t), inc=("ld", 16))
        p.op("scalar", lambda e: e.copy(out=wbc_b[:], in_=wbc_f[:]),
             waits=p.dep("ld"), inc=("act", 1))
        p.op("vector", lambda e: e.tensor_scalar_mul(out=nb[:], in0=nb[:], scalar1=-1.0),
             waits=p.dep("ld"), inc=("dve", 1))
        p.op("vector", lambda e: e.memset(onesP[:], 1.0), inc=("dve", 1))
        p.op("vector", lambda e: e.memset(ones1[:], 1.0), inc=("dve", 1))
        p.op("vector", lambda e: e.memset(cst[:, 0:1], 1e-5), inc=("dve", 1))
        p.op("vector", lambda e: e.memset(cst[:, 0:1], 1e-5), inc=("dve", 1))
        p.op("vector", lambda e: e.memset(cst[:, 1:2], 1e-38), inc=("dve", 1))

        mark_sx = []

        def tile_body(it, b, cs, ubuf, usq, xln, xln_b, lnS, brow, crow, op):
                for c in range(NCH):
                    op("sync", (lambda b=b, c=c, cs=cs: (lambda e: e.dma_start(
                        out=ubuf[:, c * C1:(c + 1) * C1],
                        in_=u_fm[b, c * 128:(c + 1) * 128, cs:cs + C1])))(),
                        waits=([("pe", mark_sx[it - 1])] if (it >= 1 and c == 0)
                               else ()),
                        inc=("ld", 16))
                for c in range(NCH):
                    op("scalar", (lambda c=c: (lambda e: e.activation(
                        out=usq[:, c * C1:(c + 1) * C1],
                        in_=ubuf[:, c * C1:(c + 1) * C1], func=AF.Square)))(),
                        waits=p.dep("ld"), inc=("act", 1))
                for c in range(NCH):
                    op("tensor", (lambda c=c: (lambda e: e.matmul(
                        ps_sx[:, :], onesP[:], ubuf[:, c * C1:(c + 1) * C1],
                        start=(c == 0), stop=(c == NCH - 1))))(),
                        waits=p.dep("ld", "act"),
                        inc=("pe", 1) if c == NCH - 1 else None)
                mark_sx.append(p.counts["pe"])
                for c in range(NCH):
                    op("tensor", (lambda c=c: (lambda e: e.matmul(
                        ps_sx2[:, :], onesP[:], usq[:, c * C1:(c + 1) * C1],
                        start=(c == 0), stop=(c == NCH - 1))))(),
                        waits=p.dep("act"),
                        inc=("pe", 1) if c == NCH - 1 else None)
                op("scalar", lambda e: e.activation(out=m_row[:], in_=ps_sx[:],
                                                    func=AF.Copy, scale=1.0 / D),
                   waits=p.dep("pe"), inc=("act", 1))
                op("scalar", lambda e: e.activation(out=msq_row[:], in_=ps_sx[:],
                                                    func=AF.Square, scale=1.0 / D),
                   waits=p.dep("pe"), inc=("act", 1))
                op("vector", lambda e: e.scalar_tensor_tensor(
                    out=var_row[:], in0=ps_sx2[:], scalar=1.0 / D, in1=msq_row[:],
                    op0=ALU.mult, op1=ALU.subtract),
                   waits=p.dep("pe", "act"), inc=("dve", 1))
                op("scalar", lambda e: e.activation(out=rstd_row[:], in_=var_row[:],
                                                    func=AF.Sqrt, bias=cst[0:1, 0:1]),
                   waits=p.dep("dve"), inc=("act", 1))
                op("vector", lambda e: e.reciprocal(out=rstd_row[:], in_=rstd_row[:]),
                   waits=p.dep("act"), inc=("dve", 1))
                op("vector", lambda e: e.tensor_mul(out=mr_row[:], in0=m_row[:], in1=rstd_row[:]),
                   waits=p.dep("act", "dve"), inc=("dve", 1), self_wait=True)
                op("tensor", lambda e: e.matmul(ps_bc1[:, :], ones1[:], rstd_row[:],
                                                start=True, stop=True),
                   waits=p.dep("act", "dve"), inc=("pe", 1))
                op("tensor", lambda e: e.matmul(ps_bc2[:, :], ones1[:], mr_row[:],
                                                start=True, stop=True),
                   waits=p.dep("dve"), inc=("pe", 1))
                for c in range(NCH):
                    op("vector", (lambda c=c: (lambda e: e.tensor_mul(
                        out=xln[:, c * C1:(c + 1) * C1],
                        in0=ubuf[:, c * C1:(c + 1) * C1], in1=ps_bc1[:, :])))(),
                        waits=p.dep("pe"), inc=("dve", 1))
                    op("vector", (lambda c=c: (lambda e: e.tensor_sub(
                        out=xln[:, c * C1:(c + 1) * C1],
                        in0=xln[:, c * C1:(c + 1) * C1], in1=ps_bc2[:, :])))(),
                        inc=("dve", 1))
                    op("scalar", (lambda c=c: (lambda e: e.copy(
                        out=xln_b[:, c * C1:(c + 1) * C1],
                        in_=xln[:, c * C1:(c + 1) * C1])))(),
                        waits=p.dep("dve"), inc=("act", 1))
                    op("gpsimd", (lambda b=b, c=c, cs=cs: (lambda e: e.dma_start(
                        out=st_xln[b][c, :, cs:cs + C1],
                        in_=xln[:, c * C1:(c + 1) * C1])))(),
                        waits=p.dep("dve"), inc=("st", 16))
                for ec in range(NCH):
                    for kc in range(NCH):
                        op("tensor", (lambda ec=ec, kc=kc: (lambda e: e.matmul(
                            ps_dt[:, :],
                            wdt_b[:, kc * D + ec * 128: kc * D + (ec + 1) * 128],
                            xln_b[:, kc * C1:(kc + 1) * C1],
                            start=(kc == 0), stop=(kc == NCH - 1))))(),
                            waits=p.dep("act"),
                            inc=("pe", 1) if kc == NCH - 1 else None)
                    op("scalar", (lambda ec=ec: (lambda e: e.activation(
                        out=stile[:], in_=ps_dt[:], func=AF.Sigmoid,
                        bias=nb[:, ec:ec + 1], scale=-1.0)))(),
                        waits=p.dep("pe"), inc=("act", 1))
                    op("scalar", (lambda ec=ec: (lambda e: e.activation(
                        out=lnS[:, ec * C1:(ec + 1) * C1], in_=stile[:],
                        func=AF.Ln, bias=cst[:, 1:2])))(),
                        inc=("act", 1))
                    op("gpsimd", (lambda b=b, ec=ec, cs=cs: (lambda e: e.dma_start(
                        out=st_lns[b][ec, :, cs:cs + C1],
                        in_=lnS[:, ec * C1:(ec + 1) * C1])))(),
                        waits=p.dep("act"), inc=("st", 16))
                for kc in range(NCH):
                    op("tensor", (lambda kc=kc: (lambda e: e.matmul(
                        ps_B[:, :], wbc_b[:, kc * NS:(kc + 1) * NS],
                        xln_b[:, kc * C1:(kc + 1) * C1],
                        start=(kc == 0), stop=(kc == NCH - 1))))(),
                        waits=p.dep("act"),
                        inc=("pe", 1) if kc == NCH - 1 else None)
                for kc in range(NCH):
                    op("tensor", (lambda kc=kc: (lambda e: e.matmul(
                        ps_C[:, :], wbc_b[:, (NCH + kc) * NS:(NCH + kc + 1) * NS],
                        xln_b[:, kc * C1:(kc + 1) * C1],
                        start=(kc == 0), stop=(kc == NCH - 1))))(),
                        inc=("pe", 1) if kc == NCH - 1 else None)
                op("scalar", lambda e: e.copy(out=brow[:], in_=ps_B[:]),
                   waits=p.dep("pe"), inc=("act", 1))
                op("scalar", lambda e: e.copy(out=crow[:], in_=ps_C[:]),
                   inc=("act", 1))
                op("gpsimd", (lambda b=b, cs=cs: (lambda e: e.dma_start(
                    out=st_bc[b][0:NS, cs:cs + C1], in_=brow[:])))(),
                    waits=p.dep("act"), inc=("st", 16))
                op("gpsimd", (lambda b=b, cs=cs: (lambda e: e.dma_start(
                    out=st_bc[b][NS:2 * NS, cs:cs + C1], in_=crow[:])))(),
                    inc=("st", 16))
                if hf == TL // C1 - 1 and not SKIP_CC:
                    for j, (nm_in, nm_out) in enumerate((("st_xln", "st_xln_G"),
                                                         ("st_lns", "st_lns_G"),
                                                         ("st_bc", "st_bc_g"))):
                        cw = [("ccx%d" % b, j)] if j else (
                            [("ccx%d" % (b - 1), 3)] if b else [])
                        op("gpsimd", (lambda b=b, nm_in=nm_in, nm_out=nm_out:
                            (lambda e: e.collective_compute(
                                "AllGather", ALU.bypass,
                                replica_groups=[list(range(NCORES))],
                                ins=[io[nm_in][b].opt()],
                                outs=[io[nm_out][b].opt()])))(),
                            waits=p.dep("st") + cw, inc=("ccx%d" % b, 1),
                            self_wait=True)


        snaps = []
        for it in range(ntile):
            b, hf = divmod(it, TL // C1)
            cs = hf * C1
            snaps.append(p.snap())
            if it >= 1:
                start = snaps[it - 1]
            else:
                start = {"act": snaps[0]["act"]}
            first = {e: True for e in ENGS}

            def op(eng, fn, waits=(), inc=None, self_wait=False,
                   _first=first, _start=start):
                w = list(waits)
                if _first[eng]:
                    w += [(s, v) for s, v in _start.items()
                          if not s.startswith("cc")]
                    _first[eng] = False
                p.op(eng, fn, w, inc, self_wait=self_wait)

            tile_body(it, b, cs, ubuf_d[it % 2], usq_d[it % 2], xln_d[it % 2],
                      xln_b_d[it % 2], lnS_d[it % 2], brow_d[it % 2],
                      crow_d[it % 2], op)

        with nc.Block() as block:
            _replay_all(p, block, sem)



def emit_l2(nc, p, sem, io):
    logA_d = io["logA_d"]; Dp_d = io["Dp_d"]
    rowsel = io["rowsel"]
    st_xln_g = io["st_xln_loc"]; st_lns_g = io["st_lns_loc"]; st_bc_g = io["st_bc_g"]
    st_y = io["st_y"]

    TC = 1024
    with contextlib.ExitStack() as stk:

        def sb(name, shape, dt=F32):
            return stk.enter_context(nc.sbuf_tensor("L2" + name, shape, dt))

        Ut_b = [sb(f"Ut{i}", [128, TC]) for i in range(2)]
        Lt_b = [sb(f"Lt{i}", [128, TC]) for i in range(2)]
        dut_b = [sb(f"dut{i}", [128, TC]) for i in range(2)]
        bc_scr = sb("bc_scr", [NS, TG])
        Bhi = sb("Bhi", [NS, TG], BF16)
        Blo = sb("Blo", [NS, TG], BF16)
        Chi = sb("Chi", [NS, TG], BF16)
        Clo = sb("Clo", [NS, TG], BF16)
        logA_sb = sb("logA_sb", [128, NS])
        scaleP = sb("scaleP", [128, NS])
        Dp_sb = sb("Dp_sb", [128, 1])
        rs_sb = sb("rs_sb", [NS, NS * 128])
        rs_bf = sb("rs_bf", [NS, NS * 128], BF16)
        da_b = [sb(f"da{i}", [128, TC]) for i in range(2)]
        dbu = sb("dbu", [128, TC])
        xbuf = sb("xbuf", [128, TC])
        zsheet = sb("zsheet", [128, TC * NS], BF16)
        state = sb("state", [128, B * NS])
        ysum = sb("ysum", [128, TC])
        yout_b = [sb(f"yout{i}", [128, TC]) for i in range(2)]

        ps_Bbc = [stk.enter_context(nc.psum_tensor(f"L2ps_Bbc{i}", [128, TC], F32))
                  for i in range(2)]
        ps_Cbc = [stk.enter_context(nc.psum_tensor(f"L2ps_Cbc{i}", [128, TC], F32))
                  for i in range(2)]

        sect = p.snap()
        p.op("sync", lambda e: e.dma_start(out=logA_sb[:], in_=logA_d),
             waits=[(s, v) for s, v in sect.items()], inc=("ld", 16))
        p.op("sync", lambda e: e.dma_start(out=Dp_sb[:], in_=Dp_d), inc=("ld", 16))
        p.op("sync", lambda e: e.dma_start(out=rs_sb[:], in_=rowsel), inc=("ld", 16))
        p.op("scalar", lambda e: e.activation(out=scaleP[:], in_=logA_sb[:], func=AF.Exp),
             waits=p.dep("ld"), inc=("act", 1))
        p.op("scalar", lambda e: e.copy(out=rs_bf[:], in_=rs_sb[:]),
             waits=p.dep("ld"), inc=("act", 1))

        for b in range(B):
            start = p.snap()
            first = {e: True for e in ENGS}
            swx_need = io["sw_mark_x"][b]

            def op(eng, fn, waits=(), inc=None, self_wait=False, _b=b):
                w = list(waits)
                if first[eng]:
                    w += [(s, v) for s, v in start.items()
                          if s not in ("gpc", "swx") and not s.startswith("cc")]
                    w += [("ccx%d" % _b, 3), ("swx", swx_need)]
                    self_wait = True
                    first[eng] = False
                p.op(eng, fn, w, inc, self_wait=self_wait)

            for sc_ in range(NCORES):
                op("sync", (lambda b=b, sc_=sc_: (lambda e: e.dma_start(
                    out=bc_scr[:, sc_ * TL:(sc_ + 1) * TL],
                    in_=st_bc_g[b][sc_, 0:NS])))(), inc=("ld", 16))
            op("scalar", lambda e: e.copy(out=Bhi[:], in_=bc_scr[:]),
               waits=p.dep("ld"), inc=("act", 1))
            op("vector", lambda e: e.tensor_sub(out=Blo[:], in0=bc_scr[:], in1=Bhi[:]),
               waits=p.dep("act"), inc=("dve", 1))
            for sc_ in range(NCORES):
                op("sync", (lambda b=b, sc_=sc_: (lambda e: e.dma_start(
                    out=bc_scr[:, sc_ * TL:(sc_ + 1) * TL],
                    in_=st_bc_g[b][sc_, NS:2 * NS])))(),
                    waits=p.dep("dve"), inc=("ld", 16))
            op("scalar", lambda e: e.copy(out=Chi[:], in_=bc_scr[:]),
               waits=p.dep("ld"), inc=("act", 1))
            op("vector", lambda e: e.tensor_sub(out=Clo[:], in0=bc_scr[:], in1=Chi[:]),
               waits=p.dep("act"), inc=("dve", 1))
            bc_ready_dve = p.counts["dve"]
            bc_ready_act = p.counts["act"]

            mark_tts = {}
            mark_z = {}
            mark_tile = {}
            mark_act_tile = {}
            mark_st = {}
            mark_dut = {}
            mark_dA0 = {}
            for ti in range(TG // TC):
                cs = ti * TC
                ub = Ut_b[ti % 2]
                lb = Lt_b[ti % 2]
                mt = mark_tile.get(ti - 2, (0, 0))
                op("sync", (lambda b=b, ti=ti, ub=ub: (lambda e: e.dma_start(
                    out=ub[:], in_=st_xln_g[b, ti])))(),
                    waits=[("dve", max(mt[0], mark_dut.get(ti - 1, 0))),
                           ("gpc", mt[1]),
                           ("act", max(mark_act_tile.get(ti - 2, 0),
                                       mark_dA0.get(ti - 1, 0)))],
                    inc=("ld", 16))
                op("sync", (lambda b=b, ti=ti, lb=lb: (lambda e: e.dma_start(
                    out=lb[:], in_=st_lns_g[b, ti])))(), inc=("ld", 16))
                op("vector", (lambda ub=ub, lb=lb, ti=ti: (lambda e: e.scalar_tensor_tensor(
                    out=dut_b[ti % 2][:], in0=lb[:], scalar=-1.0,
                    in1=ub[:], op0=ALU.mult, op1=ALU.mult)))(),
                    waits=p.dep("ld"), inc=("dve", 1))
                mark_dut[ti] = p.counts["dve"]
                ld_tile = p.counts["ld"]
                for n in range(NS):
                    sidx = b * NS + n
                    psB = ps_Bbc[n % 2]
                    psC = ps_Cbc[n % 2]
                    dab = da_b[n % 2]
                    bank_free = max(
                        mark_z.get((ti, n - 2), mark_z.get((ti - 1, NS + n - 2), 0)),
                        bc_ready_dve)
                    first_mm = True
                    for psX, hi, lo in ((psB, Bhi, Blo), (psC, Chi, Clo)):
                        for h2 in range(2):
                            for hl, srcm in ((0, hi), (1, lo)):
                                op("tensor", (lambda n=n, cs=cs, h2=h2, hl=hl,
                                              srcm=srcm, psX=psX: (lambda e: e.matmul(
                                    psX[:, h2 * 512:(h2 + 1) * 512],
                                    rs_bf[:, n * 128:(n + 1) * 128],
                                    srcm[0:NS, cs + h2 * 512: cs + (h2 + 1) * 512],
                                    start=(hl == 0), stop=(hl == 1))))(),
                                    waits=([("dve", bank_free),
                                            ("act", bc_ready_act)]
                                           if first_mm else ()),
                                    inc=("pe", 1) if (hl == 1) else None)
                                first_mm = False
                    op("scalar", (lambda n=n, dab=dab, lb=lb: (lambda e: e.activation(
                        out=dab[:], in_=lb[:], func=AF.Exp,
                        scale=scaleP[:, n:n + 1])))(),
                        waits=[("dve", mark_tts.get((ti, n - 2),
                                mark_tts.get((ti - 1, NS + n - 2), 0))),
                               ("ld", ld_tile)],
                        inc=("act", 1))
                    if n == 0:
                        mark_dA0[ti] = p.counts["act"]
                    if n == NS - 1:
                        mark_act_tile[ti] = p.counts["act"]
                    op("vector", (lambda ti=ti, psB=psB: (lambda e: e.tensor_mul(
                        out=dbu[:], in0=dut_b[ti % 2][:], in1=psB[:, :])))(),
                        waits=p.dep("pe", "act"), inc=("dve", 1))
                    if ti == 0:
                        op("vector", (lambda dab=dab: (lambda e: e.tensor_tensor_scan(
                            out=xbuf[:], data0=dab[:], data1=dbu[:], initial=0.0,
                            op0=ALU.mult, op1=ALU.add)))(),
                            inc=("dve", 1))
                    else:
                        op("vector", (lambda sidx=sidx, dab=dab:
                            (lambda e: e.tensor_tensor_scan(
                                out=xbuf[:], data0=dab[:], data1=dbu[:],
                                initial=state[:, sidx:sidx + 1],
                                op0=ALU.mult, op1=ALU.add)))(),
                            inc=("dve", 1))
                    mark_tts[(ti, n)] = p.counts["dve"]
                    op("scalar", (lambda sidx=sidx: (lambda e: e.copy(
                        out=state[:, sidx:sidx + 1], in_=xbuf[:, TC - 1:TC])))(),
                        waits=p.dep("dve"), inc=("act", 1))
                    op("vector", (lambda n=n, psC=psC: (lambda e: e.tensor_tensor(
                        out=zsheet[:, n * TC:(n + 1) * TC],
                        in0=xbuf[:], in1=psC[:, :], op=ALU.mult)))(),
                        inc=("dve", 1))
                    mark_z[(ti, n)] = p.counts["dve"]
                op("vector", (lambda: (lambda e: e.tensor_reduce(
                    out=ysum[:], in_=zsheet[:].rearrange("p (n t) -> p t n", n=NS),
                    axis=AX.X, op=ALU.add)))(),
                    inc=("dve", 1))
                op("vector", (lambda ub=ub, ti=ti: (lambda e: e.scalar_tensor_tensor(
                    out=yout_b[ti % 2][:], in0=ub[:], scalar=Dp_sb[:],
                    in1=ysum[:], op0=ALU.mult, op1=ALU.add)))(),
                    waits=[("st", mark_st.get(ti - 1, 0))],
                    inc=("dve", 1))
                mark_tile[ti] = (p.counts["dve"], p.counts["gpc"])
                op("gpsimd", (lambda b=b, ti=ti: (lambda e: e.dma_start(
                    out=st_y[b][ti], in_=yout_b[ti % 2][:])))(),
                    waits=p.dep("dve"), inc=("st", 16))
                mark_st[ti] = p.counts["st"]
                if ti == TG // TC - 1 and b == B - 1:
                    for b2 in range(B):
                        cw = ([("ccy%d" % (b2 - 1), 1)] if b2 else [("ccx3", 3)])
                        op("gpsimd", (lambda b2=b2: (lambda e: e.collective_compute(
                            "AllGather", ALU.bypass,
                            replica_groups=[list(range(NCORES))],
                            ins=[io["st_y"][b2].opt()],
                            outs=[io["st_y_G"][b2].opt()])))(),
                            waits=p.dep("st") + cw, inc=("ccy%d" % b2, 1),
                            self_wait=True)

        with nc.allow_low_precision("z-sheet / y-contraction are bf16 matmul-class operands; reduce accumulates fp32 internally"):
            with nc.Block() as block:
                _replay_all(p, block, sem)


def emit_l3(nc, p, sem, io):
    u3 = io["u3"]; wq_t = io["wq_t"]; wk_t = io["wk_t"]; wv_t = io["wv_t"]
    wo_t = io["wo_t"]; bo_t = io["bo_t"]; krl = io["krl"]; krr = io["krr"]
    bias_ri = io["bias_ri"]; sel_c = io["sel_c"]; rk_in = io["rk_in"]
    exp_in = io["exp_in"]; rv_in = io["rv_in"]
    st_y_g = io["st_y_loc"]; out_h = io["out_h"]

    CT = 512
    NT = TL * B // CT

    with contextlib.ExitStack() as stk:

        def sb(name, shape, dt=F32):
            return stk.enter_context(nc.sbuf_tensor("L3" + name, shape, dt))

        w_bf = {}
        for nm in ("wq", "wk", "wv", "wo"):
            w_bf[nm] = sb(nm + "_bf", [128, NCH * D], BF16)
        wrt_bf = sb("wrt_bf", [128, 4 * 512], BF16)
        wit_bf = sb("wit_bf", [128, 4 * 512], BF16)
        witn_bf = sb("witn_bf", [128, 4 * 512], BF16)
        bo_sb = sb("bo_sb", [128, NCH])
        bri_sb = sb("bri_sb", [128, NCH])
        sel_bf = sb("sel_bf", [128, NCH * H], BF16)
        rk_bf = sb("rk_bf", [128, B * B * NCH * H], BF16)
        exp_bf = sb("exp_bf", [H, NCH * 128], BF16)
        attn_bf = sb("attn_bf", [H, B * CT], BF16)
        rv_sb = sb("rv_sb", [128, H])
        onesP = sb("onesP", [128, 1])
        ones1 = sb("ones1", [1, 128])
        cst = sb("cst", [128, 2])
        zrow = sb("zrow", [1, CT])

        xA_d = [sb("xA%d" % i, [128, NCH * CT]) for i in range(2)]
        xA = xA_d[0]
        xB = sb("xB", [128, NCH * CT])
        ybuf4 = sb("ybuf4", [128, (NCH // 2) * CT])
        xln_b = sb("xln_b", [128, NCH * CT], BF16)
        tmp = sb("tmp", [128, CT])
        qtile = sb("qtile", [128, CT], BF16)
        ktile = sb("ktile", [128, CT], BF16)
        vf = sb("vf", [128, NCH * CT])
        usq = vf
        zbuf = sb("zbuf", [128, CT], BF16)
        scbuf = sb("scbuf", [H, B * CT])
        rrow = sb("rrow", [1, CT])
        vrv = sb("vrv", [128, CT])
        vz = sb("vz", [128, CT])
        aout_b = sb("aout_b", [128, NCH * CT], BF16)
        m_row = sb("m_row", [1, CT])
        msq_row = sb("msq_row", [1, CT])
        var_row = sb("var_row", [1, CT])
        rstd_row = sb("rstd_row", [1, CT])
        mr_row = sb("mr_row", [1, CT])

        ps_bc1 = stk.enter_context(nc.psum_tensor("L3ps_bc1", [128, CT], F32))
        ps_bc2 = stk.enter_context(nc.psum_tensor("L3ps_bc2", [128, CT], F32))
        ps_mm = stk.enter_context(nc.psum_tensor("L3ps_mm", [128, CT], F32))
        ps_sc_all = [stk.enter_context(nc.psum_tensor(f"ps_sc{s}", [128, CT], F32))
                     for s in range(4)]
        ps_st = stk.enter_context(nc.psum_tensor("L3ps_st", [128, CT], F32))

        sect = p.snap()
        sect_first = [True]

        def ld(dst, src):
            w = []
            if sect_first[0]:
                w = [(s, v) for s, v in sect.items()]
                sect_first[0] = False
            p.op("sync", (lambda dst=dst, src=src: (lambda e: e.dma_start(
                out=dst, in_=src)))(), waits=w, inc=("ld", 16))

        ld(bo_sb[:], bo_t)
        ld(bri_sb[:], bias_ri)
        ld(xA[:, 0:NCH * H], sel_c)
        ld(xA[:, 128:128 + B * B * NCH * H], rk_in)
        ld(xA[0:H, 2400:2400 + NCH * 128], exp_in)
        ld(rv_sb[:], rv_in)
        p.op("scalar", lambda e: e.copy(out=sel_bf[:], in_=xA[:, 0:NCH * H]),
             waits=p.dep("ld"), inc=("act", 1))
        p.op("scalar", lambda e: e.copy(out=rk_bf[:], in_=xA[:, 128:128 + B * B * NCH * H]),
             waits=p.dep("ld"), inc=("act", 1))
        p.op("scalar", lambda e: e.copy(out=exp_bf[:], in_=xA[0:H, 2400:2400 + NCH * 128]),
             waits=p.dep("ld"), inc=("act", 1))
        p.op("vector", lambda e: e.memset(onesP[:], 1.0), inc=("dve", 1))
        p.op("vector", lambda e: e.memset(ones1[:], 1.0), inc=("dve", 1))
        p.op("vector", lambda e: e.memset(cst[:, 0:1], 1e-5), inc=("dve", 1))
        p.op("vector", lambda e: e.memset(zrow[:], 0.0), inc=("dve", 1))

        for ri in range(2):
            dstf = wrt_bf if ri == 0 else wit_bf
            for c in range(4):
                for pc in range(4):
                    p.op("sync", (lambda ri=ri, c=c, pc=pc: (lambda e: e.dma_start(
                        out=xA[:, 0:512],
                        in_=krr[ri, c, pc * 128:(pc + 1) * 128, :])))(),
                        waits=p.dep("pe", "act"), inc=("ld", 16))
                    p.op("sync", (lambda ri=ri, c=c, pc=pc: (lambda e: e.dma_start(
                        out=xA[:, 512:1024],
                        in_=krl[ri, c, pc * 128:(pc + 1) * 128, :])))(),
                        inc=("ld", 16))
                    for qc in range(4):
                        p.op("tensor", (lambda qc=qc, c=c, pc=pc: (lambda e: e.matmul(
                            ps_sc_all[qc][:, :],
                            xA[:, 512 + qc * 128: 512 + (qc + 1) * 128],
                            xA[:, 0:512],
                            start=(c == 0 and pc == 0),
                            stop=(c == 3 and pc == 3))))(),
                            waits=p.dep("ld", "act"),
                            inc=("pe", 1) if qc == 3 else None)
            for qc in range(4):
                p.op("scalar", (lambda qc=qc, dstf=dstf: (lambda e: e.copy(
                    out=dstf[:, qc * 512:(qc + 1) * 512],
                    in_=ps_sc_all[qc][:, :])))(),
                    waits=p.dep("pe"), inc=("act", 1))
                if ri == 1:
                    p.op("scalar", (lambda qc=qc: (lambda e: e.activation(
                        out=witn_bf[:, qc * 512:(qc + 1) * 512],
                        in_=ps_sc_all[qc][:, :], func=AF.Copy, scale=-1.0)))(),
                        inc=("act", 1))
        for nm, srcw in (("wq", wq_t), ("wk", wk_t), ("wv", wv_t), ("wo", wo_t)):
            for kc in range(NCH):
                p.op("sync", (lambda kc=kc, srcw=srcw: (lambda e: e.dma_start(
                    out=xA[:, (kc % 4) * 1024:(kc % 4 + 1) * 1024],
                    in_=srcw[kc * 128:(kc + 1) * 128, :])))(),
                    waits=p.dep("act"), inc=("ld", 16))
                p.op("scalar", (lambda nm=nm, kc=kc: (lambda e: e.copy(
                    out=w_bf[nm][:, kc * D:(kc + 1) * D],
                    in_=xA[:, (kc % 4) * 1024:(kc % 4 + 1) * 1024])))(),
                    waits=p.dep("ld"), inc=("act", 1))

        def ln_block(op, xsrc, dep_sem):
            for c in range(NCH):
                op("scalar", (lambda c=c: (lambda e: e.activation(
                    out=usq[:, c * CT:(c + 1) * CT],
                    in_=xsrc[:, c * CT:(c + 1) * CT], func=AF.Square)))(),
                    waits=p.dep(dep_sem), inc=("act", 1))
            for c in range(NCH):
                op("tensor", (lambda c=c: (lambda e: e.matmul(
                    ps_st[0:1, :], onesP[:], xsrc[:, c * CT:(c + 1) * CT],
                    start=(c == 0), stop=(c == NCH - 1))))(),
                    waits=p.dep(dep_sem),
                    inc=("pe", 1) if c == NCH - 1 else None)
            for c in range(NCH):
                op("tensor", (lambda c=c: (lambda e: e.matmul(
                    ps_st[32:33, :], onesP[:], usq[:, c * CT:(c + 1) * CT],
                    start=(c == 0), stop=(c == NCH - 1))))(),
                    waits=p.dep("act"),
                    inc=("pe", 1) if c == NCH - 1 else None)
            op("scalar", lambda e: e.activation(out=m_row[:], in_=ps_st[0:1, :],
                                                func=AF.Copy, scale=1.0 / D),
               waits=p.dep("pe"), inc=("act", 1))
            op("scalar", lambda e: e.activation(out=msq_row[:], in_=ps_st[0:1, :],
                                                func=AF.Square, scale=1.0 / D),
               waits=p.dep("pe"), inc=("act", 1))
            op("vector", lambda e: e.scalar_tensor_tensor(
                out=var_row[:], in0=ps_st[32:33, :], scalar=1.0 / D, in1=msq_row[:],
                op0=ALU.mult, op1=ALU.subtract),
               waits=p.dep("pe", "act"), inc=("dve", 1))
            op("scalar", lambda e: e.activation(out=rstd_row[:], in_=var_row[:],
                                                func=AF.Sqrt, bias=cst[0:1, 0:1]),
               waits=p.dep("dve"), inc=("act", 1))
            op("vector", lambda e: e.reciprocal(out=rstd_row[:], in_=rstd_row[:]),
               waits=p.dep("act"), inc=("dve", 1))
            op("vector", lambda e: e.tensor_mul(out=mr_row[:], in0=m_row[:],
                                                in1=rstd_row[:]),
               waits=p.dep("act", "dve"), inc=("dve", 1), self_wait=True)
            op("tensor", lambda e: e.matmul(ps_bc1[:, :], ones1[:], rstd_row[:],
                                            start=True, stop=True),
               waits=p.dep("act", "dve"), inc=("pe", 1))
            op("tensor", lambda e: e.matmul(ps_bc2[:, :], ones1[:], mr_row[:],
                                            start=True, stop=True),
               waits=p.dep("dve"), inc=("pe", 1))
            for c in range(NCH):
                op("vector", (lambda c=c: (lambda e: e.tensor_mul(
                    out=tmp[:], in0=xsrc[:, c * CT:(c + 1) * CT], in1=ps_bc1[:, :])))(),
                    waits=p.dep("pe"), inc=("dve", 1))
                op("vector", (lambda c=c: (lambda e: e.tensor_tensor(
                    out=xln_b[:, c * CT:(c + 1) * CT], in0=tmp[:], in1=ps_bc2[:, :],
                    op=ALU.subtract)))(),
                    inc=("dve", 1))

        def ffn(op, xres, xdst, extra_w=()):
            for oc in range(NCH):
                half = oc // 4
                o128 = (oc % 4) * 128
                wa, wb2 = (wrt_bf, witn_bf) if half == 0 else (wit_bf, wrt_bf)
                for qc in range(4):
                    op("tensor", (lambda wa=wa, qc=qc, o128=o128: (lambda e: e.matmul(
                        ps_mm[:, :], wa[:, qc * 512 + o128: qc * 512 + o128 + 128],
                        xln_b[:, qc * CT:(qc + 1) * CT],
                        start=(qc == 0), stop=False)))(),
                        waits=p.dep("dve", "act") if qc == 0 else ())
                for qc in range(4):
                    op("tensor", (lambda wb2=wb2, qc=qc, o128=o128: (lambda e: e.matmul(
                        ps_mm[:, :], wb2[:, qc * 512 + o128: qc * 512 + o128 + 128],
                        xln_b[:, (4 + qc) * CT:(4 + qc + 1) * CT],
                        start=False, stop=(qc == 3))))(),
                        inc=("pe", 1) if qc == 3 else None)
                op("vector", (lambda oc=oc: (lambda e: e.scalar_tensor_tensor(
                    out=xdst[:, oc * CT:(oc + 1) * CT], in0=ps_mm[:, :],
                    scalar=bri_sb[:, oc:oc + 1],
                    in1=xres[:, oc * CT:(oc + 1) * CT],
                    op0=ALU.add, op1=ALU.add)))(),
                    waits=p.dep("pe") + (list(extra_w) if oc == 0 else []),
                    inc=("dve", 1))

        def proj_chunk(op, wname, dst, oc, rhs_bf, evac_eng="scalar"):
            for kc in range(NCH):
                op("tensor", (lambda wname=wname, oc=oc, kc=kc: (lambda e: e.matmul(
                    ps_mm[:, :],
                    w_bf[wname][:, kc * D + oc * 128: kc * D + (oc + 1) * 128],
                    rhs_bf[:, kc * CT:(kc + 1) * CT],
                    start=(kc == 0), stop=(kc == NCH - 1))))(),
                    waits=p.dep("dve", "act") if kc == 0 else (),
                    inc=("pe", 1) if kc == NCH - 1 else None)
            op(evac_eng, (lambda dst=dst: (lambda e: e.copy(
                out=dst, in_=ps_mm[:, :])))(),
                waits=p.dep("pe"), inc=("act", 1))

        with nc.allow_low_precision("setup flush"):
            with nc.Block() as block:
                _replay_all(p, block, sem)
        for arm in nc.switch_core_id(engines=[nc.sync], n=128):
            me = arm.logical % NCORES
            for b in range(B):
                nc.sync.wait_ge(sem["ccy%d" % b], 1)
                nc.sync.wait_ge(sem["swx"], 32 * B + 16 * b)
                nc.sync.dma_start(out=io["st_y_loc"][b],
                                  in_=io["st_y_G"][b][:, me]).then_inc(sem["swx"], 16)
        p.counts["swx"] = 32 * B + 16 * B

        mark_x1 = []
        mark_store = []

        def tile_body(it, cs, xA, op):
            u3v = u3.rearrange("d t b -> d (t b)")
            outv = out_h.rearrange("d t b -> d (t b)")

            tau0 = it * 128
            swx_all = [("swx", 32 * B + 16 * B)] if it == 0 else []
            GC = NCH // 2
            for g in range(2):
                for c in range(g * GC, (g + 1) * GC):
                    if c == 0:
                        uw = [("dve", mark_x1[it - 1])] if it >= 1 else []
                    elif c == GC:
                        uw = p.dep("dve")
                    else:
                        uw = ()
                    op("sync", (lambda c=c, cs=cs: (lambda e: e.dma_start(
                        out=xA[:, c * CT:(c + 1) * CT],
                        in_=u3v[c * 128:(c + 1) * 128, cs:cs + CT])))(),
                        waits=uw, inc=("ld", 16))
                for c in range(g * GC, (g + 1) * GC):
                    for bb in range(B):
                        cg = c - g * GC
                        op("sync", (lambda c=c, cg=cg, bb=bb, tau0=tau0:
                            (lambda e: e.dma_start(
                                out=ybuf4[:, cg * CT + bb * 128:
                                          cg * CT + (bb + 1) * 128],
                                in_=st_y_g[bb, c, :, tau0:tau0 + 128])))(),
                            waits=swx_all if (c == 0 and bb == 0) else (),
                            inc=("ld", 16),
                            self_wait=(it == 0 and c == 0 and bb == 0))
                for c in range(g * GC, (g + 1) * GC):
                    cg = c - g * GC
                    op("vector", (lambda c=c, cg=cg: (lambda e: e.tensor_add(
                        out=xA[:, c * CT:(c + 1) * CT].rearrange(
                            "p (t b) -> p t b", b=B),
                        in0=xA[:, c * CT:(c + 1) * CT].rearrange(
                            "p (t b) -> p t b", b=B),
                        in1=ybuf4[:, cg * CT:(cg + 1) * CT].rearrange(
                            "p (b t) -> p t b", b=B))))(),
                        waits=p.dep("ld"), inc=("dve", 1))
            mark_x1.append(p.counts["dve"])

            ln_block(op, xA, "dve")
            ffn(op, xA, xB,
                extra_w=([("st", mark_store[it - 1])] if it >= 1 else []))

            ln_block(op, xB, "dve")

            for oc in range(NCH):
                proj_chunk(op, "wv", vf[:, oc * CT:(oc + 1) * CT], oc, xln_b)

            for c in range(NCH):
                proj_chunk(op, "wq", qtile[:, :], c, xln_b)
                proj_chunk(op, "wk", ktile[:, :], c, xln_b)
                for s in range(B):
                    qs = qtile[:].rearrange("p (t b) -> p t b", b=B)[:, :, s:s + 1]
                    op("vector", (lambda qs=qs: (lambda e: e.tensor_tensor(
                        out=zbuf[:].rearrange("p (t b) -> p t b", b=B),
                        in0=qs.broadcast_to([128, CT // B, B]),
                        in1=ktile[:].rearrange("p (t b) -> p t b", b=B),
                        op=ALU.mult)))(),
                        waits=p.dep("act", "pe"), inc=("dve", 1))
                    if c == 0:
                        op("tensor", (lambda s=s: (lambda e: e.matmul(
                            ps_sc_all[s][:, :], zrow[:, 0:128], zrow[:],
                            start=True, stop=False,
                            skip_group_check=True)))(),
                            waits=p.dep("dve"), inc=("pe", 1))
                    op("tensor", (lambda s=s, c=c: (lambda e: e.matmul(
                        ps_sc_all[s][0:16, :], sel_bf[:, c * H:(c + 1) * H],
                        zbuf[:],
                        start=False, stop=False,
                        skip_group_check=True)))(),
                        waits=p.dep("dve"), inc=("pe", 1))
                    for t in range(B):
                        rk_tile = rk_bf[:, ((s * B + t) * NCH + c) * H:
                                        ((s * B + t) * NCH + c + 1) * H]
                        qsv = qtile[:].rearrange("p (t b) -> p b t", b=B)[:, s, :]
                        op("tensor", (lambda t=t, s=s, c=c, rk_tile=rk_tile, qsv=qsv:
                            (lambda e: e.matmul(
                                ps_sc_all[s][32:48, t * 128:(t + 1) * 128],
                                rk_tile, qsv,
                                start=False,
                                stop=(c == NCH - 1 and t == B - 1),
                                skip_group_check=True)))(),
                            inc=("pe", 1) if (c == NCH - 1 or
                                              (s == B - 1 and t == B - 1))
                            else None)
            for s in range(B):
                op("scalar", (lambda s=s: (lambda e: e.copy(
                    out=scbuf[:, s * CT:(s + 1) * CT],
                    in_=ps_sc_all[s][0:16, :])))(),
                    waits=p.dep("pe"), inc=("act", 1))
                op("vector", (lambda s=s: (lambda e: e.tensor_add(
                    out=scbuf[:, s * CT:(s + 1) * CT].rearrange(
                        "h (t b) -> h t b", b=B),
                    in0=scbuf[:, s * CT:(s + 1) * CT].rearrange(
                        "h (t b) -> h t b", b=B),
                    in1=ps_sc_all[s][32:48, :].rearrange(
                        "h (t tau) -> h tau t", tau=128))))(),
                    waits=p.dep("pe", "act"), inc=("dve", 1))
                op("scalar", (lambda s=s: (lambda e: e.activation(
                    out=scbuf[:, s * CT:(s + 1) * CT],
                    in_=scbuf[:, s * CT:(s + 1) * CT],
                    func=AF.Exp, scale=0.125)))(),
                    waits=p.dep("dve"), inc=("act", 1))
                op("tensor", (lambda s=s: (lambda e: e.matmul(
                    ps_bc2[0:1, :], onesP[0:16, :],
                    scbuf[:, s * CT:(s + 1) * CT],
                    start=True, stop=True)))(),
                    waits=p.dep("act"), inc=("pe", 1))
                op("vector", lambda e: e.reciprocal(out=rrow[:], in_=ps_bc2[0:1, :]),
                   waits=p.dep("pe"), inc=("dve", 1))
                op("tensor", lambda e: e.matmul(
                    ps_bc2[32:48, :], ones1[:, 0:16], rrow[:],
                    start=True, stop=True),
                   waits=p.dep("dve"), inc=("pe", 1))
                op("vector", (lambda s=s: (lambda e: e.tensor_mul(
                    out=scbuf[:, s * CT:(s + 1) * CT],
                    in0=scbuf[:, s * CT:(s + 1) * CT],
                    in1=ps_bc2[32:48, :])))(),
                    waits=p.dep("pe"), inc=("dve", 1))
                op("scalar", (lambda s=s: (lambda e: e.copy(
                    out=attn_bf[:, s * CT:(s + 1) * CT],
                    in_=scbuf[:, s * CT:(s + 1) * CT])))(),
                    waits=p.dep("dve"), inc=("act", 1))
            for s in range(B):
                for c in range(NCH):
                    op("tensor", (lambda s=s, c=c: (lambda e: e.matmul(
                        ps_bc1[:, :], exp_bf[:, c * 128:(c + 1) * 128],
                        attn_bf[:, s * CT:(s + 1) * CT],
                        start=True, stop=True)))(),
                        waits=p.dep("dve", "act"), inc=("pe", 1))
                    op("vector", (lambda s=s, c=c: (lambda e: e.tensor_tensor(
                        out=vrv[:].rearrange("p (t b) -> p t b", b=B),
                        in0=vf[:, c * CT:(c + 1) * CT].rearrange(
                            "p (t b) -> p t b", b=B),
                        in1=rv_sb[:, None, s * B:(s + 1) * B].broadcast_to(
                            [128, CT // B, B]),
                        op=ALU.add)))(),
                        inc=("dve", 1))
                    op("vector", (lambda: (lambda e: e.tensor_mul(
                        out=vz[:], in0=ps_bc1[:, :], in1=vrv[:])))(),
                        waits=p.dep("pe"), inc=("dve", 1))
                    op("vector", (lambda s=s, c=c: (lambda e: e.tensor_reduce(
                        out=aout_b[:, c * CT:(c + 1) * CT].rearrange(
                            "p (t b) -> p b t", b=B)[:, s, :],
                        in_=vz[:].rearrange("p (t b) -> p t b", b=B),
                        axis=AX.X, op=ALU.add)))(),
                        inc=("dve", 1))
            for oc in range(NCH):
                for kc in range(NCH):
                    op("tensor", (lambda oc=oc, kc=kc: (lambda e: e.matmul(
                        ps_mm[:, :],
                        w_bf["wo"][:, kc * D + oc * 128: kc * D + (oc + 1) * 128],
                        aout_b[:, kc * CT:(kc + 1) * CT],
                        start=(kc == 0), stop=(kc == NCH - 1))))(),
                        waits=p.dep("dve", "act") if kc == 0 else (),
                        inc=("pe", 1) if kc == NCH - 1 else None)
                op("vector", (lambda oc=oc: (lambda e: e.scalar_tensor_tensor(
                    out=xA[:, oc * CT:(oc + 1) * CT], in0=ps_mm[:, :],
                    scalar=bo_sb[:, oc:oc + 1],
                    in1=xB[:, oc * CT:(oc + 1) * CT],
                    op0=ALU.add, op1=ALU.add)))(),
                    waits=p.dep("pe"), inc=("dve", 1))
            ln_block(op, xA, "dve")
            ffn(op, xA, xB)
            for c in range(NCH):
                op("gpsimd", (lambda c=c, cs=cs: (lambda e: e.dma_start(
                    out=outv[c * 128:(c + 1) * 128, cs:cs + CT],
                    in_=xB[:, c * CT:(c + 1) * CT])))(),
                    waits=p.dep("dve"), inc=("st", 16))
            mark_store.append(p.counts["st"])

        LAG = {"sync": 1, "scalar": 0, "vector": 1, "tensor": 1, "gpsimd": 0}
        FULLSEM = {"tensor": ("act",)}
        snaps3 = []
        for it in range(NT):
            snaps3.append(p.snap())
            start = {e: dict(snaps3[max(it - LAG[e], 0)]) for e in ENGS}
            for e, ss in FULLSEM.items():
                for s in ss:
                    start[e][s] = snaps3[it][s]
            first = {e: True for e in ENGS}

            def op(eng, fn, waits=(), inc=None, self_wait=False,
                   _first=first, _start=start):
                w = list(waits)
                if _first[eng]:
                    w += [(s, v) for s, v in _start[eng].items()]
                    _first[eng] = False
                p.op(eng, fn, w, inc, self_wait=self_wait)

            tile_body(it, it * CT, xA_d[it % 2], op)

        p.op("sync", lambda e: e.nop(), waits=p.dep("st"))
        with nc.allow_low_precision("DVE reduces in fp32 internally; bf16 is the matmul operand dtype"):
            with nc.Block() as block:
                _replay_all(p, block, sem)




def build_full(upto=3):
    nc = bass.Bass("TRN2", target_bir_lowering=False,
                   detect_race_conditions=False)
    io = {}
    ext_in = [
        ("u_fm", [B, D, TL]), ("wdt_t", [D, D]), ("wb_t", [D, NS]),
        ("wc_t", [D, NS]), ("bdt_t", [128, NCH]),
        ("logA_d", [128, NS]), ("Dp_d", [128, 1]), ("id128", [128, 128]),
        ("rowsel", [NS, NS * 128]),
        ("u3", [D, TL, B]), ("wq_t", [D, D]), ("wk_t", [D, D]),
        ("wv_t", [D, D]), ("wo_t", [D, D]), ("bo_t", [128, NCH]),
        ("krl", [2, 4, 512, 512]), ("krr", [2, 4, 512, 512]),
        ("bias_ri", [128, NCH]), ("sel_c", [128, NCH * H]),
        ("rk_in", [128, B * B * NCH * H]), ("exp_in", [H, NCH * 128]),
        ("rv_in", [128, H]),
    ]
    for nm, shp in ext_in:
        io[nm] = nc.dram_tensor(nm, shp, F32, kind="ExternalInput").ap()
    io["out_h"] = nc.dram_tensor("out_h", [D, TL, B], F32,
                                 kind="ExternalOutput").ap()
    for nm in ("st_xln", "st_lns", "st_y"):
        io[nm] = [nc.dram_tensor(f"{nm}{b}", [NCH, 128, TL], F32).ap()
                  for b in range(B)]
    for nm in ("st_xln_loc", "st_lns_loc", "st_y_loc"):
        io[nm] = nc.dram_tensor(nm, [B, NCH, 128, TL], F32).ap()
    for nm in ("st_xln_G", "st_lns_G", "st_y_G"):
        io[nm] = [nc.dram_tensor(f"{nm}{b}", [NCORES, NCH, 128, TL], F32).ap()
                  for b in range(B)]
    io["st_bc"] = [nc.dram_tensor(f"st_bc{b}", [2 * NS, TL], F32).ap()
                   for b in range(B)]
    io["st_bc_g"] = [nc.dram_tensor(f"st_bc_g{b}", [NCORES, 2 * NS, TL], F32).ap()
                     for b in range(B)]

    io["cc_mark_x"] = {}
    io["cc_mark_y"] = {}
    io["sw_mark_x"] = {}
    with contextlib.ExitStack() as stk:
        sem = _mk_sems(nc, stk)
        p = Prog(SEMS)
        emit_l1(nc, p, sem, io)
        for arm in nc.switch_core_id(engines=[nc.sync], n=128):
            me = arm.logical % NCORES
            for b in range(B):
                nc.sync.wait_ge(sem["ccx%d" % b], 3)
                if b:
                    nc.sync.wait_ge(sem["swx"], 32 * b)
                nc.sync.dma_start(out=io["st_xln_loc"][b],
                                  in_=io["st_xln_G"][b][:, me]).then_inc(sem["swx"], 16)
                nc.sync.dma_start(out=io["st_lns_loc"][b],
                                  in_=io["st_lns_G"][b][:, me]).then_inc(sem["swx"], 16)
        p.counts["swx"] = 32 * B
        p.owner["swx"] = "sync"
        for b in range(B):
            io["sw_mark_x"][b] = 32 * (b + 1)
        _section_barrier(p)
        if upto >= 2:
            emit_l2(nc, p, sem, io)
            _section_barrier(p)
        if upto >= 3:
            emit_l3(nc, p, sem, io)
        if upto < 3:
            with nc.Block() as block:
                _replay_all(p, block, sem)
    return nc


def _section_barrier(p):
    snap = p.snap()
    for eng in ENGS:
        p.op(eng, lambda e: e.nop(),
             waits=[(s, v) for s, v in snap.items()
                    if s not in ("gpc", "swx") and not s.startswith("cc")])


def _host_consts():
    sel = np.zeros((NCH, 128, H), np.float32)
    for c in range(NCH):
        for j in range(2):
            sel[c, j * 64:(j + 1) * 64, 2 * c + j] = 1.0
    expand = np.zeros((NCH, H, 128), np.float32)
    for c in range(NCH):
        for j in range(2):
            expand[c, 2 * c + j, j * 64:(j + 1) * 64] = 1.0
    id128 = np.eye(128, dtype=np.float32)
    return sel, expand, id128


def kernel(**inp):
    u = np.asarray(inp["L0_attn_a_u"], np.float32)
    W_B = np.asarray(inp["L0_attn_a_W_B"], np.float32)
    W_C = np.asarray(inp["L0_attn_a_W_C"], np.float32)
    W_dt = np.asarray(inp["L0_attn_a_W_dt"], np.float32)
    b_dt = np.asarray(inp["L0_attn_a_b_dt"], np.float32)
    log_A = np.asarray(inp["L0_attn_a_log_A"], np.float32)
    D_param = np.asarray(inp["L0_attn_a_D_param"], np.float32)
    w_q = np.asarray(inp["L0_attn_b_w_q"], np.float32)
    w_k = np.asarray(inp["L0_attn_b_w_k"], np.float32)
    w_v = np.asarray(inp["L0_attn_b_w_v"], np.float32)
    w_o = np.asarray(inp["L0_attn_b_w_o"], np.float32)
    b_o = np.asarray(inp["L0_attn_b_b_o"], np.float32)
    rel_k = np.asarray(inp["L0_attn_b_rel_k"], np.float32)
    rel_v = np.asarray(inp["L0_attn_b_rel_v"], np.float32)
    G1_R = np.asarray(inp["L0_ffn_G1_R"], np.float32)
    G2_R = np.asarray(inp["L0_ffn_G2_R"], np.float32)
    G1_I = np.asarray(inp["L0_ffn_G1_I"], np.float32)
    G2_I = np.asarray(inp["L0_ffn_G2_I"], np.float32)
    bias_real = np.asarray(inp["L0_ffn_bias_real"], np.float32)
    bias_imag = np.asarray(inp["L0_ffn_bias_imag"], np.float32)

    sel, expand, id128 = _host_consts()
    sel_in = np.ascontiguousarray(sel.transpose(1, 0, 2).reshape(128, NCH * H))
    exp_in = np.ascontiguousarray(expand.transpose(1, 0, 2).reshape(H, NCH * 128))
    rowsel = np.zeros((NS, NS, 128), np.float32)
    for n in range(NS):
        rowsel[n, n, :] = 1.0
    rowsel = np.ascontiguousarray(rowsel.transpose(1, 0, 2).reshape(NS, NS * 128))

    rel_idx = np.clip(np.arange(B)[:, None] - np.arange(B)[None, :],
                      -MAX_REL, MAX_REL) + MAX_REL
    rk_sel = rel_k[rel_idx]
    rv_sel = rel_v[rel_idx]
    rk_lhsT = np.zeros((B, B, NCH, 128, H), np.float32)
    for s in range(B):
        for t in range(B):
            for c in range(NCH):
                for j in range(2):
                    rk_lhsT[s, t, c, j * 64:(j + 1) * 64, 2 * c + j] = rk_sel[s, t]
    rk_in = np.ascontiguousarray(
        rk_lhsT.transpose(3, 0, 1, 2, 4).reshape(128, B * B * NCH * H))
    rv_in = np.zeros((128, H), np.float32)
    for s in range(B):
        for t in range(B):
            rv_in[:, s * B + t] = np.tile(rv_sel[s, t], 2)
    krl = np.zeros((2, 4, 512, 512), np.float32)
    krr = np.zeros((2, 4, 512, 512), np.float32)
    for ri, (G1x, G2x) in enumerate(((G1_R, G2_R), (G1_I, G2_I))):
        for c in range(4):
            for g in range(16):
                krl[ri, c, g::16, g::16] = G1x[0][:, c, :]
            for b2 in range(32):
                krr[ri, c, b2 * 16:(b2 + 1) * 16, b2 * 16:(b2 + 1) * 16] = \
                    G2x[c, :, 0, :].T
    bias_ri = np.ascontiguousarray(
        np.concatenate([bias_real, bias_imag]).reshape(NCH, 128).T)
    bo_t = np.ascontiguousarray(b_o.reshape(NCH, 128).T)
    bdt_t = np.ascontiguousarray(b_dt.reshape(NCH, 128).T)

    shared = {
        "wdt_t": np.ascontiguousarray(W_dt.T),
        "wb_t": np.ascontiguousarray(W_B.T),
        "wc_t": np.ascontiguousarray(W_C.T),
        "bdt_t": bdt_t, "id128": id128, "rowsel": rowsel,
        "wq_t": np.ascontiguousarray(w_q.T),
        "wk_t": np.ascontiguousarray(w_k.T),
        "wv_t": np.ascontiguousarray(w_v.T),
        "wo_t": np.ascontiguousarray(w_o.T),
        "bo_t": bo_t, "krl": krl, "krr": krr, "bias_ri": bias_ri,
        "sel_c": sel_in, "rk_in": rk_in, "exp_in": exp_in, "rv_in": rv_in,
    }

    nc = build_full()
    in_maps = []
    for k in range(NCORES):
        sl = slice(k * TL, (k + 1) * TL)
        m = dict(shared)
        m["u_fm"] = np.ascontiguousarray(u[:, sl, :].transpose(0, 2, 1))
        m["u3"] = np.ascontiguousarray(u[:, sl, :].transpose(2, 1, 0))
        m["logA_d"] = np.ascontiguousarray(log_A[k * 128:(k + 1) * 128])
        m["Dp_d"] = np.ascontiguousarray(D_param[k * 128:(k + 1) * 128, None])
        in_maps.append(m)
    res = run_bass_kernel_spmd(nc, in_maps, core_ids=list(range(NCORES)))

    out = np.empty((B, TG, D), np.float32)
    for k in range(NCORES):
        oh = res.results[k]["out_h"]
        out[:, k * TL:(k + 1) * TL, :] = oh.transpose(2, 1, 0)
    return out


# version 14 (final)
# speedup vs baseline: 1.4614x; 1.0033x over previous
import contextlib
import numpy as np
import concourse.bass as bass
import concourse.mybir as mybir
from concourse.bass_utils import run_bass_kernel_spmd
from concourse.bass_isa import ReduceOp

F32 = mybir.dt.float32
BF16 = mybir.dt.bfloat16
AF = mybir.ActivationFunctionType
ALU = mybir.AluOpType
AX = mybir.AxisListType

NCORES = 8
B = 4
TG = 8192
TL = TG // NCORES
D = 1024
NCH = D // 128
NS = 16
H = 16
DK = 64
MAX_REL = 128

SKIP_CC = False
ENGS = ("sync", "scalar", "vector", "tensor", "gpsimd")


class Prog:

    def __init__(self, sem_names):
        self.counts = {s: 0 for s in sem_names}
        self.owner = {}
        self.ops = {e: [] for e in ENGS}
        self.cursor = {e: 0 for e in ENGS}
        self.last_wait = {e: {} for e in ENGS}

    def snap(self):
        return dict(self.counts)

    def op(self, eng, fn, waits=(), inc=None, self_wait=False):
        w = [(s, v) for (s, v) in waits if v > 0]
        self.ops[eng].append((w, inc, fn, self_wait))
        if inc is not None:
            s, k = inc
            if s in self.owner:
                assert self.owner[s] == eng, (s, eng, self.owner[s])
            self.owner[s] = eng
            self.counts[s] += k

    def dep(self, *sems):
        return [(s, self.counts[s]) for s in sems]

    def replay(self, engine_obj, eng_name, handles):
        last = self.last_wait[eng_name]
        ops = self.ops[eng_name]
        start = self.cursor[eng_name]
        self.cursor[eng_name] = len(ops)
        for waits, inc, fn, self_wait in ops[start:]:
            for s, v in waits:
                if self.owner.get(s) == eng_name and not self_wait:
                    continue
                if last.get(s, 0) >= v:
                    continue
                engine_obj.wait_ge(handles[s], v)
                last[s] = v
            instr = fn(engine_obj)
            if inc is not None:
                s, k = inc
                assert instr is not None
                instr.then_inc(handles[s], k)


def _replay_all(prog, block, handles):
    @block.sync
    def _(e):
        prog.replay(e, "sync", handles)

    @block.scalar
    def _(e):
        prog.replay(e, "scalar", handles)

    @block.vector
    def _(e):
        prog.replay(e, "vector", handles)

    @block.tensor
    def _(e):
        prog.replay(e, "tensor", handles)

    @block.gpsimd
    def _(e):
        prog.replay(e, "gpsimd", handles)


SEMS = ("ld", "pe", "act", "dve", "gpc", "st", "swx",
        "ccx0", "ccx1", "ccx2", "ccx3", "ccy0", "ccy1", "ccy2", "ccy3")


def _mk_sems(nc, st_):
    return {s: st_.enter_context(nc.semaphore(s)) for s in SEMS}



def emit_l1(nc, p, sem, io):
    u_fm = io["u_fm"]; wdt_t = io["wdt_t"]; wb_t = io["wb_t"]
    wc_t = io["wc_t"]; bdt_t = io["bdt_t"]
    st_xln = io["st_xln"]; st_lns = io["st_lns"]; st_bc = io["st_bc"]

    C1 = 512
    ntile = B * (TL // C1)

    with contextlib.ExitStack() as stk:

        def sb(name, shape, dt=F32):
            return stk.enter_context(nc.sbuf_tensor("L1" + name, shape, dt))

        wdt_b = sb("wdt_b", [128, NCH * D], BF16)
        wbc_f = sb("wbc_f", [128, 2 * NCH * NS])
        wbc_b = sb("wbc_b", [128, 2 * NCH * NS], BF16)
        nb = sb("nb", [128, NCH])
        onesP = sb("onesP", [128, 1])
        ones1 = sb("ones1", [1, 128])
        ubuf_d = [sb(f"ubuf{i}", [128, NCH * C1]) for i in range(2)]
        usq_d = [sb(f"usq{i}", [128, NCH * C1]) for i in range(2)]
        xln_d = [sb(f"xln{i}", [128, NCH * C1]) for i in range(2)]
        xln_b_d = [sb(f"xln_b{i}", [128, NCH * C1], BF16) for i in range(2)]
        lnS_d = [sb(f"lnS{i}", [128, NCH * C1]) for i in range(2)]
        stile = sb("stile", [128, C1])
        m_row = sb("m_row", [1, C1])
        msq_row = sb("msq_row", [1, C1])
        var_row = sb("var_row", [1, C1])
        rstd_row = sb("rstd_row", [1, C1])
        mr_row = sb("mr_row", [1, C1])
        brow_d = [sb(f"brow{i}", [NS, C1]) for i in range(2)]
        crow_d = [sb(f"crow{i}", [NS, C1]) for i in range(2)]
        cst = sb("cst", [128, 2])

        ps_sx = stk.enter_context(nc.psum_tensor("L1ps_sx", [1, C1], F32))
        ps_sx2 = stk.enter_context(nc.psum_tensor("L1ps_sx2", [1, C1], F32))
        ps_bc1 = stk.enter_context(nc.psum_tensor("L1ps_bc1", [128, C1], F32))
        ps_bc2 = stk.enter_context(nc.psum_tensor("L1ps_bc2", [128, C1], F32))
        ps_dt = stk.enter_context(nc.psum_tensor("L1ps_dt", [128, C1], F32))
        ps_B = stk.enter_context(nc.psum_tensor("L1ps_B", [NS, C1], F32))
        ps_C = stk.enter_context(nc.psum_tensor("L1ps_C", [NS, C1], F32))

        for h in range(2):
            for kc in range(4):
                p.op("sync", (lambda h=h, kc=kc: (lambda e: e.dma_start(
                    out=ubuf_d[0][:, kc * D:(kc + 1) * D],
                    in_=wdt_t[(4 * h + kc) * 128:(4 * h + kc + 1) * 128, :])))(),
                    waits=(p.dep("act") if (h == 1 and kc == 0) else ()),
                    inc=("ld", 16))
            p.op("scalar", (lambda h=h: (lambda e: e.copy(
                out=wdt_b[:, h * 4 * D:(h + 1) * 4 * D],
                in_=ubuf_d[0][:, 0:4 * D])))(),
                waits=p.dep("ld"), inc=("act", 1))
        for kc in range(NCH):
            p.op("sync", (lambda kc=kc: (lambda e: e.dma_start(
                out=wbc_f[:, kc * NS:(kc + 1) * NS],
                in_=wb_t[kc * 128:(kc + 1) * 128, :])))(),
                inc=("ld", 16))
            p.op("sync", (lambda kc=kc: (lambda e: e.dma_start(
                out=wbc_f[:, (NCH + kc) * NS:(NCH + kc + 1) * NS],
                in_=wc_t[kc * 128:(kc + 1) * 128, :])))(),
                inc=("ld", 16))
        p.op("sync", lambda e: e.dma_start(out=nb[:], in_=bdt_t), inc=("ld", 16))
        p.op("scalar", lambda e: e.copy(out=wbc_b[:], in_=wbc_f[:]),
             waits=p.dep("ld"), inc=("act", 1))
        p.op("vector", lambda e: e.tensor_scalar_mul(out=nb[:], in0=nb[:], scalar1=-1.0),
             waits=p.dep("ld"), inc=("dve", 1))
        p.op("vector", lambda e: e.memset(onesP[:], 1.0), inc=("dve", 1))
        p.op("vector", lambda e: e.memset(ones1[:], 1.0), inc=("dve", 1))
        p.op("vector", lambda e: e.memset(cst[:, 0:1], 1e-5), inc=("dve", 1))
        p.op("vector", lambda e: e.memset(cst[:, 0:1], 1e-5), inc=("dve", 1))
        p.op("vector", lambda e: e.memset(cst[:, 1:2], 1e-38), inc=("dve", 1))

        mark_sx = []

        def tile_body(it, b, cs, ubuf, usq, xln, xln_b, lnS, brow, crow, op):
                for c in range(NCH):
                    op("sync", (lambda b=b, c=c, cs=cs: (lambda e: e.dma_start(
                        out=ubuf[:, c * C1:(c + 1) * C1],
                        in_=u_fm[b, c * 128:(c + 1) * 128, cs:cs + C1])))(),
                        waits=([("pe", mark_sx[it - 1])] if (it >= 1 and c == 0)
                               else ()),
                        inc=("ld", 16))
                for c in range(NCH):
                    op("scalar", (lambda c=c: (lambda e: e.activation(
                        out=usq[:, c * C1:(c + 1) * C1],
                        in_=ubuf[:, c * C1:(c + 1) * C1], func=AF.Square)))(),
                        waits=p.dep("ld"), inc=("act", 1))
                for c in range(NCH):
                    op("tensor", (lambda c=c: (lambda e: e.matmul(
                        ps_sx[:, :], onesP[:], ubuf[:, c * C1:(c + 1) * C1],
                        start=(c == 0), stop=(c == NCH - 1))))(),
                        waits=p.dep("ld", "act"),
                        inc=("pe", 1) if c == NCH - 1 else None)
                mark_sx.append(p.counts["pe"])
                for c in range(NCH):
                    op("tensor", (lambda c=c: (lambda e: e.matmul(
                        ps_sx2[:, :], onesP[:], usq[:, c * C1:(c + 1) * C1],
                        start=(c == 0), stop=(c == NCH - 1))))(),
                        waits=p.dep("act"),
                        inc=("pe", 1) if c == NCH - 1 else None)
                op("scalar", lambda e: e.activation(out=m_row[:], in_=ps_sx[:],
                                                    func=AF.Copy, scale=1.0 / D),
                   waits=p.dep("pe"), inc=("act", 1))
                op("scalar", lambda e: e.activation(out=msq_row[:], in_=ps_sx[:],
                                                    func=AF.Square, scale=1.0 / D),
                   waits=p.dep("pe"), inc=("act", 1))
                op("vector", lambda e: e.scalar_tensor_tensor(
                    out=var_row[:], in0=ps_sx2[:], scalar=1.0 / D, in1=msq_row[:],
                    op0=ALU.mult, op1=ALU.subtract),
                   waits=p.dep("pe", "act"), inc=("dve", 1))
                op("scalar", lambda e: e.activation(out=rstd_row[:], in_=var_row[:],
                                                    func=AF.Sqrt, bias=cst[0:1, 0:1]),
                   waits=p.dep("dve"), inc=("act", 1))
                op("vector", lambda e: e.reciprocal(out=rstd_row[:], in_=rstd_row[:]),
                   waits=p.dep("act"), inc=("dve", 1))
                op("vector", lambda e: e.tensor_mul(out=mr_row[:], in0=m_row[:], in1=rstd_row[:]),
                   waits=p.dep("act", "dve"), inc=("dve", 1), self_wait=True)
                op("tensor", lambda e: e.matmul(ps_bc1[:, :], ones1[:], rstd_row[:],
                                                start=True, stop=True),
                   waits=p.dep("act", "dve"), inc=("pe", 1))
                op("tensor", lambda e: e.matmul(ps_bc2[:, :], ones1[:], mr_row[:],
                                                start=True, stop=True),
                   waits=p.dep("dve"), inc=("pe", 1))
                for c in range(NCH):
                    op("vector", (lambda c=c: (lambda e: e.tensor_mul(
                        out=xln[:, c * C1:(c + 1) * C1],
                        in0=ubuf[:, c * C1:(c + 1) * C1], in1=ps_bc1[:, :])))(),
                        waits=p.dep("pe"), inc=("dve", 1))
                    op("vector", (lambda c=c: (lambda e: e.tensor_sub(
                        out=xln[:, c * C1:(c + 1) * C1],
                        in0=xln[:, c * C1:(c + 1) * C1], in1=ps_bc2[:, :])))(),
                        inc=("dve", 1))
                    op("scalar", (lambda c=c: (lambda e: e.copy(
                        out=xln_b[:, c * C1:(c + 1) * C1],
                        in_=xln[:, c * C1:(c + 1) * C1])))(),
                        waits=p.dep("dve"), inc=("act", 1))
                    op("gpsimd", (lambda b=b, c=c, cs=cs: (lambda e: e.dma_start(
                        out=st_xln[b][c, :, cs:cs + C1],
                        in_=xln[:, c * C1:(c + 1) * C1])))(),
                        waits=p.dep("dve"), inc=("st", 16))
                for ec in range(NCH):
                    for kc in range(NCH):
                        op("tensor", (lambda ec=ec, kc=kc: (lambda e: e.matmul(
                            ps_dt[:, :],
                            wdt_b[:, kc * D + ec * 128: kc * D + (ec + 1) * 128],
                            xln_b[:, kc * C1:(kc + 1) * C1],
                            start=(kc == 0), stop=(kc == NCH - 1))))(),
                            waits=p.dep("act"),
                            inc=("pe", 1) if kc == NCH - 1 else None)
                    op("scalar", (lambda ec=ec: (lambda e: e.activation(
                        out=stile[:], in_=ps_dt[:], func=AF.Sigmoid,
                        bias=nb[:, ec:ec + 1], scale=-1.0)))(),
                        waits=p.dep("pe"), inc=("act", 1))
                    op("scalar", (lambda ec=ec: (lambda e: e.activation(
                        out=lnS[:, ec * C1:(ec + 1) * C1], in_=stile[:],
                        func=AF.Ln, bias=cst[:, 1:2])))(),
                        inc=("act", 1))
                    op("gpsimd", (lambda b=b, ec=ec, cs=cs: (lambda e: e.dma_start(
                        out=st_lns[b][ec, :, cs:cs + C1],
                        in_=lnS[:, ec * C1:(ec + 1) * C1])))(),
                        waits=p.dep("act"), inc=("st", 16))
                for kc in range(NCH):
                    op("tensor", (lambda kc=kc: (lambda e: e.matmul(
                        ps_B[:, :], wbc_b[:, kc * NS:(kc + 1) * NS],
                        xln_b[:, kc * C1:(kc + 1) * C1],
                        start=(kc == 0), stop=(kc == NCH - 1))))(),
                        waits=p.dep("act"),
                        inc=("pe", 1) if kc == NCH - 1 else None)
                for kc in range(NCH):
                    op("tensor", (lambda kc=kc: (lambda e: e.matmul(
                        ps_C[:, :], wbc_b[:, (NCH + kc) * NS:(NCH + kc + 1) * NS],
                        xln_b[:, kc * C1:(kc + 1) * C1],
                        start=(kc == 0), stop=(kc == NCH - 1))))(),
                        inc=("pe", 1) if kc == NCH - 1 else None)
                op("scalar", lambda e: e.copy(out=brow[:], in_=ps_B[:]),
                   waits=p.dep("pe"), inc=("act", 1))
                op("scalar", lambda e: e.copy(out=crow[:], in_=ps_C[:]),
                   inc=("act", 1))
                op("gpsimd", (lambda b=b, cs=cs: (lambda e: e.dma_start(
                    out=st_bc[b][0:NS, cs:cs + C1], in_=brow[:])))(),
                    waits=p.dep("act"), inc=("st", 16))
                op("gpsimd", (lambda b=b, cs=cs: (lambda e: e.dma_start(
                    out=st_bc[b][NS:2 * NS, cs:cs + C1], in_=crow[:])))(),
                    inc=("st", 16))
                if hf == TL // C1 - 1 and not SKIP_CC:
                    for j, (nm_in, nm_out) in enumerate((("st_xln", "st_xln_G"),
                                                         ("st_lns", "st_lns_G"),
                                                         ("st_bc", "st_bc_g"))):
                        cw = [("ccx%d" % b, j)] if j else (
                            [("ccx%d" % (b - 1), 3)] if b else [])
                        op("gpsimd", (lambda b=b, nm_in=nm_in, nm_out=nm_out:
                            (lambda e: e.collective_compute(
                                "AllGather", ALU.bypass,
                                replica_groups=[list(range(NCORES))],
                                ins=[io[nm_in][b].opt()],
                                outs=[io[nm_out][b].opt()])))(),
                            waits=p.dep("st") + cw, inc=("ccx%d" % b, 1),
                            self_wait=True)


        snaps = []
        for it in range(ntile):
            b, hf = divmod(it, TL // C1)
            cs = hf * C1
            snaps.append(p.snap())
            if it >= 1:
                start = snaps[it - 1]
            else:
                start = {"act": snaps[0]["act"]}
            first = {e: True for e in ENGS}

            def op(eng, fn, waits=(), inc=None, self_wait=False,
                   _first=first, _start=start):
                w = list(waits)
                if _first[eng]:
                    w += [(s, v) for s, v in _start.items()
                          if not s.startswith("cc")]
                    _first[eng] = False
                p.op(eng, fn, w, inc, self_wait=self_wait)

            tile_body(it, b, cs, ubuf_d[it % 2], usq_d[it % 2], xln_d[it % 2],
                      xln_b_d[it % 2], lnS_d[it % 2], brow_d[it % 2],
                      crow_d[it % 2], op)

        with nc.Block() as block:
            _replay_all(p, block, sem)



def emit_l2(nc, p, sem, io):
    logA_d = io["logA_d"]; Dp_d = io["Dp_d"]
    rowsel = io["rowsel"]
    st_xln_g = io["st_xln_loc"]; st_lns_g = io["st_lns_loc"]; st_bc_g = io["st_bc_g"]
    st_y = io["st_y"]

    TC = 1024
    with contextlib.ExitStack() as stk:

        def sb(name, shape, dt=F32):
            return stk.enter_context(nc.sbuf_tensor("L2" + name, shape, dt))

        Ut_b = [sb(f"Ut{i}", [128, TC]) for i in range(2)]
        Lt_b = [sb(f"Lt{i}", [128, TC]) for i in range(2)]
        dut_b = [sb(f"dut{i}", [128, TC]) for i in range(2)]
        bc_scr = sb("bc_scr", [NS, TG])
        Bhi = sb("Bhi", [NS, TG], BF16)
        Blo = sb("Blo", [NS, TG], BF16)
        Chi = sb("Chi", [NS, TG], BF16)
        Clo = sb("Clo", [NS, TG], BF16)
        logA_sb = sb("logA_sb", [128, NS])
        scaleP = sb("scaleP", [128, NS])
        Dp_sb = sb("Dp_sb", [128, 1])
        rs_sb = sb("rs_sb", [NS, NS * 128])
        rs_bf = sb("rs_bf", [NS, NS * 128], BF16)
        da_b = [sb(f"da{i}", [128, TC]) for i in range(2)]
        dbu = sb("dbu", [128, TC])
        xbuf = sb("xbuf", [128, TC])
        zsheet = sb("zsheet", [128, TC * NS], BF16)
        state = sb("state", [128, B * NS])
        ysum = sb("ysum", [128, TC])
        yout_b = [sb(f"yout{i}", [128, TC]) for i in range(2)]

        ps_Bbc = [stk.enter_context(nc.psum_tensor(f"L2ps_Bbc{i}", [128, TC], F32))
                  for i in range(2)]
        ps_Cbc = [stk.enter_context(nc.psum_tensor(f"L2ps_Cbc{i}", [128, TC], F32))
                  for i in range(2)]

        sect = p.snap()
        p.op("sync", lambda e: e.dma_start(out=logA_sb[:], in_=logA_d),
             waits=[(s, v) for s, v in sect.items()], inc=("ld", 16))
        p.op("sync", lambda e: e.dma_start(out=Dp_sb[:], in_=Dp_d), inc=("ld", 16))
        p.op("sync", lambda e: e.dma_start(out=rs_sb[:], in_=rowsel), inc=("ld", 16))
        p.op("scalar", lambda e: e.activation(out=scaleP[:], in_=logA_sb[:], func=AF.Exp),
             waits=p.dep("ld"), inc=("act", 1))
        p.op("scalar", lambda e: e.copy(out=rs_bf[:], in_=rs_sb[:]),
             waits=p.dep("ld"), inc=("act", 1))

        for b in range(B):
            start = p.snap()
            first = {e: True for e in ENGS}
            swx_need = io["sw_mark_x"][b]

            def op(eng, fn, waits=(), inc=None, self_wait=False, _b=b):
                w = list(waits)
                if first[eng]:
                    w += [(s, v) for s, v in start.items()
                          if s not in ("gpc", "swx") and not s.startswith("cc")]
                    w += [("ccx%d" % _b, 3), ("swx", swx_need)]
                    self_wait = True
                    first[eng] = False
                p.op(eng, fn, w, inc, self_wait=self_wait)

            for sc_ in range(NCORES):
                op("sync", (lambda b=b, sc_=sc_: (lambda e: e.dma_start(
                    out=bc_scr[:, sc_ * TL:(sc_ + 1) * TL],
                    in_=st_bc_g[b][sc_, 0:NS])))(), inc=("ld", 16))
            op("scalar", lambda e: e.copy(out=Bhi[:], in_=bc_scr[:]),
               waits=p.dep("ld"), inc=("act", 1))
            op("vector", lambda e: e.tensor_sub(out=Blo[:], in0=bc_scr[:], in1=Bhi[:]),
               waits=p.dep("act"), inc=("dve", 1))
            for sc_ in range(NCORES):
                op("sync", (lambda b=b, sc_=sc_: (lambda e: e.dma_start(
                    out=bc_scr[:, sc_ * TL:(sc_ + 1) * TL],
                    in_=st_bc_g[b][sc_, NS:2 * NS])))(),
                    waits=p.dep("dve"), inc=("ld", 16))
            op("scalar", lambda e: e.copy(out=Chi[:], in_=bc_scr[:]),
               waits=p.dep("ld"), inc=("act", 1))
            op("vector", lambda e: e.tensor_sub(out=Clo[:], in0=bc_scr[:], in1=Chi[:]),
               waits=p.dep("act"), inc=("dve", 1))
            bc_ready_dve = p.counts["dve"]
            bc_ready_act = p.counts["act"]

            mark_tts = {}
            mark_z = {}
            mark_tile = {}
            mark_act_tile = {}
            mark_st = {}
            mark_dut = {}
            mark_dA0 = {}
            for ti in range(TG // TC):
                cs = ti * TC
                ub = Ut_b[ti % 2]
                lb = Lt_b[ti % 2]
                mt = mark_tile.get(ti - 2, (0, 0))
                op("sync", (lambda b=b, ti=ti, ub=ub: (lambda e: e.dma_start(
                    out=ub[:], in_=st_xln_g[b, ti])))(),
                    waits=[("dve", max(mt[0], mark_dut.get(ti - 1, 0))),
                           ("gpc", mt[1]),
                           ("act", max(mark_act_tile.get(ti - 2, 0),
                                       mark_dA0.get(ti - 1, 0)))],
                    inc=("ld", 16))
                op("sync", (lambda b=b, ti=ti, lb=lb: (lambda e: e.dma_start(
                    out=lb[:], in_=st_lns_g[b, ti])))(), inc=("ld", 16))
                op("vector", (lambda ub=ub, lb=lb, ti=ti: (lambda e: e.scalar_tensor_tensor(
                    out=dut_b[ti % 2][:], in0=lb[:], scalar=-1.0,
                    in1=ub[:], op0=ALU.mult, op1=ALU.mult)))(),
                    waits=p.dep("ld"), inc=("dve", 1))
                mark_dut[ti] = p.counts["dve"]
                ld_tile = p.counts["ld"]
                for n in range(NS):
                    sidx = b * NS + n
                    psB = ps_Bbc[n % 2]
                    psC = ps_Cbc[n % 2]
                    dab = da_b[n % 2]
                    bank_free = max(
                        mark_z.get((ti, n - 2), mark_z.get((ti - 1, NS + n - 2), 0)),
                        bc_ready_dve)
                    first_mm = True
                    for psX, hi, lo in ((psB, Bhi, Blo), (psC, Chi, Clo)):
                        for h2 in range(2):
                            for hl, srcm in ((0, hi), (1, lo)):
                                op("tensor", (lambda n=n, cs=cs, h2=h2, hl=hl,
                                              srcm=srcm, psX=psX: (lambda e: e.matmul(
                                    psX[:, h2 * 512:(h2 + 1) * 512],
                                    rs_bf[:, n * 128:(n + 1) * 128],
                                    srcm[0:NS, cs + h2 * 512: cs + (h2 + 1) * 512],
                                    start=(hl == 0), stop=(hl == 1))))(),
                                    waits=([("dve", bank_free),
                                            ("act", bc_ready_act)]
                                           if first_mm else ()),
                                    inc=("pe", 1) if (hl == 1) else None)
                                first_mm = False
                    op("scalar", (lambda n=n, dab=dab, lb=lb: (lambda e: e.activation(
                        out=dab[:], in_=lb[:], func=AF.Exp,
                        scale=scaleP[:, n:n + 1])))(),
                        waits=[("dve", mark_tts.get((ti, n - 2),
                                mark_tts.get((ti - 1, NS + n - 2), 0))),
                               ("ld", ld_tile)],
                        inc=("act", 1))
                    if n == 0:
                        mark_dA0[ti] = p.counts["act"]
                    if n == NS - 1:
                        mark_act_tile[ti] = p.counts["act"]
                    op("vector", (lambda ti=ti, psB=psB: (lambda e: e.tensor_mul(
                        out=dbu[:], in0=dut_b[ti % 2][:], in1=psB[:, :])))(),
                        waits=p.dep("pe", "act"), inc=("dve", 1))
                    if ti == 0:
                        op("vector", (lambda dab=dab: (lambda e: e.tensor_tensor_scan(
                            out=xbuf[:], data0=dab[:], data1=dbu[:], initial=0.0,
                            op0=ALU.mult, op1=ALU.add)))(),
                            inc=("dve", 1))
                    else:
                        op("vector", (lambda sidx=sidx, dab=dab:
                            (lambda e: e.tensor_tensor_scan(
                                out=xbuf[:], data0=dab[:], data1=dbu[:],
                                initial=state[:, sidx:sidx + 1],
                                op0=ALU.mult, op1=ALU.add)))(),
                            inc=("dve", 1))
                    mark_tts[(ti, n)] = p.counts["dve"]
                    op("scalar", (lambda sidx=sidx: (lambda e: e.copy(
                        out=state[:, sidx:sidx + 1], in_=xbuf[:, TC - 1:TC])))(),
                        waits=p.dep("dve"), inc=("act", 1))
                    op("vector", (lambda n=n, psC=psC: (lambda e: e.tensor_tensor(
                        out=zsheet[:, n * TC:(n + 1) * TC],
                        in0=xbuf[:], in1=psC[:, :], op=ALU.mult)))(),
                        inc=("dve", 1))
                    mark_z[(ti, n)] = p.counts["dve"]
                op("vector", (lambda: (lambda e: e.tensor_reduce(
                    out=ysum[:], in_=zsheet[:].rearrange("p (n t) -> p t n", n=NS),
                    axis=AX.X, op=ALU.add)))(),
                    inc=("dve", 1))
                op("vector", (lambda ub=ub, ti=ti: (lambda e: e.scalar_tensor_tensor(
                    out=yout_b[ti % 2][:], in0=ub[:], scalar=Dp_sb[:],
                    in1=ysum[:], op0=ALU.mult, op1=ALU.add)))(),
                    waits=[("st", mark_st.get(ti - 1, 0))],
                    inc=("dve", 1))
                mark_tile[ti] = (p.counts["dve"], p.counts["gpc"])
                op("gpsimd", (lambda b=b, ti=ti: (lambda e: e.dma_start(
                    out=st_y[b][ti], in_=yout_b[ti % 2][:])))(),
                    waits=p.dep("dve"), inc=("st", 16))
                mark_st[ti] = p.counts["st"]
                if ti == TG // TC - 1 and b == B - 1:
                    for b2 in range(B):
                        cw = ([("ccy%d" % (b2 - 1), 1)] if b2 else [("ccx3", 3)])
                        op("gpsimd", (lambda b2=b2: (lambda e: e.collective_compute(
                            "AllGather", ALU.bypass,
                            replica_groups=[list(range(NCORES))],
                            ins=[io["st_y"][b2].opt()],
                            outs=[io["st_y_G"][b2].opt()])))(),
                            waits=p.dep("st") + cw, inc=("ccy%d" % b2, 1),
                            self_wait=True)

        with nc.allow_low_precision("z-sheet / y-contraction are bf16 matmul-class operands; reduce accumulates fp32 internally"):
            with nc.Block() as block:
                _replay_all(p, block, sem)


def emit_l3(nc, p, sem, io):
    u3 = io["u3"]; wq_t = io["wq_t"]; wk_t = io["wk_t"]; wv_t = io["wv_t"]
    wo_t = io["wo_t"]; bo_t = io["bo_t"]; krl = io["krl"]; krr = io["krr"]
    bias_ri = io["bias_ri"]; sel_c = io["sel_c"]; rk_in = io["rk_in"]
    exp_in = io["exp_in"]; rv_in = io["rv_in"]
    st_y_g = io["st_y_loc"]; out_h = io["out_h"]

    CT = 512
    NT = TL * B // CT

    with contextlib.ExitStack() as stk:

        def sb(name, shape, dt=F32):
            return stk.enter_context(nc.sbuf_tensor("L3" + name, shape, dt))

        w_bf = {}
        for nm in ("wq", "wk", "wv", "wo"):
            w_bf[nm] = sb(nm + "_bf", [128, NCH * D], BF16)
        wrt_bf = sb("wrt_bf", [128, 4 * 512], BF16)
        wit_bf = sb("wit_bf", [128, 4 * 512], BF16)
        witn_bf = sb("witn_bf", [128, 4 * 512], BF16)
        bo_sb = sb("bo_sb", [128, NCH])
        bri_sb = sb("bri_sb", [128, NCH])
        sel_bf = sb("sel_bf", [128, NCH * H], BF16)
        rk_bf = sb("rk_bf", [128, B * B * NCH * H], BF16)
        exp_bf = sb("exp_bf", [H, NCH * 128], BF16)
        attn_bf = sb("attn_bf", [H, B * CT], BF16)
        rv_sb = sb("rv_sb", [128, H])
        onesP = sb("onesP", [128, 1])
        ones1 = sb("ones1", [1, 128])
        cst = sb("cst", [128, 2])
        zrow = sb("zrow", [1, CT])

        xA_d = [sb("xA%d" % i, [128, NCH * CT]) for i in range(2)]
        xA = xA_d[0]
        xB = sb("xB", [128, NCH * CT])
        ybuf4 = sb("ybuf4", [128, (NCH // 2) * CT])
        xln_b = sb("xln_b", [128, NCH * CT], BF16)
        tmp = sb("tmp", [128, CT])
        qtile = sb("qtile", [128, CT], BF16)
        ktile = sb("ktile", [128, CT], BF16)
        vf = sb("vf", [128, NCH * CT])
        usq = vf
        zbuf = sb("zbuf", [128, CT], BF16)
        scbuf = sb("scbuf", [H, B * CT])
        rrow = sb("rrow", [1, CT])
        vrv = sb("vrv", [128, CT])
        vz = sb("vz", [128, CT])
        aout_b = sb("aout_b", [128, NCH * CT], BF16)
        m_row = sb("m_row", [1, CT])
        msq_row = sb("msq_row", [1, CT])
        var_row = sb("var_row", [1, CT])
        rstd_row = sb("rstd_row", [1, CT])
        mr_row = sb("mr_row", [1, CT])

        ps_bc1 = stk.enter_context(nc.psum_tensor("L3ps_bc1", [128, CT], F32))
        ps_bc2 = stk.enter_context(nc.psum_tensor("L3ps_bc2", [128, CT], F32))
        ps_mm = stk.enter_context(nc.psum_tensor("L3ps_mm", [128, CT], F32))
        ps_sc_all = [stk.enter_context(nc.psum_tensor(f"ps_sc{s}", [128, CT], F32))
                     for s in range(4)]
        ps_st = stk.enter_context(nc.psum_tensor("L3ps_st", [128, CT], F32))

        sect = p.snap()
        sect_first = [True]

        def ld(dst, src):
            w = []
            if sect_first[0]:
                w = [(s, v) for s, v in sect.items()]
                sect_first[0] = False
            p.op("sync", (lambda dst=dst, src=src: (lambda e: e.dma_start(
                out=dst, in_=src)))(), waits=w, inc=("ld", 16))

        ld(bo_sb[:], bo_t)
        ld(bri_sb[:], bias_ri)
        ld(xA[:, 0:NCH * H], sel_c)
        ld(xA[:, 128:128 + B * B * NCH * H], rk_in)
        ld(xA[0:H, 2400:2400 + NCH * 128], exp_in)
        ld(rv_sb[:], rv_in)
        p.op("scalar", lambda e: e.copy(out=sel_bf[:], in_=xA[:, 0:NCH * H]),
             waits=p.dep("ld"), inc=("act", 1))
        p.op("scalar", lambda e: e.copy(out=rk_bf[:], in_=xA[:, 128:128 + B * B * NCH * H]),
             waits=p.dep("ld"), inc=("act", 1))
        p.op("scalar", lambda e: e.copy(out=exp_bf[:], in_=xA[0:H, 2400:2400 + NCH * 128]),
             waits=p.dep("ld"), inc=("act", 1))
        p.op("vector", lambda e: e.memset(onesP[:], 1.0), inc=("dve", 1))
        p.op("vector", lambda e: e.memset(ones1[:], 1.0), inc=("dve", 1))
        p.op("vector", lambda e: e.memset(cst[:, 0:1], 1e-5), inc=("dve", 1))
        p.op("vector", lambda e: e.memset(zrow[:], 0.0), inc=("dve", 1))

        for ri in range(2):
            dstf = wrt_bf if ri == 0 else wit_bf
            for c in range(4):
                for pc in range(4):
                    p.op("sync", (lambda ri=ri, c=c, pc=pc: (lambda e: e.dma_start(
                        out=xA[:, 0:512],
                        in_=krr[ri, c, pc * 128:(pc + 1) * 128, :])))(),
                        waits=p.dep("pe", "act"), inc=("ld", 16))
                    p.op("sync", (lambda ri=ri, c=c, pc=pc: (lambda e: e.dma_start(
                        out=xA[:, 512:1024],
                        in_=krl[ri, c, pc * 128:(pc + 1) * 128, :])))(),
                        inc=("ld", 16))
                    for qc in range(4):
                        p.op("tensor", (lambda qc=qc, c=c, pc=pc: (lambda e: e.matmul(
                            ps_sc_all[qc][:, :],
                            xA[:, 512 + qc * 128: 512 + (qc + 1) * 128],
                            xA[:, 0:512],
                            start=(c == 0 and pc == 0),
                            stop=(c == 3 and pc == 3))))(),
                            waits=p.dep("ld", "act"),
                            inc=("pe", 1) if qc == 3 else None)
            for qc in range(4):
                p.op("scalar", (lambda qc=qc, dstf=dstf: (lambda e: e.copy(
                    out=dstf[:, qc * 512:(qc + 1) * 512],
                    in_=ps_sc_all[qc][:, :])))(),
                    waits=p.dep("pe"), inc=("act", 1))
                if ri == 1:
                    p.op("scalar", (lambda qc=qc: (lambda e: e.activation(
                        out=witn_bf[:, qc * 512:(qc + 1) * 512],
                        in_=ps_sc_all[qc][:, :], func=AF.Copy, scale=-1.0)))(),
                        inc=("act", 1))
        for nm, srcw in (("wq", wq_t), ("wk", wk_t), ("wv", wv_t), ("wo", wo_t)):
            for kc in range(NCH):
                p.op("sync", (lambda kc=kc, srcw=srcw: (lambda e: e.dma_start(
                    out=xA[:, (kc % 4) * 1024:(kc % 4 + 1) * 1024],
                    in_=srcw[kc * 128:(kc + 1) * 128, :])))(),
                    waits=p.dep("act"), inc=("ld", 16))
                p.op("scalar", (lambda nm=nm, kc=kc: (lambda e: e.copy(
                    out=w_bf[nm][:, kc * D:(kc + 1) * D],
                    in_=xA[:, (kc % 4) * 1024:(kc % 4 + 1) * 1024])))(),
                    waits=p.dep("ld"), inc=("act", 1))

        def ln_block(op, xsrc, dep_sem):
            for c in range(NCH):
                op("scalar", (lambda c=c: (lambda e: e.activation(
                    out=usq[:, c * CT:(c + 1) * CT],
                    in_=xsrc[:, c * CT:(c + 1) * CT], func=AF.Square)))(),
                    waits=p.dep(dep_sem), inc=("act", 1))
            for c in range(NCH):
                op("tensor", (lambda c=c: (lambda e: e.matmul(
                    ps_st[0:1, :], onesP[:], xsrc[:, c * CT:(c + 1) * CT],
                    start=(c == 0), stop=(c == NCH - 1))))(),
                    waits=p.dep(dep_sem),
                    inc=("pe", 1) if c == NCH - 1 else None)
            for c in range(NCH):
                op("tensor", (lambda c=c: (lambda e: e.matmul(
                    ps_st[32:33, :], onesP[:], usq[:, c * CT:(c + 1) * CT],
                    start=(c == 0), stop=(c == NCH - 1))))(),
                    waits=p.dep("act"),
                    inc=("pe", 1) if c == NCH - 1 else None)
            op("scalar", lambda e: e.activation(out=m_row[:], in_=ps_st[0:1, :],
                                                func=AF.Copy, scale=1.0 / D),
               waits=p.dep("pe"), inc=("act", 1))
            op("scalar", lambda e: e.activation(out=msq_row[:], in_=ps_st[0:1, :],
                                                func=AF.Square, scale=1.0 / D),
               waits=p.dep("pe"), inc=("act", 1))
            op("vector", lambda e: e.scalar_tensor_tensor(
                out=var_row[:], in0=ps_st[32:33, :], scalar=1.0 / D, in1=msq_row[:],
                op0=ALU.mult, op1=ALU.subtract),
               waits=p.dep("pe", "act"), inc=("dve", 1))
            op("scalar", lambda e: e.activation(out=rstd_row[:], in_=var_row[:],
                                                func=AF.Sqrt, bias=cst[0:1, 0:1]),
               waits=p.dep("dve"), inc=("act", 1))
            op("vector", lambda e: e.reciprocal(out=rstd_row[:], in_=rstd_row[:]),
               waits=p.dep("act"), inc=("dve", 1))
            op("vector", lambda e: e.tensor_mul(out=mr_row[:], in0=m_row[:],
                                                in1=rstd_row[:]),
               waits=p.dep("act", "dve"), inc=("dve", 1), self_wait=True)
            op("tensor", lambda e: e.matmul(ps_bc1[:, :], ones1[:], rstd_row[:],
                                            start=True, stop=True),
               waits=p.dep("act", "dve"), inc=("pe", 1))
            op("tensor", lambda e: e.matmul(ps_bc2[:, :], ones1[:], mr_row[:],
                                            start=True, stop=True),
               waits=p.dep("dve"), inc=("pe", 1))
            for c in range(NCH):
                op("vector", (lambda c=c: (lambda e: e.tensor_mul(
                    out=tmp[:], in0=xsrc[:, c * CT:(c + 1) * CT], in1=ps_bc1[:, :])))(),
                    waits=p.dep("pe"), inc=("dve", 1))
                op("vector", (lambda c=c: (lambda e: e.tensor_tensor(
                    out=xln_b[:, c * CT:(c + 1) * CT], in0=tmp[:], in1=ps_bc2[:, :],
                    op=ALU.subtract)))(),
                    inc=("dve", 1))

        def ffn(op, xres, xdst, extra_w=()):
            for oc in range(NCH):
                half = oc // 4
                o128 = (oc % 4) * 128
                wa, wb2 = (wrt_bf, witn_bf) if half == 0 else (wit_bf, wrt_bf)
                for qc in range(4):
                    op("tensor", (lambda wa=wa, qc=qc, o128=o128: (lambda e: e.matmul(
                        ps_mm[:, :], wa[:, qc * 512 + o128: qc * 512 + o128 + 128],
                        xln_b[:, qc * CT:(qc + 1) * CT],
                        start=(qc == 0), stop=False)))(),
                        waits=p.dep("dve", "act") if qc == 0 else ())
                for qc in range(4):
                    op("tensor", (lambda wb2=wb2, qc=qc, o128=o128: (lambda e: e.matmul(
                        ps_mm[:, :], wb2[:, qc * 512 + o128: qc * 512 + o128 + 128],
                        xln_b[:, (4 + qc) * CT:(4 + qc + 1) * CT],
                        start=False, stop=(qc == 3))))(),
                        inc=("pe", 1) if qc == 3 else None)
                op("vector", (lambda oc=oc: (lambda e: e.scalar_tensor_tensor(
                    out=xdst[:, oc * CT:(oc + 1) * CT], in0=ps_mm[:, :],
                    scalar=bri_sb[:, oc:oc + 1],
                    in1=xres[:, oc * CT:(oc + 1) * CT],
                    op0=ALU.add, op1=ALU.add)))(),
                    waits=p.dep("pe") + (list(extra_w) if oc == 0 else []),
                    inc=("dve", 1))

        def proj_chunk(op, wname, dst, oc, rhs_bf, evac_eng="scalar"):
            for kc in range(NCH):
                op("tensor", (lambda wname=wname, oc=oc, kc=kc: (lambda e: e.matmul(
                    ps_mm[:, :],
                    w_bf[wname][:, kc * D + oc * 128: kc * D + (oc + 1) * 128],
                    rhs_bf[:, kc * CT:(kc + 1) * CT],
                    start=(kc == 0), stop=(kc == NCH - 1))))(),
                    waits=p.dep("dve", "act") if kc == 0 else (),
                    inc=("pe", 1) if kc == NCH - 1 else None)
            op(evac_eng, (lambda dst=dst: (lambda e: e.copy(
                out=dst, in_=ps_mm[:, :])))(),
                waits=p.dep("pe"), inc=("act", 1))

        with nc.allow_low_precision("setup flush"):
            with nc.Block() as block:
                _replay_all(p, block, sem)
        for arm in nc.switch_core_id(engines=[nc.sync], n=128):
            me = arm.logical % NCORES
            for b in range(B):
                nc.sync.wait_ge(sem["ccy%d" % b], 1)
                nc.sync.wait_ge(sem["swx"], 32 * B + 16 * b)
                nc.sync.dma_start(out=io["st_y_loc"][b],
                                  in_=io["st_y_G"][b][:, me]).then_inc(sem["swx"], 16)
        p.counts["swx"] = 32 * B + 16 * B

        mark_x1 = []
        mark_store = []

        def tile_body(it, cs, xA, op):
            u3v = u3.rearrange("d t b -> d (t b)")
            outv = out_h.rearrange("d t b -> d (t b)")

            tau0 = it * 128
            swx_all = [("swx", 32 * B + 16 * B)] if it == 0 else []
            GC = NCH // 2
            for g in range(2):
                for c in range(g * GC, (g + 1) * GC):
                    if c == 0:
                        uw = [("dve", mark_x1[it - 1])] if it >= 1 else []
                    elif c == GC:
                        uw = p.dep("dve")
                    else:
                        uw = ()
                    op("sync", (lambda c=c, cs=cs: (lambda e: e.dma_start(
                        out=xA[:, c * CT:(c + 1) * CT],
                        in_=u3v[c * 128:(c + 1) * 128, cs:cs + CT])))(),
                        waits=uw, inc=("ld", 16))
                for c in range(g * GC, (g + 1) * GC):
                    for bb in range(B):
                        cg = c - g * GC
                        op("sync", (lambda c=c, cg=cg, bb=bb, tau0=tau0:
                            (lambda e: e.dma_start(
                                out=ybuf4[:, cg * CT + bb * 128:
                                          cg * CT + (bb + 1) * 128],
                                in_=st_y_g[bb, c, :, tau0:tau0 + 128])))(),
                            waits=swx_all if (c == 0 and bb == 0) else (),
                            inc=("ld", 16),
                            self_wait=(it == 0 and c == 0 and bb == 0))
                for c in range(g * GC, (g + 1) * GC):
                    cg = c - g * GC
                    op("vector", (lambda c=c, cg=cg: (lambda e: e.tensor_add(
                        out=xA[:, c * CT:(c + 1) * CT].rearrange(
                            "p (t b) -> p t b", b=B),
                        in0=xA[:, c * CT:(c + 1) * CT].rearrange(
                            "p (t b) -> p t b", b=B),
                        in1=ybuf4[:, cg * CT:(cg + 1) * CT].rearrange(
                            "p (b t) -> p t b", b=B))))(),
                        waits=p.dep("ld"), inc=("dve", 1))
            mark_x1.append(p.counts["dve"])

            ln_block(op, xA, "dve")
            ffn(op, xA, xB,
                extra_w=([("st", mark_store[it - 1])] if it >= 1 else []))

            ln_block(op, xB, "dve")

            for oc in range(NCH):
                proj_chunk(op, "wv", vf[:, oc * CT:(oc + 1) * CT], oc, xln_b)

            for c in range(NCH):
                proj_chunk(op, "wq", qtile[:, :], c, xln_b)
                proj_chunk(op, "wk", ktile[:, :], c, xln_b)
                for s in range(B):
                    qs = qtile[:].rearrange("p (t b) -> p t b", b=B)[:, :, s:s + 1]
                    op("vector", (lambda qs=qs: (lambda e: e.tensor_tensor(
                        out=zbuf[:].rearrange("p (t b) -> p t b", b=B),
                        in0=qs.broadcast_to([128, CT // B, B]),
                        in1=ktile[:].rearrange("p (t b) -> p t b", b=B),
                        op=ALU.mult)))(),
                        waits=p.dep("act", "pe"), inc=("dve", 1))
                    if c == 0:
                        op("tensor", (lambda s=s: (lambda e: e.matmul(
                            ps_sc_all[s][:, :], zrow[:, 0:128], zrow[:],
                            start=True, stop=False,
                            skip_group_check=True)))(),
                            waits=p.dep("dve"), inc=("pe", 1))
                    op("tensor", (lambda s=s, c=c: (lambda e: e.matmul(
                        ps_sc_all[s][0:16, :], sel_bf[:, c * H:(c + 1) * H],
                        zbuf[:],
                        start=False, stop=False,
                        skip_group_check=True)))(),
                        waits=p.dep("dve"), inc=("pe", 1))
                    for t in range(B):
                        rk_tile = rk_bf[:, ((s * B + t) * NCH + c) * H:
                                        ((s * B + t) * NCH + c + 1) * H]
                        qsv = qtile[:].rearrange("p (t b) -> p b t", b=B)[:, s, :]
                        op("tensor", (lambda t=t, s=s, c=c, rk_tile=rk_tile, qsv=qsv:
                            (lambda e: e.matmul(
                                ps_sc_all[s][32:48, t * 128:(t + 1) * 128],
                                rk_tile, qsv,
                                start=False,
                                stop=(c == NCH - 1 and t == B - 1),
                                skip_group_check=True)))(),
                            inc=("pe", 1) if (c == NCH - 1 or
                                              (s == B - 1 and t == B - 1))
                            else None)
            for s in range(B):
                op("scalar", (lambda s=s: (lambda e: e.copy(
                    out=scbuf[:, s * CT:(s + 1) * CT],
                    in_=ps_sc_all[s][0:16, :])))(),
                    waits=p.dep("pe"), inc=("act", 1))
                op("vector", (lambda s=s: (lambda e: e.tensor_add(
                    out=scbuf[:, s * CT:(s + 1) * CT].rearrange(
                        "h (t b) -> h t b", b=B),
                    in0=scbuf[:, s * CT:(s + 1) * CT].rearrange(
                        "h (t b) -> h t b", b=B),
                    in1=ps_sc_all[s][32:48, :].rearrange(
                        "h (t tau) -> h tau t", tau=128))))(),
                    waits=p.dep("pe", "act"), inc=("dve", 1))
                op("scalar", (lambda s=s: (lambda e: e.activation(
                    out=scbuf[:, s * CT:(s + 1) * CT],
                    in_=scbuf[:, s * CT:(s + 1) * CT],
                    func=AF.Exp, scale=0.125)))(),
                    waits=p.dep("dve"), inc=("act", 1))
                op("tensor", (lambda s=s: (lambda e: e.matmul(
                    ps_bc2[0:1, :], onesP[0:16, :],
                    scbuf[:, s * CT:(s + 1) * CT],
                    start=True, stop=True)))(),
                    waits=p.dep("act"), inc=("pe", 1))
                op("vector", lambda e: e.reciprocal(out=rrow[:], in_=ps_bc2[0:1, :]),
                   waits=p.dep("pe"), inc=("dve", 1))
                op("tensor", lambda e: e.matmul(
                    ps_bc2[32:48, :], ones1[:, 0:16], rrow[:],
                    start=True, stop=True),
                   waits=p.dep("dve"), inc=("pe", 1))
                op("vector", (lambda s=s: (lambda e: e.tensor_mul(
                    out=scbuf[:, s * CT:(s + 1) * CT],
                    in0=scbuf[:, s * CT:(s + 1) * CT],
                    in1=ps_bc2[32:48, :])))(),
                    waits=p.dep("pe"), inc=("dve", 1))
                op("scalar", (lambda s=s: (lambda e: e.copy(
                    out=attn_bf[:, s * CT:(s + 1) * CT],
                    in_=scbuf[:, s * CT:(s + 1) * CT])))(),
                    waits=p.dep("dve"), inc=("act", 1))
            for s in range(B):
                for c in range(NCH):
                    op("tensor", (lambda s=s, c=c: (lambda e: e.matmul(
                        ps_bc1[:, :], exp_bf[:, c * 128:(c + 1) * 128],
                        attn_bf[:, s * CT:(s + 1) * CT],
                        start=True, stop=True)))(),
                        waits=p.dep("dve", "act"), inc=("pe", 1))
                    op("vector", (lambda s=s, c=c: (lambda e: e.tensor_tensor(
                        out=vrv[:].rearrange("p (t b) -> p t b", b=B),
                        in0=vf[:, c * CT:(c + 1) * CT].rearrange(
                            "p (t b) -> p t b", b=B),
                        in1=rv_sb[:, None, s * B:(s + 1) * B].broadcast_to(
                            [128, CT // B, B]),
                        op=ALU.add)))(),
                        inc=("dve", 1))
                    op("vector", (lambda: (lambda e: e.tensor_mul(
                        out=vz[:], in0=ps_bc1[:, :], in1=vrv[:])))(),
                        waits=p.dep("pe"), inc=("dve", 1))
                    op("vector", (lambda s=s, c=c: (lambda e: e.tensor_reduce(
                        out=aout_b[:, c * CT:(c + 1) * CT].rearrange(
                            "p (t b) -> p b t", b=B)[:, s, :],
                        in_=vz[:].rearrange("p (t b) -> p t b", b=B),
                        axis=AX.X, op=ALU.add)))(),
                        inc=("dve", 1))
            for oc in range(NCH):
                for kc in range(NCH):
                    op("tensor", (lambda oc=oc, kc=kc: (lambda e: e.matmul(
                        ps_mm[:, :],
                        w_bf["wo"][:, kc * D + oc * 128: kc * D + (oc + 1) * 128],
                        aout_b[:, kc * CT:(kc + 1) * CT],
                        start=(kc == 0), stop=(kc == NCH - 1))))(),
                        waits=p.dep("dve", "act") if kc == 0 else (),
                        inc=("pe", 1) if kc == NCH - 1 else None)
                op("vector", (lambda oc=oc: (lambda e: e.scalar_tensor_tensor(
                    out=xA[:, oc * CT:(oc + 1) * CT], in0=ps_mm[:, :],
                    scalar=bo_sb[:, oc:oc + 1],
                    in1=xB[:, oc * CT:(oc + 1) * CT],
                    op0=ALU.add, op1=ALU.add)))(),
                    waits=p.dep("pe"), inc=("dve", 1))
            ln_block(op, xA, "dve")
            ffn(op, xA, xB)
            for c in range(NCH):
                op("gpsimd", (lambda c=c, cs=cs: (lambda e: e.dma_start(
                    out=outv[c * 128:(c + 1) * 128, cs:cs + CT],
                    in_=xB[:, c * CT:(c + 1) * CT])))(),
                    waits=p.dep("dve"), inc=("st", 16))
            mark_store.append(p.counts["st"])

        LAG = {"sync": 1, "scalar": 1, "vector": 1, "tensor": 1, "gpsimd": 1}
        FULLSEM = {"tensor": ("act",)}
        snaps3 = []
        for it in range(NT):
            snaps3.append(p.snap())
            start = {e: dict(snaps3[max(it - LAG[e], 0)]) for e in ENGS}
            for e, ss in FULLSEM.items():
                for s in ss:
                    start[e][s] = snaps3[it][s]
            first = {e: True for e in ENGS}

            def op(eng, fn, waits=(), inc=None, self_wait=False,
                   _first=first, _start=start):
                w = list(waits)
                if _first[eng]:
                    w += [(s, v) for s, v in _start[eng].items()]
                    _first[eng] = False
                p.op(eng, fn, w, inc, self_wait=self_wait)

            tile_body(it, it * CT, xA_d[it % 2], op)

        p.op("sync", lambda e: e.nop(), waits=p.dep("st"))
        with nc.allow_low_precision("DVE reduces in fp32 internally; bf16 is the matmul operand dtype"):
            with nc.Block() as block:
                _replay_all(p, block, sem)




def build_full(upto=3):
    nc = bass.Bass("TRN2", target_bir_lowering=False,
                   detect_race_conditions=False)
    io = {}
    ext_in = [
        ("u_fm", [B, D, TL]), ("wdt_t", [D, D]), ("wb_t", [D, NS]),
        ("wc_t", [D, NS]), ("bdt_t", [128, NCH]),
        ("logA_d", [128, NS]), ("Dp_d", [128, 1]), ("id128", [128, 128]),
        ("rowsel", [NS, NS * 128]),
        ("u3", [D, TL, B]), ("wq_t", [D, D]), ("wk_t", [D, D]),
        ("wv_t", [D, D]), ("wo_t", [D, D]), ("bo_t", [128, NCH]),
        ("krl", [2, 4, 512, 512]), ("krr", [2, 4, 512, 512]),
        ("bias_ri", [128, NCH]), ("sel_c", [128, NCH * H]),
        ("rk_in", [128, B * B * NCH * H]), ("exp_in", [H, NCH * 128]),
        ("rv_in", [128, H]),
    ]
    for nm, shp in ext_in:
        io[nm] = nc.dram_tensor(nm, shp, F32, kind="ExternalInput").ap()
    io["out_h"] = nc.dram_tensor("out_h", [D, TL, B], F32,
                                 kind="ExternalOutput").ap()
    for nm in ("st_xln", "st_lns", "st_y"):
        io[nm] = [nc.dram_tensor(f"{nm}{b}", [NCH, 128, TL], F32).ap()
                  for b in range(B)]
    for nm in ("st_xln_loc", "st_lns_loc", "st_y_loc"):
        io[nm] = nc.dram_tensor(nm, [B, NCH, 128, TL], F32).ap()
    for nm in ("st_xln_G", "st_lns_G", "st_y_G"):
        io[nm] = [nc.dram_tensor(f"{nm}{b}", [NCORES, NCH, 128, TL], F32).ap()
                  for b in range(B)]
    io["st_bc"] = [nc.dram_tensor(f"st_bc{b}", [2 * NS, TL], F32).ap()
                   for b in range(B)]
    io["st_bc_g"] = [nc.dram_tensor(f"st_bc_g{b}", [NCORES, 2 * NS, TL], F32).ap()
                     for b in range(B)]

    io["cc_mark_x"] = {}
    io["cc_mark_y"] = {}
    io["sw_mark_x"] = {}
    with contextlib.ExitStack() as stk:
        sem = _mk_sems(nc, stk)
        p = Prog(SEMS)
        emit_l1(nc, p, sem, io)
        for arm in nc.switch_core_id(engines=[nc.sync], n=128):
            me = arm.logical % NCORES
            for b in range(B):
                nc.sync.wait_ge(sem["ccx%d" % b], 3)
                if b:
                    nc.sync.wait_ge(sem["swx"], 32 * b)
                nc.sync.dma_start(out=io["st_xln_loc"][b],
                                  in_=io["st_xln_G"][b][:, me]).then_inc(sem["swx"], 16)
                nc.sync.dma_start(out=io["st_lns_loc"][b],
                                  in_=io["st_lns_G"][b][:, me]).then_inc(sem["swx"], 16)
        p.counts["swx"] = 32 * B
        p.owner["swx"] = "sync"
        for b in range(B):
            io["sw_mark_x"][b] = 32 * (b + 1)
        _section_barrier(p)
        if upto >= 2:
            emit_l2(nc, p, sem, io)
            _section_barrier(p)
        if upto >= 3:
            emit_l3(nc, p, sem, io)
        if upto < 3:
            with nc.Block() as block:
                _replay_all(p, block, sem)
    return nc


def _section_barrier(p):
    snap = p.snap()
    for eng in ENGS:
        p.op(eng, lambda e: e.nop(),
             waits=[(s, v) for s, v in snap.items()
                    if s not in ("gpc", "swx") and not s.startswith("cc")])


def _host_consts():
    sel = np.zeros((NCH, 128, H), np.float32)
    for c in range(NCH):
        for j in range(2):
            sel[c, j * 64:(j + 1) * 64, 2 * c + j] = 1.0
    expand = np.zeros((NCH, H, 128), np.float32)
    for c in range(NCH):
        for j in range(2):
            expand[c, 2 * c + j, j * 64:(j + 1) * 64] = 1.0
    id128 = np.eye(128, dtype=np.float32)
    return sel, expand, id128


def kernel(**inp):
    u = np.asarray(inp["L0_attn_a_u"], np.float32)
    W_B = np.asarray(inp["L0_attn_a_W_B"], np.float32)
    W_C = np.asarray(inp["L0_attn_a_W_C"], np.float32)
    W_dt = np.asarray(inp["L0_attn_a_W_dt"], np.float32)
    b_dt = np.asarray(inp["L0_attn_a_b_dt"], np.float32)
    log_A = np.asarray(inp["L0_attn_a_log_A"], np.float32)
    D_param = np.asarray(inp["L0_attn_a_D_param"], np.float32)
    w_q = np.asarray(inp["L0_attn_b_w_q"], np.float32)
    w_k = np.asarray(inp["L0_attn_b_w_k"], np.float32)
    w_v = np.asarray(inp["L0_attn_b_w_v"], np.float32)
    w_o = np.asarray(inp["L0_attn_b_w_o"], np.float32)
    b_o = np.asarray(inp["L0_attn_b_b_o"], np.float32)
    rel_k = np.asarray(inp["L0_attn_b_rel_k"], np.float32)
    rel_v = np.asarray(inp["L0_attn_b_rel_v"], np.float32)
    G1_R = np.asarray(inp["L0_ffn_G1_R"], np.float32)
    G2_R = np.asarray(inp["L0_ffn_G2_R"], np.float32)
    G1_I = np.asarray(inp["L0_ffn_G1_I"], np.float32)
    G2_I = np.asarray(inp["L0_ffn_G2_I"], np.float32)
    bias_real = np.asarray(inp["L0_ffn_bias_real"], np.float32)
    bias_imag = np.asarray(inp["L0_ffn_bias_imag"], np.float32)

    sel, expand, id128 = _host_consts()
    sel_in = np.ascontiguousarray(sel.transpose(1, 0, 2).reshape(128, NCH * H))
    exp_in = np.ascontiguousarray(expand.transpose(1, 0, 2).reshape(H, NCH * 128))
    rowsel = np.zeros((NS, NS, 128), np.float32)
    for n in range(NS):
        rowsel[n, n, :] = 1.0
    rowsel = np.ascontiguousarray(rowsel.transpose(1, 0, 2).reshape(NS, NS * 128))

    rel_idx = np.clip(np.arange(B)[:, None] - np.arange(B)[None, :],
                      -MAX_REL, MAX_REL) + MAX_REL
    rk_sel = rel_k[rel_idx]
    rv_sel = rel_v[rel_idx]
    rk_lhsT = np.zeros((B, B, NCH, 128, H), np.float32)
    for s in range(B):
        for t in range(B):
            for c in range(NCH):
                for j in range(2):
                    rk_lhsT[s, t, c, j * 64:(j + 1) * 64, 2 * c + j] = rk_sel[s, t]
    rk_in = np.ascontiguousarray(
        rk_lhsT.transpose(3, 0, 1, 2, 4).reshape(128, B * B * NCH * H))
    rv_in = np.zeros((128, H), np.float32)
    for s in range(B):
        for t in range(B):
            rv_in[:, s * B + t] = np.tile(rv_sel[s, t], 2)
    krl = np.zeros((2, 4, 512, 512), np.float32)
    krr = np.zeros((2, 4, 512, 512), np.float32)
    for ri, (G1x, G2x) in enumerate(((G1_R, G2_R), (G1_I, G2_I))):
        for c in range(4):
            for g in range(16):
                krl[ri, c, g::16, g::16] = G1x[0][:, c, :]
            for b2 in range(32):
                krr[ri, c, b2 * 16:(b2 + 1) * 16, b2 * 16:(b2 + 1) * 16] = \
                    G2x[c, :, 0, :].T
    bias_ri = np.ascontiguousarray(
        np.concatenate([bias_real, bias_imag]).reshape(NCH, 128).T)
    bo_t = np.ascontiguousarray(b_o.reshape(NCH, 128).T)
    bdt_t = np.ascontiguousarray(b_dt.reshape(NCH, 128).T)

    shared = {
        "wdt_t": np.ascontiguousarray(W_dt.T),
        "wb_t": np.ascontiguousarray(W_B.T),
        "wc_t": np.ascontiguousarray(W_C.T),
        "bdt_t": bdt_t, "id128": id128, "rowsel": rowsel,
        "wq_t": np.ascontiguousarray(w_q.T),
        "wk_t": np.ascontiguousarray(w_k.T),
        "wv_t": np.ascontiguousarray(w_v.T),
        "wo_t": np.ascontiguousarray(w_o.T),
        "bo_t": bo_t, "krl": krl, "krr": krr, "bias_ri": bias_ri,
        "sel_c": sel_in, "rk_in": rk_in, "exp_in": exp_in, "rv_in": rv_in,
    }

    nc = build_full()
    in_maps = []
    for k in range(NCORES):
        sl = slice(k * TL, (k + 1) * TL)
        m = dict(shared)
        m["u_fm"] = np.ascontiguousarray(u[:, sl, :].transpose(0, 2, 1))
        m["u3"] = np.ascontiguousarray(u[:, sl, :].transpose(2, 1, 0))
        m["logA_d"] = np.ascontiguousarray(log_A[k * 128:(k + 1) * 128])
        m["Dp_d"] = np.ascontiguousarray(D_param[k * 128:(k + 1) * 128, None])
        in_maps.append(m)
    res = run_bass_kernel_spmd(nc, in_maps, core_ids=list(range(NCORES)))

    out = np.empty((B, TG, D), np.float32)
    for k in range(NCORES):
        oh = res.results[k]["out_h"]
        out[:, k * TL:(k + 1) * TL, :] = oh.transpose(2, 1, 0)
    return out
